# version 18
# baseline (speedup 1.0000x reference)
import contextlib
import numpy as np
import concourse.bass as bass
import concourse.mybir as mybir
from concourse.bass_utils import run_bass_kernel_spmd

F32 = mybir.dt.float32
BF16 = mybir.dt.bfloat16
AF = mybir.ActivationFunctionType
ALU = mybir.AluOpType
AX = mybir.AxisListType

D = 1024
DC = 8
S = 8192
TQ = 2048
KB = 512
NBLK = S // KB
NPRO = TQ // KB
NQT = TQ // 128
E = 8
FF = 8192
TH = 1024
SCALE = 0.03125
BIGPOS = 3.0e38
CAP = 384
HSLOT = 4 * CAP

ENGS = ("t", "v", "s", "g", "y")


class Ctr:
    def __init__(self, stack, nc, name):
        self.sem = stack.enter_context(nc.semaphore(name))
        self.n = 0


class Sched:
    def __init__(self):
        self.q = {k: [] for k in ENGS}
        self.lastw = {k: {} for k in ENGS}

    def op(self, eng, fn, inc=None, k=1):
        if inc is not None:
            inc.n += k
        self.q[eng].append((fn, inc, k))

    def wait(self, eng, ev):
        if ev is None:
            return
        ctr, v = ev
        if v <= 0 or self.lastw[eng].get(id(ctr), -1) >= v:
            return
        self.lastw[eng][id(ctr)] = v
        self.q[eng].append((None, ctr, v))

    def replay(self, eng, e):
        for fn, aux, k in self.q[eng]:
            if fn is None:
                e.wait_ge(aux.sem, k)
            else:
                inst = fn(e)
                if aux is not None:
                    inst.then_inc(aux.sem, k)


def build_program():
    nc = bass.Bass("TRN2", target_bir_lowering=False)

    xb = nc.dram_tensor("xb", [S, D], F32, kind="ExternalInput").ap()
    xq = nc.dram_tensor("xq", [TQ, D], F32, kind="ExternalInput").ap()
    wqT = nc.dram_tensor("wqT", [D, D], F32, kind="ExternalInput").ap()
    wkT = nc.dram_tensor("wkT", [D, D], F32, kind="ExternalInput").ap()
    wvT = nc.dram_tensor("wvT", [D, D], F32, kind="ExternalInput").ap()
    woT = nc.dram_tensor("woT", [D, D], F32, kind="ExternalInput").ap()
    bq_in = nc.dram_tensor("bq_in", [128, 8], F32, kind="ExternalInput").ap()
    bk_in = nc.dram_tensor("bk_in", [128, 8], F32, kind="ExternalInput").ap()
    bv_in = nc.dram_tensor("bv_in", [128, 8], F32, kind="ExternalInput").ap()
    bo_in = nc.dram_tensor("bo_in", [128, 8], F32, kind="ExternalInput").ap()
    rw = nc.dram_tensor("rw", [D, E], F32, kind="ExternalInput").ap()
    rb_in = nc.dram_tensor("rb_in", [1, E], F32, kind="ExternalInput").ap()
    w1 = nc.dram_tensor("w1", [E, D, FF], F32, kind="ExternalInput").ap()
    b1_in = nc.dram_tensor("b1_in", [128, E, FF // 128], F32, kind="ExternalInput").ap()
    w2 = nc.dram_tensor("w2", [E, FF // 2, D], F32, kind="ExternalInput").ap()
    b2_in = nc.dram_tensor("b2_in", [128, E, DC], F32, kind="ExternalInput").ap()
    triu_in = nc.dram_tensor("triu_in", [128, 128], F32, kind="ExternalInput").ap()
    ones2d_in = nc.dram_tensor("ones2d_in", [128, 128], F32, kind="ExternalInput").ap()
    iota_in = nc.dram_tensor("iota_in", [1, HSLOT], F32, kind="ExternalInput").ap()
    ecap_in = nc.dram_tensor("ecap_in", [1, E], F32, kind="ExternalInput").ap()
    ident_in = nc.dram_tensor("ident_in", [128, 128], F32, kind="ExternalInput").ap()
    ones_in = nc.dram_tensor("ones_in", [1, 128], F32, kind="ExternalInput").ap()
    out = nc.dram_tensor("out", [TQ, D], F32, kind="ExternalOutput").ap()

    with contextlib.ExitStack() as pstack:
        def psb(name, shape, dtype=F32):
            return pstack.enter_context(nc.sbuf_tensor(name, shape, dtype))

        ident = psb("ident", [128, 128], BF16)
        ones = psb("ones", [1, 128], BF16)
        eps_col = psb("eps_col", [128, 1], F32)
        bo2_col = psb("bo2_col", [128, 8], F32)

        te = Ctr(pstack, nc, "te")
        ve = Ctr(pstack, nc, "ve")
        se = Ctr(pstack, nc, "se")
        gc = Ctr(pstack, nc, "gc")
        _ctr_n = [0]

        def new_ctr(tag):
            _ctr_n[0] += 1
            return Ctr(pstack, nc, f"c{_ctr_n[0]}_{tag}")

        def T(v):
            return (te, v)

        def V(v):
            return (ve, v)

        def SE(v):
            return (se, v)

        def G(v):
            return (gd, v)

        def Y(v):
            return (sy, v)

        with contextlib.ExitStack() as astack:
            def asb(name, shape, dtype=F32):
                return astack.enter_context(nc.sbuf_tensor(name, shape, dtype))

            wq_wo = asb("wq_wo", [128, DC, D], BF16)
            wk_sb = asb("wk_sb", [128, DC, D], BF16)
            wv_sb = asb("wv_sb", [128, DC, D], BF16)
            bq_col = asb("bq_col", [128, 8], F32)
            bk_col = asb("bk_col", [128, 8], F32)
            bv_col = asb("bv_col", [128, 8], BF16)
            bo_col = asb("bo_col", [128, 8], F32)
            x_blk = asb("x_blk", [128, 4, D], BF16)
            xn_sb = asb("xn_sb", [128, 2, D], BF16)
            xnT = asb("xnT", [128, DC, KB], BF16)
            KT = asb("KT", [128, 8, KB], BF16)
            V_sb = asb("V_sb", [128, 4, D], BF16)
            QT = asb("QT", [128, 8, TQ], BF16)
            OACC = asb("OACC", [128, NQT, D], F32)
            P_sb = asb("P_sb", [128, 2, KB], BF16)
            PT_sb = asb("PT_sb", [128, 2, 4, 128], BF16)
            st12 = asb("st12", [128, 4, 12], F32)
            mv_sb = asb("mv_sb", [128, 4, 2], F32)
            lnv_sb = asb("lnv_sb", [128, 4], F32)
            rstd_sb = asb("rstd_sb", [128, 4], F32)
            nm2 = asb("nm2", [128, 2, NQT], F32)
            l2 = asb("l2", [128, 2, NQT], F32)
            nmblk = asb("nmblk", [128, NQT], F32)
            d_sb = asb("d_sb", [128, NQT], F32)
            alpha = asb("alpha", [128, NQT], F32)
            bsum = asb("bsum", [128, NQT], F32)
            linv = asb("linv", [128, NQT], F32)
            nms = asb("nms", [128, NQT], F32)
            O_bf = asb("O_bf", [128, 2, D], BF16)
            OT_sb = asb("OT_sb", [128, 2, DC, 128], BF16)
            xq_f32 = asb("xq_f32", [128, 1, D], F32)
            r_out = asb("r_out", [128, 2, D], F32)

            ps_s = astack.enter_context(nc.psum_tensor("ps_s", [128, 2, 512], F32))
            ps_o = astack.enter_context(nc.psum_tensor("ps_o", [128, 2, 2, 512], F32))
            ps_tr = astack.enter_context(nc.psum_tensor("ps_tr", [128, 2, 8, 128], BF16))

            sc = Sched()

            ps_s_free = {0: None, 1: None}
            ps_o_free = {0: None, 1: None}
            trcopy = {0: None, 1: None}
            xn_used = {0: None, 1: None}
            pt_used = {0: None, 1: None}
            p_trans = {0: None, 1: None}
            x_used = [None]
            blk_i = [0]

            sc.op("g", lambda e: e.memset(eps_col[:, :], 1e-5), gc)
            sc.op("g", lambda e: e.memset(nm2[:, 0, :], BIGPOS), gc)
            sc.op("g", lambda e: e.memset(l2[:, 0, :], 0.0), gc)
            sc.op("g", lambda e: e.memset(OACC[:, :, :], 0.0), gc)
            setup_evs = []

            def setup_dma(fn, tag):
                c = new_ctr(tag)
                sc.op("g", fn, c, 16)
                setup_evs.append((c, c.n))

            setup_dma(lambda e: e.dma_start(out=ident[:, :], in_=ident_in), "id")
            setup_dma(lambda e: e.dma_start(out=ones[:, :], in_=ones_in), "on")
            setup_dma(lambda e: e.dma_start(
                out=wq_wo[:, :, :], in_=wqT.rearrange("(dc p) e -> p dc e", p=128)), "wq")
            setup_dma(lambda e: e.dma_start(
                out=wk_sb[:, :, :], in_=wkT.rearrange("(dc p) e -> p dc e", p=128)), "wk")
            setup_dma(lambda e: e.dma_start(
                out=wv_sb[:, :, :], in_=wvT.rearrange("(dc p) e -> p dc e", p=128)), "wv")
            setup_dma(lambda e: e.dma_start(out=bq_col[:, :], in_=bq_in), "bq")
            setup_dma(lambda e: e.dma_start(out=bk_col[:, :], in_=bk_in), "bk")
            setup_dma(lambda e: e.dma_start(out=bv_col[:, :], in_=bv_in), "bv")
            setup_dma(lambda e: e.dma_start(out=bo_col[:, :], in_=bo_in), "bo")
            setup_c = (gc, gc.n)

            def wait_setup(eng):
                for ev in setup_evs:
                    sc.wait(eng, ev)
            c_x = new_ctr("x")
            c_wo = new_ctr("wo")
            c_xq = [new_ctr("xq0"), new_ctr("xq1")]
            c_ro = [new_ctr("ro0"), new_ctr("ro1")]

            def ln_block(src_ap, tok0):
                sc.wait("g", x_used[0])
                sc.op("g", lambda e, a=src_ap, t=tok0: e.dma_start(
                    out=x_blk[:, :, :],
                    in_=a[t:t + KB, :].rearrange("(tt p) d -> p tt d", p=128)), c_x, 16)
                x_dma = (c_x, c_x.n)
                sc.wait("v", x_dma)
                last = None
                s_rstd = {}
                for tt in range(4):
                    sc.op("v", lambda e, tt=tt: e.bn_stats(
                        st12[:, tt, 0:6], x_blk[:, tt, 0:512]))
                    sc.op("v", lambda e, tt=tt: e.bn_stats(
                        st12[:, tt, 6:12], x_blk[:, tt, 512:1024]))
                    sc.op("v", lambda e: e.drain())
                    sc.op("v", lambda e, tt=tt: e.bn_aggr(
                        mv_sb[:, tt, :], st12[:, tt, :]))
                    sc.op("v", lambda e: e.drain(), ve)
                    v_st = V(ve.n)
                    sc.wait("s", v_st)
                    wait_setup("s")
                    sc.wait("s", setup_c)
                    sc.op("s", lambda e, tt=tt: e.activation(
                        lnv_sb[:, tt:tt + 1], mv_sb[:, tt, 1:2], AF.Ln,
                        bias=eps_col[:, :], scale=1.0))
                    sc.op("s", lambda e: e.drain())
                    sc.op("s", lambda e, tt=tt: e.activation(
                        rstd_sb[:, tt:tt + 1], lnv_sb[:, tt:tt + 1], AF.Exp,
                        bias=0.0, scale=-0.5))
                    sc.op("s", lambda e: e.drain(), se)
                    s_rstd[tt] = SE(se.n)
                for tt in range(4):
                    xp = tt % 2
                    sc.wait("v", s_rstd[tt])
                    sc.wait("v", xn_used[xp])
                    sc.op("v", lambda e, tt=tt, xp=xp: e.tensor_scalar(
                        xn_sb[:, xp, :], x_blk[:, tt, :],
                        mv_sb[:, tt, 0:1], rstd_sb[:, tt:tt + 1],
                        op0=ALU.subtract, op1=ALU.mult), ve)
                    v_n = V(ve.n)
                    if tt == 3:
                        x_used[0] = v_n
                    par = tt % 2
                    sc.wait("t", v_n)
                    sc.wait("t", trcopy[par])
                    for dc in range(DC):
                        inc = te if dc == DC - 1 else None
                        sc.op("t", lambda e, xp=xp, dc=dc, par=par: e.transpose(
                            ps_tr[:, par, dc, :], xn_sb[:, xp, dc * 128:(dc + 1) * 128],
                            ident[:, :]), inc)
                    xn_used[xp] = T(te.n)
                    sc.wait("v", xn_used[xp])
                    sc.op("v", lambda e, tt=tt, par=par: e.tensor_copy(
                        xnT[:, :, tt * 128:(tt + 1) * 128], ps_tr[:, par, :, :]), ve)
                    trcopy[par] = V(ve.n)
                    last = V(ve.n)
                blk_i[0] += 1
                return last

            wait_setup("t")
            sc.wait("v", setup_c)

            for ob in range(NPRO):
                v_xnT = ln_block(xq, ob * KB)
                sc.wait("t", v_xnT)
                for ec in range(8):
                    par = ec % 2
                    sc.wait("t", ps_s_free[par])
                    for dc in range(DC):
                        inc = te if dc == DC - 1 else None
                        sc.op("t", lambda e, ec=ec, dc=dc, par=par: e.matmul(
                            ps_s[:, par, :], wq_wo[:, dc, ec * 128:(ec + 1) * 128],
                            xnT[:, dc, :], start=(dc == 0), stop=(dc == DC - 1)), inc)
                    t_q = T(te.n)
                    sc.wait("v", t_q)
                    sc.op("v", lambda e, ec=ec, ob=ob, par=par: e.tensor_scalar(
                        QT[:, ec, ob * KB:(ob + 1) * KB], ps_s[:, par, :],
                        bq_col[:, ec:ec + 1], None, op0=ALU.add), ve)
                    ps_s_free[par] = V(ve.n)
            qt_done_te = T(te.n)
            qt_done_ve = V(ve.n)

            sc.wait("g", qt_done_te)
            sc.op("g", lambda e: e.dma_start(
                out=wq_wo[:, :, :], in_=woT.rearrange("(dc p) e -> p dc e", p=128)), c_wo, 16)
            wo_loaded = (c_wo, c_wo.n)

            sc.wait("t", wo_loaded)
            for ec in range(8):
                sc.wait("t", ps_s_free[0])
                for dc in range(DC):
                    inc = te if dc == DC - 1 else None
                    sc.op("t", lambda e, ec=ec, dc=dc: e.matmul(
                        ps_s[:, 0, 0:1], wq_wo[:, dc, ec * 128:(ec + 1) * 128],
                        bv_col[:, dc:dc + 1], start=(dc == 0), stop=(dc == DC - 1)), inc)
                t_b = T(te.n)
                sc.wait("v", t_b)
                sc.op("v", lambda e, ec=ec: e.tensor_scalar(
                    bo2_col[:, ec:ec + 1], ps_s[:, 0, 0:1],
                    bo_col[:, ec:ec + 1], None, op0=ALU.add), ve)
                ps_s_free[0] = V(ve.n)

            v_xnT_cur = ln_block(xb, 0)
            for kb in range(NBLK):
                rp, wp = kb % 2, (kb + 1) % 2
                sc.wait("t", v_xnT_cur)
                sc.wait("t", qt_done_ve)
                kt_ready = None
                v_ready = None

                def emit_kt(ec):
                    nonlocal kt_ready
                    par = ec % 2
                    sc.wait("t", ps_s_free[par])
                    for dc in range(DC):
                        inc = te if dc == DC - 1 else None
                        sc.op("t", lambda e, ec=ec, dc=dc, par=par: e.matmul(
                            ps_s[:, par, :], wk_sb[:, dc, ec * 128:(ec + 1) * 128],
                            xnT[:, dc, :], start=(dc == 0), stop=(dc == DC - 1)), inc)
                    t_k = T(te.n)
                    sc.wait("v", t_k)
                    sc.op("v", lambda e, ec=ec, par=par: e.tensor_scalar(
                        KT[:, ec, :], ps_s[:, par, :],
                        bk_col[:, ec:ec + 1], None, op0=ALU.add), ve)
                    ps_s_free[par] = V(ve.n)
                    kt_ready = V(ve.n)

                def emit_v(tt):
                    nonlocal v_ready
                    par = tt % 2
                    sc.wait("t", ps_o_free[par])
                    for eh in range(2):
                        for dc in range(DC):
                            inc = te if (eh == 1 and dc == DC - 1) else None
                            sc.op("t", lambda e, tt=tt, eh=eh, dc=dc, par=par: e.matmul(
                                ps_o[:, par, eh, :],
                                xnT[:, dc, tt * 128:(tt + 1) * 128],
                                wv_sb[:, dc, eh * 512:(eh + 1) * 512],
                                start=(dc == 0), stop=(dc == DC - 1)), inc)
                    t_v = T(te.n)
                    sc.wait("v", t_v)
                    sc.op("v", lambda e, tt=tt, par=par: e.tensor_copy(
                        V_sb[:, tt, :],
                        ps_o[:, par, :, :].rearrange("p a b -> p (a b)")), ve)
                    ps_o_free[par] = V(ve.n)
                    v_ready = V(ve.n)

                for i4 in range(4):
                    emit_kt(2 * i4)
                    emit_kt(2 * i4 + 1)
                    emit_v(i4)

                v_xnT_cur = (ln_block(xb, (kb + 1) * KB)
                             if kb + 1 < NBLK else None)

                s_p_ev = {}
                v_pt_ev = {}
                for qi in range(NQT + 2):
                    if qi < NQT:
                        par = qi % 2
                        sc.wait("t", kt_ready)
                        sc.wait("t", ps_s_free[par])
                        for ec in range(8):
                            inc = te if ec == 7 else None
                            sc.op("t", lambda e, ec=ec, qi=qi, par=par: e.matmul(
                                ps_s[:, par, :], QT[:, ec, qi * 128:(qi + 1) * 128],
                                KT[:, ec, :], start=(ec == 0), stop=(ec == 7)), inc)
                        t_s = T(te.n)
                        sc.wait("v", t_s)
                        sc.op("v", lambda e, qi=qi, par=par: e.reduce_max(
                            nmblk[:, qi:qi + 1], ps_s[:, par, :], axis=AX.X,
                            negate=True))
                        sc.op("v", lambda e: e.drain())
                        sc.op("v", lambda e, qi=qi, rp=rp, wp=wp: e.tensor_tensor(
                            nm2[:, wp, qi:qi + 1], nm2[:, rp, qi:qi + 1],
                            nmblk[:, qi:qi + 1], op=ALU.min))
                        sc.op("v", lambda e: e.drain())
                        sc.op("v", lambda e, qi=qi, rp=rp, wp=wp: e.tensor_tensor(
                            d_sb[:, qi:qi + 1], nm2[:, wp, qi:qi + 1],
                            nm2[:, rp, qi:qi + 1], op=ALU.subtract))
                        sc.op("v", lambda e: e.drain(), ve)
                        v_stat = V(ve.n)
                        pp = qi % 2
                        sc.wait("s", v_stat)
                        sc.wait("s", p_trans[pp])
                        sc.op("s", lambda e, qi=qi: e.activation(
                            alpha[:, qi:qi + 1], d_sb[:, qi:qi + 1], AF.Exp,
                            bias=0.0, scale=SCALE))
                        sc.op("s", lambda e, qi=qi, wp=wp: e.mul(
                            nms[:, qi:qi + 1], nm2[:, wp, qi:qi + 1], SCALE))
                        sc.op("s", lambda e: e.drain())
                        sc.op("s", lambda e, qi=qi, par=par, pp=pp: e.activation(
                            P_sb[:, pp, :], ps_s[:, par, :], AF.Exp,
                            bias=nms[:, qi:qi + 1], scale=SCALE,
                            accum_out=bsum[:, qi:qi + 1]))
                        sc.op("s", lambda e: e.drain(), se)
                        s_p_ev[qi] = SE(se.n)
                        ps_s_free[par] = SE(se.n)
                        sc.wait("v", s_p_ev[qi])
                        sc.op("v", lambda e, qi=qi, rp=rp, wp=wp: e.scalar_tensor_tensor(
                            l2[:, wp, qi:qi + 1], l2[:, rp, qi:qi + 1],
                            alpha[:, qi:qi + 1], bsum[:, qi:qi + 1],
                            op0=ALU.mult, op1=ALU.add), ve)
                    if 1 <= qi <= NQT:
                        qj = qi - 1
                        par = qj % 2
                        pp = qj % 2
                        sc.wait("t", s_p_ev[qj])
                        sc.wait("t", trcopy[par])
                        for kc in range(4):
                            inc = te if kc == 3 else None
                            sc.op("t", lambda e, kc=kc, par=par, pp=pp: e.transpose(
                                ps_tr[:, par, kc, :],
                                P_sb[:, pp, kc * 128:(kc + 1) * 128],
                                ident[:, :]), inc)
                        p_trans[pp] = T(te.n)
                        t_ptr = T(te.n)
                        sc.wait("v", t_ptr)
                        sc.wait("v", pt_used[par])
                        sc.op("v", lambda e, par=par: e.tensor_copy(
                            PT_sb[:, par, :, :], ps_tr[:, par, 0:4, :]), ve)
                        trcopy[par] = V(ve.n)
                        v_pt_ev[qj] = V(ve.n)
                    if qi >= 2:
                        qj = qi - 2
                        par = qj % 2
                        sc.wait("t", v_pt_ev[qj])
                        sc.wait("t", v_ready)
                        sc.wait("t", ps_o_free[par])
                        for kc in range(4):
                            for h in range(2):
                                inc = te if (kc == 3 and h == 1) else None
                                sc.op("t", lambda e, kc=kc, h=h, par=par: e.matmul(
                                    ps_o[:, par, h, :], PT_sb[:, par, kc, :],
                                    V_sb[:, kc, h * 512:(h + 1) * 512],
                                    start=(kc == 0), stop=(kc == 3)), inc)
                        t_av = T(te.n)
                        pt_used[par] = T(te.n)
                        sc.wait("v", t_av)
                        for h in range(2):
                            sc.op("v",
                                  lambda e, qj=qj, h=h, par=par: e.scalar_tensor_tensor(
                                      OACC[:, qj, h * 512:(h + 1) * 512],
                                      OACC[:, qj, h * 512:(h + 1) * 512],
                                      alpha[:, qj:qj + 1], ps_o[:, par, h, :],
                                      op0=ALU.mult, op1=ALU.add), ve)
                        ps_o_free[par] = V(ve.n)

            final_p = NBLK % 2
            xq_dma = {}
            xq_used = {}
            rstored = {}
            ot_used = {}
            obf_used = {}
            for qi in range(NQT):
                par = qi % 2
                sc.wait("y", xq_used.get(qi - 1))
                sc.op("y", lambda e, qi=qi: e.dma_start(
                    out=xq_f32[:, 0, :], in_=xq[qi * 128:(qi + 1) * 128, :]),
                    c_xq[0], 16)
                xq_dma[qi] = (c_xq[0], c_xq[0].n)
                sc.op("v", lambda e, qi=qi, fp=final_p: e.reciprocal(
                    linv[:, qi:qi + 1], l2[:, fp, qi:qi + 1]))
                sc.op("v", lambda e: e.drain(), ve)
                v_linv = V(ve.n)
                sc.wait("s", v_linv)
                sc.wait("s", obf_used.get(par))
                sc.op("s", lambda e, qi=qi, par=par: e.mul(
                    O_bf[:, par, :], OACC[:, qi, :], linv[:, qi:qi + 1]), se)
                s_obf = SE(se.n)
                sc.wait("t", s_obf)
                sc.wait("t", trcopy[par])
                for dc in range(DC):
                    inc = te if dc == DC - 1 else None
                    sc.op("t", lambda e, dc=dc, par=par: e.transpose(
                        ps_tr[:, par, dc, :], O_bf[:, par, dc * 128:(dc + 1) * 128],
                        ident[:, :]), inc)
                obf_used[par] = T(te.n)
                t_otr = T(te.n)
                sc.wait("v", t_otr)
                sc.wait("v", ot_used.get(par))
                sc.op("v", lambda e, par=par: e.tensor_copy(
                    OT_sb[:, par, :, :], ps_tr[:, par, :, :]), ve)
                trcopy[par] = V(ve.n)
                v_ot = V(ve.n)
                sc.wait("t", v_ot)
                sc.wait("t", wo_loaded)
                sc.wait("t", ps_o_free[par])
                for h in range(2):
                    for dc in range(DC):
                        inc = te if (h == 1 and dc == DC - 1) else None
                        sc.op("t", lambda e, h=h, dc=dc, par=par: e.matmul(
                            ps_o[:, par, h, :], OT_sb[:, par, dc, :],
                            wq_wo[:, dc, h * 512:(h + 1) * 512],
                            start=(dc == 0), stop=(dc == DC - 1)), inc)
                t_proj = T(te.n)
                ot_used[par] = T(te.n)
                sc.wait("v", t_proj)
                sc.wait("v", xq_dma[qi])
                sc.wait("v", rstored.get(qi - 2))
                for h in range(2):
                    inc = ve if h == 1 else None
                    sc.op("v", lambda e, h=h, par=par: e.tensor_tensor(
                        r_out[:, par, h * 512:(h + 1) * 512],
                        xq_f32[:, 0, h * 512:(h + 1) * 512],
                        ps_o[:, par, h, :], op=ALU.add), inc)
                ps_o_free[par] = V(ve.n)
                xq_used[qi] = V(ve.n)
                v_r = V(ve.n)
                sc.wait("y", v_r)
                sc.op("y", lambda e, qi=qi, par=par: e.dma_start(
                    out=out[qi * 128:(qi + 1) * 128, :], in_=r_out[:, par, :]),
                    c_ro[par], 16)
                rstored[qi] = (c_ro[par], c_ro[par].n)

            sc.wait("y", (c_ro[0], c_ro[0].n))
            sc.wait("y", (c_ro[1], c_ro[1].n))

            with nc.Block() as blk:
                @blk.tensor
                def _(e):
                    sc.replay("t", e)

                @blk.vector
                def _(e):
                    sc.replay("v", e)

                @blk.scalar
                def _(e):
                    sc.replay("s", e)

                @blk.gpsimd
                def _(e):
                    sc.replay("g", e)

                @blk.sync
                def _(e):
                    sc.replay("y", e)

        with contextlib.ExitStack() as fstack:
            def fsb(name, shape, dtype=F32):
                return fstack.enter_context(nc.sbuf_tensor(name, shape, dtype))

            xn_tok = fsb("xn_tok", [128, 16, D], BF16)
            ffn_acc = fsb("ffn_acc", [128, 16, D], BF16)
            iota_bc = fsb("iota_bc", [128, HSLOT], F32)
            pos_col = fsb("pos_col", [128, 16], F32)
            pos_colh = fsb("pos_colh", [128, 16], F32)
            b2_sb = fsb("b2_sb", [128, E, DC], F32)
            ones_f32 = fsb("ones_f32", [1, 128], F32)
            b1_sb = fsb("b1_sb", [128, E, 64], F32)
            bo_bc = fsb("bo_bc", [128, D], BF16)

            b1stack = contextlib.ExitStack()

            def fsb1(name, shape, dtype=F32):
                return b1stack.enter_context(nc.sbuf_tensor(name, shape, dtype))

            xf_bf = fsb1("xf_bf", [128, 4, D], BF16)
            xnTb = fsb1("xnTb", [128, DC, 512], BF16)
            st12f = fsb1("st12f", [128, 4, 12], F32)
            mvf = fsb1("mvf", [128, 4, 2], F32)
            lnvf = fsb1("lnvf", [128, 4], F32)
            rstdf = fsb1("rstdf", [128, 4], F32)
            rw_sb = fsb1("rw_sb", [128, DC, E], BF16)
            rb_row = fsb1("rb_row", [1, E], BF16)
            rb_bc = fsb1("rb_bc", [128, E], BF16)
            logits = fsb1("logits", [128, 16, E], F32)
            lgmax = fsb1("lgmax", [128, 16], F32)
            mask_bf = fsb1("mask_bf", [128, 16, E], BF16)
            triu_sb = fsb1("triu_sb", [128, 128], BF16)
            ones2d = fsb1("ones2d", [128, 128], BF16)
            iota_row = fsb1("iota_row", [1, HSLOT], F32)
            ecap_row = fsb1("ecap_row", [1, E], F32)
            ecap_bc = fsb1("ecap_bc", [128, E], F32)
            tmp8 = fsb1("tmp8", [128, E], F32)
            possum = fsb1("possum", [128, 1], F32)
            pos_row = fsb1("pos_row", [1, 2048], F32)
            bo2_bf = fsb1("bo2_bf", [128, 8], BF16)
            borow = fsb1("borow", [1, D], BF16)
            ps_b = b1stack.enter_context(nc.psum_tensor("ps_b", [128, 7, 512], F32))
            ps_trf = b1stack.enter_context(nc.psum_tensor("ps_trf", [128, 8, 128], BF16))

            sc = Sched()

            bank_free = {k: None for k in range(8)}
            trf_free = [None]
            w1_used = {0: None, 1: None}
            w2_used = {0: None, 1: None}
            sig_ready = {}
            sig_used = {}
            xf_used = [None]

            c_fs = [new_ctr("fs") for _ in range(5)]
            c_ys = [new_ctr("ys") for _ in range(5)]
            c_xf = new_ctr("xf")
            c_w1 = [new_ctr("w1a"), new_ctr("w1b"), new_ctr("w1c")]
            c_w2 = [new_ctr("w2a"), new_ctr("w2b")]
            c_ri = [new_ctr("ri0"), new_ctr("ri1")]
            c_os = [new_ctr("os0"), new_ctr("os1")]

            sc.op("g", lambda e: e.dma_start(
                out=rw_sb[:, :, :], in_=rw.rearrange("(dc p) e -> p dc e", p=128)),
                c_fs[0], 16)
            sc.op("g", lambda e: e.dma_start(out=b1_sb[:, :, :], in_=b1_in), c_fs[1], 16)
            sc.op("g", lambda e: e.dma_start(out=rb_row[:, :], in_=rb_in), c_fs[2], 16)
            sc.op("g", lambda e: e.dma_start(out=triu_sb[:, :], in_=triu_in), c_fs[3], 16)
            sc.op("g", lambda e: e.dma_start(out=ones2d[:, :], in_=ones2d_in), c_fs[4], 16)
            sc.op("y", lambda e: e.dma_start(out=iota_row[:, :], in_=iota_in), c_ys[0], 16)
            sc.op("y", lambda e: e.dma_start(out=ecap_row[:, :], in_=ecap_in), c_ys[1], 16)
            sc.op("g", lambda e: e.dma_start(out=b2_sb[:, :, :], in_=b2_in), c_ys[2], 16)
            sc.op("y", lambda e: e.dma_start(out=ones_f32[:, :], in_=ones_in), c_ys[3], 16)
            f_evs = [(c, 16) for c in c_fs] + [(c, 16) for c in c_ys[:4]]

            def wait_fsetup(eng):
                for ev in f_evs:
                    sc.wait(eng, ev)

            sc.op("v", lambda e: e.memset(ffn_acc[:, :, :], 0.0), ve)
            acc_zero = V(ve.n)

            wait_fsetup("t")
            sc.op("t", lambda e: e.matmul(
                ps_b[:, 0, 0:E], ones[:, :], rb_row[:, :], start=True, stop=True), te)
            t_rb = T(te.n)
            sc.wait("v", t_rb)
            sc.op("v", lambda e: e.tensor_copy(rb_bc[:, :], ps_b[:, 0, 0:E]), ve)
            rb_done = V(ve.n)
            bank_free[0] = rb_done

            sc.op("t", lambda e: e.matmul(
                ps_b[:, 1, 0:E], ones_f32[:, :], ecap_row[:, :],
                start=True, stop=True), te)
            t_ec = T(te.n)
            sc.wait("v", t_ec)
            sc.op("v", lambda e: e.tensor_copy(ecap_bc[:, :], ps_b[:, 1, 0:E]), ve)
            ec_done = V(ve.n)
            bank_free[1] = ec_done

            for j in range(HSLOT // 512):
                sc.op("t", lambda e, j=j: e.matmul(
                    ps_b[:, 2, :], ones_f32[:, :], iota_row[:, j * 512:(j + 1) * 512],
                    start=True, stop=True), te)
                t_io = T(te.n)
                sc.wait("v", t_io)
                sc.op("v", lambda e, j=j: e.tensor_copy(
                    iota_bc[:, j * 512:(j + 1) * 512], ps_b[:, 2, :]), ve)
                bank_free[2] = V(ve.n)
                sc.wait("t", bank_free[2])
            iota_done = V(ve.n)

            sc.op("v", lambda e: e.tensor_copy(bo2_bf[:, :], bo2_col[:, :]), ve)
            v_b2 = V(ve.n)
            sc.wait("t", v_b2)
            for dc in range(DC):
                inc = te if dc == DC - 1 else None
                sc.op("t", lambda e, dc=dc: e.transpose(
                    ps_trf[0:1, dc, :], bo2_bf[:, dc:dc + 1], ident[:, :]), inc)
            t_bt = T(te.n)
            sc.wait("v", t_bt)
            sc.op("v", lambda e: e.tensor_copy(
                borow[:, :], ps_trf[0:1, :, :].rearrange("p a b -> p (a b)")))
            sc.op("v", lambda e: e.drain(), ve)
            trf_free[0] = V(ve.n)
            v_brow = V(ve.n)
            sc.wait("t", v_brow)
            for h2 in range(2):
                sc.wait("t", bank_free[3])
                sc.op("t", lambda e, h2=h2: e.matmul(
                    ps_b[:, 3, :], ones[:, :], borow[:, h2 * 512:(h2 + 1) * 512],
                    start=True, stop=True), te)
                t_bo = T(te.n)
                sc.wait("v", t_bo)
                sc.op("v", lambda e, h2=h2: e.tensor_copy(
                    bo_bc[:, h2 * 512:(h2 + 1) * 512], ps_b[:, 3, :]), ve)
                bank_free[3] = V(ve.n)
            bo_done = V(ve.n)

            for b4 in range(4):
                sc.wait("g", xf_used[0])
                sc.op("g", lambda e, t0=b4 * 512: e.dma_start(
                    out=xf_bf[:, :, :],
                    in_=xq[t0:t0 + 512, :].rearrange("(tt p) d -> p tt d", p=128)),
                    c_xf, 16)
                g_x = (c_xf, c_xf.n)
                sc.wait("v", g_x)
                s_rstd = {}
                for tt in range(4):
                    sc.op("v", lambda e, tt=tt: e.bn_stats(
                        st12f[:, tt, 0:6], xf_bf[:, tt, 0:512]))
                    sc.op("v", lambda e, tt=tt: e.bn_stats(
                        st12f[:, tt, 6:12], xf_bf[:, tt, 512:1024]))
                    sc.op("v", lambda e: e.drain())
                    sc.op("v", lambda e, tt=tt: e.bn_aggr(
                        mvf[:, tt, :], st12f[:, tt, :]))
                    sc.op("v", lambda e: e.drain(), ve)
                    v_st = V(ve.n)
                    sc.wait("s", v_st)
                    sc.op("s", lambda e, tt=tt: e.activation(
                        lnvf[:, tt:tt + 1], mvf[:, tt, 1:2], AF.Ln,
                        bias=eps_col[:, :], scale=1.0))
                    sc.op("s", lambda e: e.drain())
                    sc.op("s", lambda e, tt=tt: e.activation(
                        rstdf[:, tt:tt + 1], lnvf[:, tt:tt + 1], AF.Exp,
                        bias=0.0, scale=-0.5))
                    sc.op("s", lambda e: e.drain(), se)
                    s_rstd[tt] = SE(se.n)
                for tt in range(4):
                    gt = b4 * 4 + tt
                    sc.wait("v", s_rstd[tt])
                    sc.op("v", lambda e, tt=tt, gt=gt: e.tensor_scalar(
                        xn_tok[:, gt, :], xf_bf[:, tt, :],
                        mvf[:, tt, 0:1], rstdf[:, tt:tt + 1],
                        op0=ALU.subtract, op1=ALU.mult), ve)
                    v_n = V(ve.n)
                    sc.wait("t", v_n)
                    sc.wait("t", trf_free[0])
                    for dc in range(DC):
                        inc = te if dc == DC - 1 else None
                        sc.op("t", lambda e, gt=gt, dc=dc: e.transpose(
                            ps_trf[:, dc, :], xn_tok[:, gt, dc * 128:(dc + 1) * 128],
                            ident[:, :]), inc)
                    t_tr = T(te.n)
                    sc.wait("v", t_tr)
                    sc.op("v", lambda e, tt=tt: e.tensor_copy(
                        xnTb[:, :, tt * 128:(tt + 1) * 128], ps_trf[:, :, :]), ve)
                    trf_free[0] = V(ve.n)
                xnTb_done = V(ve.n)
                xf_used[0] = V(ve.n)
                sc.wait("t", xnTb_done)
                for tt in range(4):
                    gt = b4 * 4 + tt
                    bk = 4 + (tt % 2)
                    sc.wait("t", bank_free[bk])
                    for dc in range(DC):
                        inc = te if dc == DC - 1 else None
                        sc.op("t", lambda e, tt=tt, dc=dc, bk=bk: e.matmul(
                            ps_b[:, bk, 0:E], xnTb[:, dc, tt * 128:(tt + 1) * 128],
                            rw_sb[:, dc, :], start=(dc == 0), stop=(dc == DC - 1)), inc)
                    t_lg = T(te.n)
                    sc.wait("v", t_lg)
                    sc.wait("v", rb_done)
                    sc.op("v", lambda e, gt=gt, bk=bk: e.tensor_tensor(
                        logits[:, gt, :], ps_b[:, bk, 0:E], rb_bc[:, :], op=ALU.add), ve)
                    bank_free[bk] = V(ve.n)
                    sc.op("v", lambda e: e.drain())
                    sc.op("v", lambda e, gt=gt: e.reduce_max(
                        lgmax[:, gt:gt + 1], logits[:, gt, :], axis=AX.X))
                    sc.op("v", lambda e: e.drain())
                    sc.op("v", lambda e, gt=gt: e.tensor_scalar(
                        mask_bf[:, gt, :], logits[:, gt, :],
                        lgmax[:, gt:gt + 1], None, op0=ALU.is_ge))
                    sc.op("v", lambda e: e.drain(), ve)
            mask_done = V(ve.n)

            sc.wait("t", mask_done)
            for j in range(16):
                bk = 5 + (j % 2)
                sc.wait("t", bank_free[bk])
                for i in range(j + 1):
                    lhs = triu_sb if i == j else ones2d
                    inc = te if i == j else None
                    sc.op("t", lambda e, i=i, j=j, bk=bk, lhs=lhs: e.matmul(
                        ps_b[:, bk, 0:E], lhs[:, :], mask_bf[:, i, :],
                        start=(i == 0), stop=(i == j)), inc)
                t_cum = T(te.n)
                sc.wait("v", t_cum)
                sc.wait("v", ec_done)
                sc.op("v", lambda e, bk=bk: e.tensor_tensor(
                    tmp8[:, :], ps_b[:, bk, 0:E], ecap_bc[:, :], op=ALU.add), ve)
                bank_free[bk] = V(ve.n)
                sc.op("v", lambda e: e.drain())
                sc.op("v", lambda e, j=j: e.tensor_tensor(
                    tmp8[:, :], tmp8[:, :], mask_bf[:, j, :], op=ALU.mult))
                sc.op("v", lambda e: e.drain())
                sc.op("v", lambda e: e.reduce_sum(
                    possum[:, :], tmp8[:, :], axis=AX.X))
                sc.op("v", lambda e: e.drain())
                sc.op("v", lambda e, j=j: e.tensor_scalar_add(
                    pos_col[:, j:j + 1], possum[:, :], -1.0))
                sc.op("v", lambda e: e.drain(), ve)
            sc.op("v", lambda e: e.tensor_scalar_add(
                pos_colh[:, :], pos_col[:, :], float(-HSLOT)))
            sc.op("v", lambda e: e.drain(), ve)
            pos_done = V(ve.n)


            with nc.Block() as blk:
                @blk.tensor
                def _(e):
                    sc.replay("t", e)

                @blk.vector
                def _(e):
                    sc.replay("v", e)

                @blk.scalar
                def _(e):
                    sc.replay("s", e)

                @blk.gpsimd
                def _(e):
                    sc.replay("g", e)

                @blk.sync
                def _(e):
                    sc.replay("y", e)

            b1stack.close()

            def fsb2(name, shape, dtype=F32):
                return fstack.enter_context(nc.sbuf_tensor(name, shape, dtype))

            PT_sl = fsb2("PT_sl", [128, 16, CAP], BF16)
            Pg_t = fsb2("Pg_t", [128, 2, CAP // 128, 128], BF16)
            xn_g = fsb2("xn_g", [128, DC, CAP], BF16)
            u_sb = fsb2("u_sb", [128, 32, CAP], BF16)
            sig_sb = fsb2("sig_sb", [128, 4, CAP], BF16)
            w1_ch = fsb2("w1_ch", [128, 3, DC, 512], BF16)
            w2_ch = fsb2("w2_ch", [128, 2, 32, 128], BF16)
            yT_sb = fsb2("yT_sb", [128, DC, CAP], BF16)
            y_g = fsb2("y_g", [128, CAP // 128, D], BF16)
            r_in = fsb2("r_in", [128, 1, D], F32)
            out_sb = fsb2("out_sb", [128, 2, D], F32)
            ps_b = fstack.enter_context(nc.psum_tensor("ps_b2", [128, 7, 512], F32))
            ps_g2 = fstack.enter_context(nc.psum_tensor("ps_g2", [128, 4, 128], BF16))

            sc = Sched()
            bank_free = {k: None for k in range(8)}
            g2_free = [None]
            w1_used = {0: None, 1: None, 2: None}
            w2_used = {0: None, 1: None}
            sig_ready = {}
            sig_used = {}
            iota_done = None
            pos_done = None
            acc_zero = None
            bo_done = None
            pt_scat = [None]

            for ex in range(E):
                hh = ex // 4
                sl = ex % 4
                pcol = pos_col if hh == 0 else pos_colh
                sc.wait("v", pt_scat[0])
                for tc in range(16):
                    sc.op("v", lambda e, tc=tc, sl=sl, pcol=pcol: e.tensor_scalar(
                        PT_sl[:, tc, :], iota_bc[:, sl * CAP:(sl + 1) * CAP],
                        pcol[:, tc:tc + 1], None, op0=ALU.is_equal))
                sc.op("v", lambda e: e.drain(), ve)
                pt_built = V(ve.n)
                sc.wait("t", pt_built)
                for dc in range(DC):
                    bk = dc % 4
                    sc.wait("t", bank_free[bk])
                    for tc in range(16):
                        inc = te if tc == 15 else None
                        sc.op("t", lambda e, dc=dc, tc=tc, bk=bk: e.matmul(
                            ps_b[:, bk, 0:CAP], xn_tok[:, tc, dc * 128:(dc + 1) * 128],
                            PT_sl[:, tc, :], start=(tc == 0), stop=(tc == 15)), inc)
                    t_g = T(te.n)
                    sc.wait("v", t_g)
                    sc.op("v", lambda e, dc=dc, bk=bk: e.tensor_copy(
                        xn_g[:, dc, :], ps_b[:, bk, 0:CAP]), ve)
                    bank_free[bk] = V(ve.n)
                xng_done = V(ve.n)

                order = []
                for i in range(8):
                    order += [8 + i, i]
                for oi, ci in enumerate(order):
                    wb = oi % 3
                    sc.wait("g", w1_used[wb])
                    sc.op("g", lambda e, ex=ex, ci=ci, wb=wb: e.dma_start(
                        out=w1_ch[:, wb, :, :],
                        in_=w1[ex, :, ci * 512:(ci + 1) * 512]
                        .rearrange("(dc p) f -> p dc f", p=128)), c_w1[wb], 16)
                    g_w1 = (c_w1[wb], c_w1[wb].n)
                    sc.wait("t", g_w1)
                    sc.wait("t", xng_done)
                    is_g = ci >= 8
                    for fm in range(4):
                        bk = 4 + (fm % 3)
                        sc.wait("t", bank_free[bk])
                        for dc in range(DC):
                            inc = te if dc == DC - 1 else None
                            sc.op("t", lambda e, wb=wb, fm=fm, dc=dc, bk=bk: e.matmul(
                                ps_b[:, bk, 0:CAP],
                                w1_ch[:, wb, dc, fm * 128:(fm + 1) * 128],
                                xn_g[:, dc, :],
                                start=(dc == 0), stop=(dc == DC - 1)), inc)
                        t_h = T(te.n)
                        fc64 = ci * 4 + fm
                        if is_g:
                            sc.wait("s", t_h)
                            sc.wait("s", sig_used.get(fm))
                            sc.op("s", lambda e, ex=ex, fm=fm, bk=bk, fc64=fc64:
                                  e.activation(
                                      sig_sb[:, fm, :], ps_b[:, bk, 0:CAP], AF.Sigmoid,
                                      bias=b1_sb[:, ex, fc64:fc64 + 1], scale=1.0))
                            sc.op("s", lambda e: e.drain(), se)
                            bank_free[bk] = SE(se.n)
                            sig_ready[fm] = SE(se.n)
                        else:
                            sc.wait("v", t_h)
                            sc.wait("v", sig_ready[fm])
                            fc = ci * 4 + fm
                            sc.op("v", lambda e, ex=ex, fm=fm, bk=bk, fc64=fc64, fc=fc:
                                  e.scalar_tensor_tensor(
                                      u_sb[:, fc, :], ps_b[:, bk, 0:CAP],
                                      b1_sb[:, ex, fc64:fc64 + 1],
                                      sig_sb[:, fm, :],
                                      op0=ALU.add, op1=ALU.mult), ve)
                            bank_free[bk] = V(ve.n)
                            sig_used[fm] = V(ve.n)
                    w1_used[wb] = T(te.n)
                u_all = V(ve.n)

                v_yT_ev = {}
                for dch in range(DC + 1):
                    if dch < DC:
                        wb = dch % 2
                        sc.wait("g", w2_used[wb])
                        sc.op("g", lambda e, ex=ex, dch=dch, wb=wb: e.dma_start(
                            out=w2_ch[:, wb, :, :],
                            in_=w2[ex, :, dch * 128:(dch + 1) * 128]
                            .rearrange("(fc p) d -> p fc d", p=128)), c_w2[wb], 16)
                        g_w2 = (c_w2[wb], c_w2[wb].n)
                        sc.wait("t", g_w2)
                        sc.wait("t", u_all)
                        bk = 4 + (dch % 3)
                        sc.wait("t", bank_free[bk])
                        for fc in range(32):
                            inc = te if fc == 31 else None
                            sc.op("t", lambda e, wb=wb, fc=fc, bk=bk: e.matmul(
                                ps_b[:, bk, 0:CAP], w2_ch[:, wb, fc, :], u_sb[:, fc, :],
                                start=(fc == 0), stop=(fc == 31)), inc)
                        t_y = T(te.n)
                        w2_used[wb] = T(te.n)
                        sc.wait("v", t_y)
                        sc.op("v", lambda e, ex=ex, dch=dch, bk=bk: e.tensor_scalar(
                            yT_sb[:, dch, :], ps_b[:, bk, 0:CAP],
                            b2_sb[:, ex, dch:dch + 1], None, op0=ALU.add), ve)
                        bank_free[bk] = V(ve.n)
                        v_yT_ev[dch] = V(ve.n)
                    if dch >= 1:
                        dj = dch - 1
                        sc.wait("t", v_yT_ev[dj])
                        sc.wait("t", g2_free[0])
                        npc = CAP // 128
                        for js in range(npc):
                            inc = te if js == npc - 1 else None
                            sc.op("t", lambda e, dj=dj, js=js: e.transpose(
                                ps_g2[:, js, :], yT_sb[:, dj, js * 128:(js + 1) * 128],
                                ident[:, :]), inc)
                        t_ytr = T(te.n)
                        sc.wait("v", t_ytr)
                        sc.op("v", lambda e, dj=dj, npc=npc: e.tensor_copy(
                            y_g[:, :, dj * 128:(dj + 1) * 128], ps_g2[:, 0:npc, :]), ve)
                        g2_free[0] = V(ve.n)
                yg_done = V(ve.n)

                pg_ready = {}
                pg_used = {0: None, 1: None}
                for tt in range(17):
                    if tt < 16:
                        gb = tt % 2
                        npc = CAP // 128
                        sc.wait("t", g2_free[0])
                        for pc in range(npc):
                            inc = te if pc == npc - 1 else None
                            sc.op("t", lambda e, tt=tt, pc=pc: e.transpose(
                                ps_g2[:, pc, :], PT_sl[:, tt, pc * 128:(pc + 1) * 128],
                                ident[:, :]), inc)
                        t_pgt = T(te.n)
                        if tt == 15:
                            pt_scat[0] = T(te.n)
                        sc.wait("v", t_pgt)
                        sc.wait("v", pg_used[gb])
                        sc.op("v", lambda e, gb=gb, npc=npc: e.tensor_copy(
                            Pg_t[:, gb, :, :], ps_g2[:, 0:npc, :]))
                        sc.op("v", lambda e: e.drain(), ve)
                        g2_free[0] = V(ve.n)
                        pg_ready[tt] = V(ve.n)
                    if tt >= 1:
                        ts2 = tt - 1
                        gb = ts2 % 2
                        sc.wait("t", pg_ready[ts2])
                        sc.wait("t", yg_done)
                        npc2 = CAP // 128
                        for dh in range(2):
                            bk = dh
                            sc.wait("t", bank_free[bk])
                            for pc in range(npc2):
                                inc = te if pc == npc2 - 1 else None
                                sc.op("t", lambda e, pc=pc, dh=dh, bk=bk, gb=gb:
                                      e.matmul(
                                          ps_b[:, bk, :], Pg_t[:, gb, pc, :],
                                          y_g[:, pc, dh * 512:(dh + 1) * 512],
                                          start=(pc == 0), stop=(pc == npc2 - 1)), inc)
                            t_sc = T(te.n)
                            if dh == 1:
                                pg_used[gb] = T(te.n)
                            sc.wait("v", t_sc)
                            sc.op("v", lambda e, ts2=ts2, dh=dh, bk=bk: e.tensor_tensor(
                                ffn_acc[:, ts2, dh * 512:(dh + 1) * 512],
                                ffn_acc[:, ts2, dh * 512:(dh + 1) * 512],
                                ps_b[:, bk, :], op=ALU.add), ve)
                            bank_free[bk] = V(ve.n)
            ffn_done = V(ve.n)

            rin_dma = {}
            rin_used = {}
            outst = {}
            for tt in range(16):
                par = tt % 2
                sc.wait("y", rin_used.get(tt - 1))
                sc.op("y", lambda e, tt=tt: e.dma_start(
                    out=r_in[:, 0, :], in_=out[tt * 128:(tt + 1) * 128, :]),
                    c_ri[0], 16)
                rin_dma[tt] = (c_ri[0], c_ri[0].n)
                sc.wait("v", ffn_done)
                sc.wait("v", rin_dma[tt])
                sc.wait("v", bo_done)
                sc.wait("v", outst.get(tt - 2))
                sc.op("v", lambda e, tt=tt, par=par: e.tensor_tensor(
                    out_sb[:, par, :], r_in[:, 0, :], ffn_acc[:, tt, :],
                    op=ALU.add))
                sc.op("v", lambda e: e.drain())
                sc.op("v", lambda e, par=par: e.tensor_tensor(
                    out_sb[:, par, :], out_sb[:, par, :], bo_bc[:, :],
                    op=ALU.add), ve)
                rin_used[tt] = V(ve.n)
                v_o = V(ve.n)
                sc.wait("y", v_o)
                sc.op("y", lambda e, tt=tt, par=par: e.dma_start(
                    out=out[tt * 128:(tt + 1) * 128, :], in_=out_sb[:, par, :]),
                    c_os[par], 16)
                outst[tt] = (c_os[par], c_os[par].n)

            sc.wait("y", (c_os[0], c_os[0].n))
            sc.wait("y", (c_os[1], c_os[1].n))

            with nc.Block() as blk:
                @blk.tensor
                def _(e):
                    sc.replay("t", e)

                @blk.vector
                def _(e):
                    sc.replay("v", e)

                @blk.scalar
                def _(e):
                    sc.replay("s", e)

                @blk.gpsimd
                def _(e):
                    sc.replay("g", e)

                @blk.sync
                def _(e):
                    sc.replay("y", e)

    return nc


def kernel(L0_attn_x, L0_attn_w_q, L0_attn_b_q, L0_attn_w_k, L0_attn_b_k,
           L0_attn_w_v, L0_attn_b_v, L0_attn_w_out, L0_attn_b_out,
           L0_ffn_router_w, L0_ffn_router_b, L0_ffn_w1, L0_ffn_b1,
           L0_ffn_w2, L0_ffn_b2):
    nc = build_program()

    x = np.ascontiguousarray(np.asarray(L0_attn_x, dtype=np.float32))
    wqT = np.ascontiguousarray(np.asarray(L0_attn_w_q, np.float32).T)
    wkT = np.ascontiguousarray(np.asarray(L0_attn_w_k, np.float32).T)
    wvT = np.ascontiguousarray(np.asarray(L0_attn_w_v, np.float32).T)
    woT = np.ascontiguousarray(np.asarray(L0_attn_w_out, np.float32).T)
    bq = np.ascontiguousarray(np.asarray(L0_attn_b_q, np.float32).reshape(8, 128).T)
    bk = np.ascontiguousarray(np.asarray(L0_attn_b_k, np.float32).reshape(8, 128).T)
    bv = np.ascontiguousarray(np.asarray(L0_attn_b_v, np.float32).reshape(8, 128).T)
    bo = np.ascontiguousarray(np.asarray(L0_attn_b_out, np.float32).reshape(8, 128).T)
    rwh = np.ascontiguousarray(np.asarray(L0_ffn_router_w, np.float32))
    rbh = np.ascontiguousarray(np.asarray(L0_ffn_router_b, np.float32).reshape(1, E))
    w1h = np.ascontiguousarray(np.asarray(L0_ffn_w1, np.float32))
    b1h = np.ascontiguousarray(
        np.asarray(L0_ffn_b1, np.float32).reshape(E, FF // 128, 128).transpose(2, 0, 1))
    w2h = np.ascontiguousarray(np.asarray(L0_ffn_w2, np.float32))
    b2h = np.ascontiguousarray(
        np.asarray(L0_ffn_b2, np.float32).reshape(E, DC, 128).transpose(2, 0, 1))
    triu = np.triu(np.ones((128, 128), np.float32))
    ones2d_h = np.ones((128, 128), np.float32)
    iota_h = np.arange(HSLOT, dtype=np.float32).reshape(1, HSLOT)
    ecap_h = (np.arange(8, dtype=np.float32) * float(CAP)).reshape(1, 8)
    ident = np.eye(128, dtype=np.float32)
    ones = np.ones((1, 128), dtype=np.float32)

    in_maps = []
    for c in range(8):
        b, qq = c // 4, c % 4
        in_maps.append({
            "xb": np.ascontiguousarray(x[b]),
            "xq": np.ascontiguousarray(x[b, qq * TQ:(qq + 1) * TQ]),
            "wqT": wqT, "wkT": wkT, "wvT": wvT, "woT": woT,
            "bq_in": bq, "bk_in": bk, "bv_in": bv, "bo_in": bo,
            "rw": rwh, "rb_in": rbh,
            "w1": w1h, "b1_in": b1h, "w2": w2h, "b2_in": b2h,
            "triu_in": triu, "ones2d_in": ones2d_h, "iota_in": iota_h,
            "ecap_in": ecap_h,
            "ident_in": ident, "ones_in": ones,
        })

    res = run_bass_kernel_spmd(nc, in_maps, core_ids=list(range(8)))
    full = np.empty((2, S, D), dtype=np.float32)
    for c in range(8):
        b, qq = c // 4, c % 4
        full[b, qq * TQ:(qq + 1) * TQ] = res.results[c]["out"]
    return full


# version 19
# speedup vs baseline: 1.1736x; 1.1736x over previous
import contextlib
import numpy as np
import concourse.bass as bass
import concourse.mybir as mybir
from concourse.bass_utils import run_bass_kernel_spmd

F32 = mybir.dt.float32
BF16 = mybir.dt.bfloat16
AF = mybir.ActivationFunctionType
ALU = mybir.AluOpType
AX = mybir.AxisListType

D = 1024
DC = 8
S = 8192
TQ = 2048
KB = 512
NBLK = S // KB
NPRO = TQ // KB
NQT = TQ // 128
E = 8
FF = 8192
TH = 1024
SCALE = 0.03125
BIGPOS = 3.0e38
CAP = 384
HSLOT = 4 * CAP

ENGS = ("t", "v", "s", "g", "y")


class Ctr:
    def __init__(self, stack, nc, name):
        self.sem = stack.enter_context(nc.semaphore(name))
        self.n = 0


class Sched:
    def __init__(self):
        self.q = {k: [] for k in ENGS}
        self.lastw = {k: {} for k in ENGS}

    def op(self, eng, fn, inc=None, k=1):
        if inc is not None:
            inc.n += k
        self.q[eng].append((fn, inc, k))

    def wait(self, eng, ev):
        if ev is None:
            return
        ctr, v = ev
        if v <= 0 or self.lastw[eng].get(id(ctr), -1) >= v:
            return
        self.lastw[eng][id(ctr)] = v
        self.q[eng].append((None, ctr, v))

    def replay(self, eng, e):
        for fn, aux, k in self.q[eng]:
            if fn is None:
                e.wait_ge(aux.sem, k)
            else:
                inst = fn(e)
                if aux is not None:
                    inst.then_inc(aux.sem, k)


def build_program():
    nc = bass.Bass("TRN2", target_bir_lowering=False)

    xb = nc.dram_tensor("xb", [S, D], F32, kind="ExternalInput").ap()
    xq = nc.dram_tensor("xq", [TQ, D], F32, kind="ExternalInput").ap()
    wqT = nc.dram_tensor("wqT", [D, D], F32, kind="ExternalInput").ap()
    wkT = nc.dram_tensor("wkT", [D, D], F32, kind="ExternalInput").ap()
    wvT = nc.dram_tensor("wvT", [D, D], F32, kind="ExternalInput").ap()
    woT = nc.dram_tensor("woT", [D, D], F32, kind="ExternalInput").ap()
    bq_in = nc.dram_tensor("bq_in", [128, 8], F32, kind="ExternalInput").ap()
    bk_in = nc.dram_tensor("bk_in", [128, 8], F32, kind="ExternalInput").ap()
    bv_in = nc.dram_tensor("bv_in", [128, 8], F32, kind="ExternalInput").ap()
    bo_in = nc.dram_tensor("bo_in", [128, 8], F32, kind="ExternalInput").ap()
    rw = nc.dram_tensor("rw", [D, E], F32, kind="ExternalInput").ap()
    rb_in = nc.dram_tensor("rb_in", [1, E], F32, kind="ExternalInput").ap()
    w1 = nc.dram_tensor("w1", [E, D, FF], F32, kind="ExternalInput").ap()
    b1_in = nc.dram_tensor("b1_in", [128, E, FF // 128], F32, kind="ExternalInput").ap()
    w2 = nc.dram_tensor("w2", [E, FF // 2, D], F32, kind="ExternalInput").ap()
    b2_in = nc.dram_tensor("b2_in", [128, E, DC], F32, kind="ExternalInput").ap()
    triu_in = nc.dram_tensor("triu_in", [128, 128], F32, kind="ExternalInput").ap()
    ones2d_in = nc.dram_tensor("ones2d_in", [128, 128], F32, kind="ExternalInput").ap()
    iota_in = nc.dram_tensor("iota_in", [1, HSLOT], F32, kind="ExternalInput").ap()
    ecap_in = nc.dram_tensor("ecap_in", [1, E], F32, kind="ExternalInput").ap()
    ident_in = nc.dram_tensor("ident_in", [128, 128], F32, kind="ExternalInput").ap()
    ones_in = nc.dram_tensor("ones_in", [1, 128], F32, kind="ExternalInput").ap()
    out = nc.dram_tensor("out", [TQ, D], F32, kind="ExternalOutput").ap()

    with contextlib.ExitStack() as pstack:
        def psb(name, shape, dtype=F32):
            return pstack.enter_context(nc.sbuf_tensor(name, shape, dtype))

        ident = psb("ident", [128, 128], BF16)
        ones = psb("ones", [1, 128], BF16)
        eps_col = psb("eps_col", [128, 1], F32)
        bo2_col = psb("bo2_col", [128, 8], F32)

        te = Ctr(pstack, nc, "te")
        ve = Ctr(pstack, nc, "ve")
        se = Ctr(pstack, nc, "se")
        gc = Ctr(pstack, nc, "gc")
        _ctr_n = [0]

        def new_ctr(tag):
            _ctr_n[0] += 1
            return Ctr(pstack, nc, f"c{_ctr_n[0]}_{tag}")

        def T(v):
            return (te, v)

        def V(v):
            return (ve, v)

        def SE(v):
            return (se, v)

        def G(v):
            return (gd, v)

        def Y(v):
            return (sy, v)

        with contextlib.ExitStack() as astack:
            def asb(name, shape, dtype=F32):
                return astack.enter_context(nc.sbuf_tensor(name, shape, dtype))

            wq_wo = asb("wq_wo", [128, DC, D], BF16)
            wk_sb = asb("wk_sb", [128, DC, D], BF16)
            wv_sb = asb("wv_sb", [128, DC, D], BF16)
            bq_col = asb("bq_col", [128, 8], F32)
            bk_col = asb("bk_col", [128, 8], F32)
            bv_col = asb("bv_col", [128, 8], BF16)
            bo_col = asb("bo_col", [128, 8], F32)
            x_blk = asb("x_blk", [128, 4, D], BF16)
            xn_sb = asb("xn_sb", [128, 2, D], BF16)
            xnT = asb("xnT", [128, DC, KB], BF16)
            KT = asb("KT", [128, 8, KB], BF16)
            V_sb = asb("V_sb", [128, 4, D], BF16)
            QT = asb("QT", [128, 8, TQ], BF16)
            OACC = asb("OACC", [128, NQT, D], F32)
            P_sb = asb("P_sb", [128, 2, KB], BF16)
            PT_sb = asb("PT_sb", [128, 2, 4, 128], BF16)
            st12 = asb("st12", [128, 4, 12], F32)
            mv_sb = asb("mv_sb", [128, 4, 2], F32)
            lnv_sb = asb("lnv_sb", [128, 4], F32)
            rstd_sb = asb("rstd_sb", [128, 4], F32)
            nm2 = asb("nm2", [128, 2, NQT], F32)
            l2 = asb("l2", [128, 2, NQT], F32)
            nmblk = asb("nmblk", [128, NQT], F32)
            d_sb = asb("d_sb", [128, NQT], F32)
            alpha = asb("alpha", [128, NQT], F32)
            bsum = asb("bsum", [128, NQT], F32)
            linv = asb("linv", [128, NQT], F32)
            nms = asb("nms", [128, NQT], F32)
            O_bf = asb("O_bf", [128, 2, D], BF16)
            OT_sb = asb("OT_sb", [128, 2, DC, 128], BF16)
            xq_f32 = asb("xq_f32", [128, 1, D], F32)
            r_out = asb("r_out", [128, 2, D], F32)

            ps_s = astack.enter_context(nc.psum_tensor("ps_s", [128, 2, 512], F32))
            ps_o = astack.enter_context(nc.psum_tensor("ps_o", [128, 2, 2, 512], F32))
            ps_tr = astack.enter_context(nc.psum_tensor("ps_tr", [128, 2, 8, 128], BF16))

            sc = Sched()

            ps_s_free = {0: None, 1: None}
            ps_o_free = {0: None, 1: None}
            trcopy = {0: None, 1: None}
            xn_used = {0: None, 1: None}
            pt_used = {0: None, 1: None}
            p_trans = {0: None, 1: None}
            x_used = [None]
            blk_i = [0]

            sc.op("g", lambda e: e.memset(eps_col[:, :], 1e-5), gc)
            sc.op("g", lambda e: e.memset(nm2[:, 0, :], BIGPOS), gc)
            sc.op("g", lambda e: e.memset(l2[:, 0, :], 0.0), gc)
            sc.op("g", lambda e: e.memset(OACC[:, :, :], 0.0), gc)
            setup_evs = []

            def setup_dma(fn, tag):
                c = new_ctr(tag)
                sc.op("g", fn, c, 16)
                setup_evs.append((c, c.n))

            setup_dma(lambda e: e.dma_start(out=ident[:, :], in_=ident_in), "id")
            setup_dma(lambda e: e.dma_start(out=ones[:, :], in_=ones_in), "on")
            setup_dma(lambda e: e.dma_start(
                out=wq_wo[:, :, :], in_=wqT.rearrange("(dc p) e -> p dc e", p=128)), "wq")
            setup_dma(lambda e: e.dma_start(
                out=wk_sb[:, :, :], in_=wkT.rearrange("(dc p) e -> p dc e", p=128)), "wk")
            setup_dma(lambda e: e.dma_start(
                out=wv_sb[:, :, :], in_=wvT.rearrange("(dc p) e -> p dc e", p=128)), "wv")
            setup_dma(lambda e: e.dma_start(out=bq_col[:, :], in_=bq_in), "bq")
            setup_dma(lambda e: e.dma_start(out=bk_col[:, :], in_=bk_in), "bk")
            setup_dma(lambda e: e.dma_start(out=bv_col[:, :], in_=bv_in), "bv")
            setup_dma(lambda e: e.dma_start(out=bo_col[:, :], in_=bo_in), "bo")
            setup_c = (gc, gc.n)

            def wait_setup(eng):
                for ev in setup_evs:
                    sc.wait(eng, ev)
            c_x = new_ctr("x")
            c_wo = new_ctr("wo")
            c_xq = [new_ctr("xq0"), new_ctr("xq1")]
            c_ro = [new_ctr("ro0"), new_ctr("ro1")]

            def ln_block(src_ap, tok0):
                sc.wait("g", x_used[0])
                sc.op("g", lambda e, a=src_ap, t=tok0: e.dma_start(
                    out=x_blk[:, :, :],
                    in_=a[t:t + KB, :].rearrange("(tt p) d -> p tt d", p=128)), c_x, 16)
                x_dma = (c_x, c_x.n)
                sc.wait("v", x_dma)
                last = None
                s_rstd = {}
                for tt in range(4):
                    sc.op("v", lambda e, tt=tt: e.bn_stats(
                        st12[:, tt, 0:6], x_blk[:, tt, 0:512]))
                    sc.op("v", lambda e, tt=tt: e.bn_stats(
                        st12[:, tt, 6:12], x_blk[:, tt, 512:1024]))
                    sc.op("v", lambda e: e.drain())
                    sc.op("v", lambda e, tt=tt: e.bn_aggr(
                        mv_sb[:, tt, :], st12[:, tt, :]))
                    sc.op("v", lambda e: e.drain(), ve)
                    v_st = V(ve.n)
                    sc.wait("s", v_st)
                    wait_setup("s")
                    sc.wait("s", setup_c)
                    sc.op("s", lambda e, tt=tt: e.activation(
                        lnv_sb[:, tt:tt + 1], mv_sb[:, tt, 1:2], AF.Ln,
                        bias=eps_col[:, :], scale=1.0))
                    sc.op("s", lambda e: e.drain())
                    sc.op("s", lambda e, tt=tt: e.activation(
                        rstd_sb[:, tt:tt + 1], lnv_sb[:, tt:tt + 1], AF.Exp,
                        bias=0.0, scale=-0.5))
                    sc.op("s", lambda e: e.drain(), se)
                    s_rstd[tt] = SE(se.n)
                for tt in range(4):
                    xp = tt % 2
                    sc.wait("v", s_rstd[tt])
                    sc.wait("v", xn_used[xp])
                    sc.op("v", lambda e, tt=tt, xp=xp: e.tensor_scalar(
                        xn_sb[:, xp, :], x_blk[:, tt, :],
                        mv_sb[:, tt, 0:1], rstd_sb[:, tt:tt + 1],
                        op0=ALU.subtract, op1=ALU.mult), ve)
                    v_n = V(ve.n)
                    if tt == 3:
                        x_used[0] = v_n
                    par = tt % 2
                    sc.wait("t", v_n)
                    sc.wait("t", trcopy[par])
                    for dc in range(DC):
                        inc = te if dc == DC - 1 else None
                        sc.op("t", lambda e, xp=xp, dc=dc, par=par: e.transpose(
                            ps_tr[:, par, dc, :], xn_sb[:, xp, dc * 128:(dc + 1) * 128],
                            ident[:, :]), inc)
                    xn_used[xp] = T(te.n)
                    sc.wait("v", xn_used[xp])
                    sc.op("v", lambda e, tt=tt, par=par: e.tensor_copy(
                        xnT[:, :, tt * 128:(tt + 1) * 128], ps_tr[:, par, :, :]), ve)
                    trcopy[par] = V(ve.n)
                    last = V(ve.n)
                blk_i[0] += 1
                return last

            wait_setup("t")
            sc.wait("v", setup_c)

            for ob in range(NPRO):
                v_xnT = ln_block(xq, ob * KB)
                sc.wait("t", v_xnT)
                for ec in range(8):
                    par = ec % 2
                    sc.wait("t", ps_s_free[par])
                    for dc in range(DC):
                        inc = te if dc == DC - 1 else None
                        sc.op("t", lambda e, ec=ec, dc=dc, par=par: e.matmul(
                            ps_s[:, par, :], wq_wo[:, dc, ec * 128:(ec + 1) * 128],
                            xnT[:, dc, :], start=(dc == 0), stop=(dc == DC - 1)), inc)
                    t_q = T(te.n)
                    sc.wait("v", t_q)
                    sc.op("v", lambda e, ec=ec, ob=ob, par=par: e.tensor_scalar(
                        QT[:, ec, ob * KB:(ob + 1) * KB], ps_s[:, par, :],
                        bq_col[:, ec:ec + 1], None, op0=ALU.add), ve)
                    ps_s_free[par] = V(ve.n)
            qt_done_te = T(te.n)
            qt_done_ve = V(ve.n)

            sc.wait("g", qt_done_te)
            sc.op("g", lambda e: e.dma_start(
                out=wq_wo[:, :, :], in_=woT.rearrange("(dc p) e -> p dc e", p=128)), c_wo, 16)
            wo_loaded = (c_wo, c_wo.n)

            sc.wait("t", wo_loaded)
            for ec in range(8):
                sc.wait("t", ps_s_free[0])
                for dc in range(DC):
                    inc = te if dc == DC - 1 else None
                    sc.op("t", lambda e, ec=ec, dc=dc: e.matmul(
                        ps_s[:, 0, 0:1], wq_wo[:, dc, ec * 128:(ec + 1) * 128],
                        bv_col[:, dc:dc + 1], start=(dc == 0), stop=(dc == DC - 1)), inc)
                t_b = T(te.n)
                sc.wait("v", t_b)
                sc.op("v", lambda e, ec=ec: e.tensor_scalar(
                    bo2_col[:, ec:ec + 1], ps_s[:, 0, 0:1],
                    bo_col[:, ec:ec + 1], None, op0=ALU.add), ve)
                ps_s_free[0] = V(ve.n)

            v_xnT_cur = ln_block(xb, 0)
            for kb in range(NBLK):
                rp, wp = kb % 2, (kb + 1) % 2
                sc.wait("t", v_xnT_cur)
                sc.wait("t", qt_done_ve)
                kt_ready = None
                v_ready = None

                def emit_kt(ec):
                    nonlocal kt_ready
                    par = ec % 2
                    sc.wait("t", ps_s_free[par])
                    for dc in range(DC):
                        inc = te if dc == DC - 1 else None
                        sc.op("t", lambda e, ec=ec, dc=dc, par=par: e.matmul(
                            ps_s[:, par, :], wk_sb[:, dc, ec * 128:(ec + 1) * 128],
                            xnT[:, dc, :], start=(dc == 0), stop=(dc == DC - 1)), inc)
                    t_k = T(te.n)
                    sc.wait("v", t_k)
                    sc.op("v", lambda e, ec=ec, par=par: e.tensor_scalar(
                        KT[:, ec, :], ps_s[:, par, :],
                        bk_col[:, ec:ec + 1], None, op0=ALU.add), ve)
                    ps_s_free[par] = V(ve.n)
                    kt_ready = V(ve.n)

                def emit_v(tt):
                    nonlocal v_ready
                    par = tt % 2
                    sc.wait("t", ps_o_free[par])
                    for eh in range(2):
                        for dc in range(DC):
                            inc = te if (eh == 1 and dc == DC - 1) else None
                            sc.op("t", lambda e, tt=tt, eh=eh, dc=dc, par=par: e.matmul(
                                ps_o[:, par, eh, :],
                                xnT[:, dc, tt * 128:(tt + 1) * 128],
                                wv_sb[:, dc, eh * 512:(eh + 1) * 512],
                                start=(dc == 0), stop=(dc == DC - 1)), inc)
                    t_v = T(te.n)
                    sc.wait("v", t_v)
                    sc.op("v", lambda e, tt=tt, par=par: e.tensor_copy(
                        V_sb[:, tt, :],
                        ps_o[:, par, :, :].rearrange("p a b -> p (a b)")), ve)
                    ps_o_free[par] = V(ve.n)
                    v_ready = V(ve.n)

                for i4 in range(4):
                    emit_kt(2 * i4)
                    emit_kt(2 * i4 + 1)
                    emit_v(i4)

                v_xnT_cur = (ln_block(xb, (kb + 1) * KB)
                             if kb + 1 < NBLK else None)

                s_p_ev = {}
                v_pt_ev = {}
                for qi in range(NQT + 2):
                    if qi < NQT:
                        par = qi % 2
                        sc.wait("t", kt_ready)
                        sc.wait("t", ps_s_free[par])
                        for ec in range(8):
                            inc = te if ec == 7 else None
                            sc.op("t", lambda e, ec=ec, qi=qi, par=par: e.matmul(
                                ps_s[:, par, :], QT[:, ec, qi * 128:(qi + 1) * 128],
                                KT[:, ec, :], start=(ec == 0), stop=(ec == 7)), inc)
                        t_s = T(te.n)
                        sc.wait("v", t_s)
                        sc.op("v", lambda e, qi=qi, par=par: e.reduce_max(
                            nmblk[:, qi:qi + 1], ps_s[:, par, :], axis=AX.X,
                            negate=True))
                        sc.op("v", lambda e: e.drain())
                        sc.op("v", lambda e, qi=qi, rp=rp, wp=wp: e.tensor_tensor(
                            nm2[:, wp, qi:qi + 1], nm2[:, rp, qi:qi + 1],
                            nmblk[:, qi:qi + 1], op=ALU.min))
                        sc.op("v", lambda e: e.drain())
                        sc.op("v", lambda e, qi=qi, rp=rp, wp=wp: e.tensor_tensor(
                            d_sb[:, qi:qi + 1], nm2[:, wp, qi:qi + 1],
                            nm2[:, rp, qi:qi + 1], op=ALU.subtract))
                        sc.op("v", lambda e: e.drain(), ve)
                        v_stat = V(ve.n)
                        pp = qi % 2
                        sc.wait("s", v_stat)
                        sc.wait("s", p_trans[pp])
                        sc.op("s", lambda e, qi=qi: e.activation(
                            alpha[:, qi:qi + 1], d_sb[:, qi:qi + 1], AF.Exp,
                            bias=0.0, scale=SCALE))
                        sc.op("s", lambda e, qi=qi, wp=wp: e.mul(
                            nms[:, qi:qi + 1], nm2[:, wp, qi:qi + 1], SCALE))
                        sc.op("s", lambda e: e.drain())
                        sc.op("s", lambda e, qi=qi, par=par, pp=pp: e.activation(
                            P_sb[:, pp, :], ps_s[:, par, :], AF.Exp,
                            bias=nms[:, qi:qi + 1], scale=SCALE,
                            accum_out=bsum[:, qi:qi + 1]))
                        sc.op("s", lambda e: e.drain(), se)
                        s_p_ev[qi] = SE(se.n)
                        ps_s_free[par] = SE(se.n)
                        sc.wait("v", s_p_ev[qi])
                        sc.op("v", lambda e, qi=qi, rp=rp, wp=wp: e.scalar_tensor_tensor(
                            l2[:, wp, qi:qi + 1], l2[:, rp, qi:qi + 1],
                            alpha[:, qi:qi + 1], bsum[:, qi:qi + 1],
                            op0=ALU.mult, op1=ALU.add), ve)
                    if 1 <= qi <= NQT:
                        qj = qi - 1
                        par = qj % 2
                        pp = qj % 2
                        sc.wait("t", s_p_ev[qj])
                        sc.wait("t", trcopy[par])
                        for kc in range(4):
                            inc = te if kc == 3 else None
                            sc.op("t", lambda e, kc=kc, par=par, pp=pp: e.transpose(
                                ps_tr[:, par, kc, :],
                                P_sb[:, pp, kc * 128:(kc + 1) * 128],
                                ident[:, :]), inc)
                        p_trans[pp] = T(te.n)
                        t_ptr = T(te.n)
                        sc.wait("v", t_ptr)
                        sc.wait("v", pt_used[par])
                        sc.op("v", lambda e, par=par: e.tensor_copy(
                            PT_sb[:, par, :, :], ps_tr[:, par, 0:4, :]), ve)
                        trcopy[par] = V(ve.n)
                        v_pt_ev[qj] = V(ve.n)
                    if qi >= 2:
                        qj = qi - 2
                        par = qj % 2
                        sc.wait("t", v_pt_ev[qj])
                        sc.wait("t", v_ready)
                        sc.wait("t", ps_o_free[par])
                        for kc in range(4):
                            for h in range(2):
                                inc = te if (kc == 3 and h == 1) else None
                                sc.op("t", lambda e, kc=kc, h=h, par=par: e.matmul(
                                    ps_o[:, par, h, :], PT_sb[:, par, kc, :],
                                    V_sb[:, kc, h * 512:(h + 1) * 512],
                                    start=(kc == 0), stop=(kc == 3)), inc)
                        t_av = T(te.n)
                        pt_used[par] = T(te.n)
                        sc.wait("v", t_av)
                        for h in range(2):
                            sc.op("v",
                                  lambda e, qj=qj, h=h, par=par: e.scalar_tensor_tensor(
                                      OACC[:, qj, h * 512:(h + 1) * 512],
                                      OACC[:, qj, h * 512:(h + 1) * 512],
                                      alpha[:, qj:qj + 1], ps_o[:, par, h, :],
                                      op0=ALU.mult, op1=ALU.add), ve)
                        ps_o_free[par] = V(ve.n)

            final_p = NBLK % 2
            xq_dma = {}
            xq_used = {}
            rstored = {}
            ot_used = {}
            obf_used = {}
            for qi in range(NQT):
                par = qi % 2
                sc.wait("y", xq_used.get(qi - 1))
                sc.op("y", lambda e, qi=qi: e.dma_start(
                    out=xq_f32[:, 0, :], in_=xq[qi * 128:(qi + 1) * 128, :]),
                    c_xq[0], 16)
                xq_dma[qi] = (c_xq[0], c_xq[0].n)
                sc.op("v", lambda e, qi=qi, fp=final_p: e.reciprocal(
                    linv[:, qi:qi + 1], l2[:, fp, qi:qi + 1]))
                sc.op("v", lambda e: e.drain(), ve)
                v_linv = V(ve.n)
                sc.wait("s", v_linv)
                sc.wait("s", obf_used.get(par))
                sc.op("s", lambda e, qi=qi, par=par: e.mul(
                    O_bf[:, par, :], OACC[:, qi, :], linv[:, qi:qi + 1]), se)
                s_obf = SE(se.n)
                sc.wait("t", s_obf)
                sc.wait("t", trcopy[par])
                for dc in range(DC):
                    inc = te if dc == DC - 1 else None
                    sc.op("t", lambda e, dc=dc, par=par: e.transpose(
                        ps_tr[:, par, dc, :], O_bf[:, par, dc * 128:(dc + 1) * 128],
                        ident[:, :]), inc)
                obf_used[par] = T(te.n)
                t_otr = T(te.n)
                sc.wait("v", t_otr)
                sc.wait("v", ot_used.get(par))
                sc.op("v", lambda e, par=par: e.tensor_copy(
                    OT_sb[:, par, :, :], ps_tr[:, par, :, :]), ve)
                trcopy[par] = V(ve.n)
                v_ot = V(ve.n)
                sc.wait("t", v_ot)
                sc.wait("t", wo_loaded)
                sc.wait("t", ps_o_free[par])
                for h in range(2):
                    for dc in range(DC):
                        inc = te if (h == 1 and dc == DC - 1) else None
                        sc.op("t", lambda e, h=h, dc=dc, par=par: e.matmul(
                            ps_o[:, par, h, :], OT_sb[:, par, dc, :],
                            wq_wo[:, dc, h * 512:(h + 1) * 512],
                            start=(dc == 0), stop=(dc == DC - 1)), inc)
                t_proj = T(te.n)
                ot_used[par] = T(te.n)
                sc.wait("v", t_proj)
                sc.wait("v", xq_dma[qi])
                sc.wait("v", rstored.get(qi - 2))
                for h in range(2):
                    inc = ve if h == 1 else None
                    sc.op("v", lambda e, h=h, par=par: e.tensor_tensor(
                        r_out[:, par, h * 512:(h + 1) * 512],
                        xq_f32[:, 0, h * 512:(h + 1) * 512],
                        ps_o[:, par, h, :], op=ALU.add), inc)
                ps_o_free[par] = V(ve.n)
                xq_used[qi] = V(ve.n)
                v_r = V(ve.n)
                sc.wait("y", v_r)
                sc.op("y", lambda e, qi=qi, par=par: e.dma_start(
                    out=out[qi * 128:(qi + 1) * 128, :], in_=r_out[:, par, :]),
                    c_ro[par], 16)
                rstored[qi] = (c_ro[par], c_ro[par].n)

            sc.wait("y", (c_ro[0], c_ro[0].n))
            sc.wait("y", (c_ro[1], c_ro[1].n))

            with nc.Block() as blk:
                @blk.tensor
                def _(e):
                    sc.replay("t", e)

                @blk.vector
                def _(e):
                    sc.replay("v", e)

                @blk.scalar
                def _(e):
                    sc.replay("s", e)

                @blk.gpsimd
                def _(e):
                    sc.replay("g", e)

                @blk.sync
                def _(e):
                    sc.replay("y", e)

        with contextlib.ExitStack() as fstack:
            def fsb(name, shape, dtype=F32):
                return fstack.enter_context(nc.sbuf_tensor(name, shape, dtype))

            xn_tok = fsb("xn_tok", [128, 16, D], BF16)
            ffn_acc = fsb("ffn_acc", [128, 16, D], BF16)
            iota_bc = fsb("iota_bc", [128, HSLOT], F32)
            pos_col = fsb("pos_col", [128, 16], F32)
            pos_colh = fsb("pos_colh", [128, 16], F32)
            b2_sb = fsb("b2_sb", [128, E, DC], F32)
            ones_f32 = fsb("ones_f32", [1, 128], F32)
            b1_sb = fsb("b1_sb", [128, E, 64], F32)
            bo_bc = fsb("bo_bc", [128, D], BF16)

            b1stack = contextlib.ExitStack()

            def fsb1(name, shape, dtype=F32):
                return b1stack.enter_context(nc.sbuf_tensor(name, shape, dtype))

            xf_bf = fsb1("xf_bf", [128, 4, D], BF16)
            xnTb = fsb1("xnTb", [128, DC, 512], BF16)
            st12f = fsb1("st12f", [128, 4, 12], F32)
            mvf = fsb1("mvf", [128, 4, 2], F32)
            lnvf = fsb1("lnvf", [128, 4], F32)
            rstdf = fsb1("rstdf", [128, 4], F32)
            rw_sb = fsb1("rw_sb", [128, DC, E], BF16)
            rb_row = fsb1("rb_row", [1, E], BF16)
            rb_bc = fsb1("rb_bc", [128, E], BF16)
            logits = fsb1("logits", [128, 16, E], F32)
            lgmax = fsb1("lgmax", [128, 16], F32)
            mask_bf = fsb1("mask_bf", [128, 16, E], BF16)
            triu_sb = fsb1("triu_sb", [128, 128], BF16)
            ones2d = fsb1("ones2d", [128, 128], BF16)
            iota_row = fsb1("iota_row", [1, HSLOT], F32)
            ecap_row = fsb1("ecap_row", [1, E], F32)
            ecap_bc = fsb1("ecap_bc", [128, E], F32)
            tmp8 = fsb1("tmp8", [128, E], F32)
            possum = fsb1("possum", [128, 1], F32)
            pos_row = fsb1("pos_row", [1, 2048], F32)
            bo2_bf = fsb1("bo2_bf", [128, 8], BF16)
            borow = fsb1("borow", [1, D], BF16)
            ps_b = b1stack.enter_context(nc.psum_tensor("ps_b", [128, 7, 512], F32))
            ps_trf = b1stack.enter_context(nc.psum_tensor("ps_trf", [128, 8, 128], BF16))

            sc = Sched()

            bank_free = {k: None for k in range(8)}
            trf_free = [None]
            w1_used = {0: None, 1: None}
            w2_used = {0: None, 1: None}
            sig_ready = {}
            sig_used = {}
            xf_used = [None]

            c_fs = [new_ctr("fs") for _ in range(5)]
            c_ys = [new_ctr("ys") for _ in range(5)]
            c_xf = new_ctr("xf")
            c_w1 = [new_ctr("w1a"), new_ctr("w1b"), new_ctr("w1c")]
            c_w2 = [new_ctr("w2a"), new_ctr("w2b")]
            c_ri = [new_ctr("ri0"), new_ctr("ri1")]
            c_os = [new_ctr("os0"), new_ctr("os1")]

            sc.op("g", lambda e: e.dma_start(
                out=rw_sb[:, :, :], in_=rw.rearrange("(dc p) e -> p dc e", p=128)),
                c_fs[0], 16)
            sc.op("g", lambda e: e.dma_start(out=b1_sb[:, :, :], in_=b1_in), c_fs[1], 16)
            sc.op("g", lambda e: e.dma_start(out=rb_row[:, :], in_=rb_in), c_fs[2], 16)
            sc.op("g", lambda e: e.dma_start(out=triu_sb[:, :], in_=triu_in), c_fs[3], 16)
            sc.op("g", lambda e: e.dma_start(out=ones2d[:, :], in_=ones2d_in), c_fs[4], 16)
            sc.op("y", lambda e: e.dma_start(out=iota_row[:, :], in_=iota_in), c_ys[0], 16)
            sc.op("y", lambda e: e.dma_start(out=ecap_row[:, :], in_=ecap_in), c_ys[1], 16)
            sc.op("g", lambda e: e.dma_start(out=b2_sb[:, :, :], in_=b2_in), c_ys[2], 16)
            sc.op("y", lambda e: e.dma_start(out=ones_f32[:, :], in_=ones_in), c_ys[3], 16)
            f_evs = [(c, 16) for c in c_fs] + [(c, 16) for c in c_ys[:4]]

            def wait_fsetup(eng):
                for ev in f_evs:
                    sc.wait(eng, ev)

            sc.op("v", lambda e: e.memset(ffn_acc[:, :, :], 0.0), ve)
            acc_zero = V(ve.n)

            wait_fsetup("t")
            sc.op("t", lambda e: e.matmul(
                ps_b[:, 0, 0:E], ones[:, :], rb_row[:, :], start=True, stop=True), te)
            t_rb = T(te.n)
            sc.wait("v", t_rb)
            sc.op("v", lambda e: e.tensor_copy(rb_bc[:, :], ps_b[:, 0, 0:E]), ve)
            rb_done = V(ve.n)
            bank_free[0] = rb_done

            sc.op("t", lambda e: e.matmul(
                ps_b[:, 1, 0:E], ones_f32[:, :], ecap_row[:, :],
                start=True, stop=True), te)
            t_ec = T(te.n)
            sc.wait("v", t_ec)
            sc.op("v", lambda e: e.tensor_copy(ecap_bc[:, :], ps_b[:, 1, 0:E]), ve)
            ec_done = V(ve.n)
            bank_free[1] = ec_done

            for j in range(HSLOT // 512):
                sc.op("t", lambda e, j=j: e.matmul(
                    ps_b[:, 2, :], ones_f32[:, :], iota_row[:, j * 512:(j + 1) * 512],
                    start=True, stop=True), te)
                t_io = T(te.n)
                sc.wait("v", t_io)
                sc.op("v", lambda e, j=j: e.tensor_copy(
                    iota_bc[:, j * 512:(j + 1) * 512], ps_b[:, 2, :]), ve)
                bank_free[2] = V(ve.n)
                sc.wait("t", bank_free[2])
            iota_done = V(ve.n)

            sc.op("v", lambda e: e.tensor_copy(bo2_bf[:, :], bo2_col[:, :]), ve)
            v_b2 = V(ve.n)
            sc.wait("t", v_b2)
            for dc in range(DC):
                inc = te if dc == DC - 1 else None
                sc.op("t", lambda e, dc=dc: e.transpose(
                    ps_trf[0:1, dc, :], bo2_bf[:, dc:dc + 1], ident[:, :]), inc)
            t_bt = T(te.n)
            sc.wait("v", t_bt)
            sc.op("v", lambda e: e.tensor_copy(
                borow[:, :], ps_trf[0:1, :, :].rearrange("p a b -> p (a b)")))
            sc.op("v", lambda e: e.drain(), ve)
            trf_free[0] = V(ve.n)
            v_brow = V(ve.n)
            sc.wait("t", v_brow)
            for h2 in range(2):
                sc.wait("t", bank_free[3])
                sc.op("t", lambda e, h2=h2: e.matmul(
                    ps_b[:, 3, :], ones[:, :], borow[:, h2 * 512:(h2 + 1) * 512],
                    start=True, stop=True), te)
                t_bo = T(te.n)
                sc.wait("v", t_bo)
                sc.op("v", lambda e, h2=h2: e.tensor_copy(
                    bo_bc[:, h2 * 512:(h2 + 1) * 512], ps_b[:, 3, :]), ve)
                bank_free[3] = V(ve.n)
            bo_done = V(ve.n)

            for b4 in range(4):
                sc.wait("g", xf_used[0])
                sc.op("g", lambda e, t0=b4 * 512: e.dma_start(
                    out=xf_bf[:, :, :],
                    in_=xq[t0:t0 + 512, :].rearrange("(tt p) d -> p tt d", p=128)),
                    c_xf, 16)
                g_x = (c_xf, c_xf.n)
                sc.wait("v", g_x)
                s_rstd = {}
                for tt in range(4):
                    sc.op("v", lambda e, tt=tt: e.bn_stats(
                        st12f[:, tt, 0:6], xf_bf[:, tt, 0:512]))
                    sc.op("v", lambda e, tt=tt: e.bn_stats(
                        st12f[:, tt, 6:12], xf_bf[:, tt, 512:1024]))
                    sc.op("v", lambda e: e.drain())
                    sc.op("v", lambda e, tt=tt: e.bn_aggr(
                        mvf[:, tt, :], st12f[:, tt, :]))
                    sc.op("v", lambda e: e.drain(), ve)
                    v_st = V(ve.n)
                    sc.wait("s", v_st)
                    sc.op("s", lambda e, tt=tt: e.activation(
                        lnvf[:, tt:tt + 1], mvf[:, tt, 1:2], AF.Ln,
                        bias=eps_col[:, :], scale=1.0))
                    sc.op("s", lambda e: e.drain())
                    sc.op("s", lambda e, tt=tt: e.activation(
                        rstdf[:, tt:tt + 1], lnvf[:, tt:tt + 1], AF.Exp,
                        bias=0.0, scale=-0.5))
                    sc.op("s", lambda e: e.drain(), se)
                    s_rstd[tt] = SE(se.n)
                for tt in range(4):
                    gt = b4 * 4 + tt
                    sc.wait("v", s_rstd[tt])
                    sc.op("v", lambda e, tt=tt, gt=gt: e.tensor_scalar(
                        xn_tok[:, gt, :], xf_bf[:, tt, :],
                        mvf[:, tt, 0:1], rstdf[:, tt:tt + 1],
                        op0=ALU.subtract, op1=ALU.mult), ve)
                    v_n = V(ve.n)
                    sc.wait("t", v_n)
                    sc.wait("t", trf_free[0])
                    for dc in range(DC):
                        inc = te if dc == DC - 1 else None
                        sc.op("t", lambda e, gt=gt, dc=dc: e.transpose(
                            ps_trf[:, dc, :], xn_tok[:, gt, dc * 128:(dc + 1) * 128],
                            ident[:, :]), inc)
                    t_tr = T(te.n)
                    sc.wait("v", t_tr)
                    sc.op("v", lambda e, tt=tt: e.tensor_copy(
                        xnTb[:, :, tt * 128:(tt + 1) * 128], ps_trf[:, :, :]), ve)
                    trf_free[0] = V(ve.n)
                xnTb_done = V(ve.n)
                xf_used[0] = V(ve.n)
                sc.wait("t", xnTb_done)
                for tt in range(4):
                    gt = b4 * 4 + tt
                    bk = 4 + (tt % 2)
                    sc.wait("t", bank_free[bk])
                    for dc in range(DC):
                        inc = te if dc == DC - 1 else None
                        sc.op("t", lambda e, tt=tt, dc=dc, bk=bk: e.matmul(
                            ps_b[:, bk, 0:E], xnTb[:, dc, tt * 128:(tt + 1) * 128],
                            rw_sb[:, dc, :], start=(dc == 0), stop=(dc == DC - 1)), inc)
                    t_lg = T(te.n)
                    sc.wait("v", t_lg)
                    sc.wait("v", rb_done)
                    sc.op("v", lambda e, gt=gt, bk=bk: e.tensor_tensor(
                        logits[:, gt, :], ps_b[:, bk, 0:E], rb_bc[:, :], op=ALU.add), ve)
                    bank_free[bk] = V(ve.n)
                    sc.op("v", lambda e: e.drain())
                    sc.op("v", lambda e, gt=gt: e.reduce_max(
                        lgmax[:, gt:gt + 1], logits[:, gt, :], axis=AX.X))
                    sc.op("v", lambda e: e.drain())
                    sc.op("v", lambda e, gt=gt: e.tensor_scalar(
                        mask_bf[:, gt, :], logits[:, gt, :],
                        lgmax[:, gt:gt + 1], None, op0=ALU.is_ge))
                    sc.op("v", lambda e: e.drain(), ve)
            mask_done = V(ve.n)

            sc.wait("t", mask_done)
            for j in range(16):
                bk = 5 + (j % 2)
                sc.wait("t", bank_free[bk])
                for i in range(j + 1):
                    lhs = triu_sb if i == j else ones2d
                    inc = te if i == j else None
                    sc.op("t", lambda e, i=i, j=j, bk=bk, lhs=lhs: e.matmul(
                        ps_b[:, bk, 0:E], lhs[:, :], mask_bf[:, i, :],
                        start=(i == 0), stop=(i == j)), inc)
                t_cum = T(te.n)
                sc.wait("v", t_cum)
                sc.wait("v", ec_done)
                sc.op("v", lambda e, bk=bk: e.tensor_tensor(
                    tmp8[:, :], ps_b[:, bk, 0:E], ecap_bc[:, :], op=ALU.add), ve)
                bank_free[bk] = V(ve.n)
                sc.op("v", lambda e: e.drain())
                sc.op("v", lambda e, j=j: e.tensor_tensor(
                    tmp8[:, :], tmp8[:, :], mask_bf[:, j, :], op=ALU.mult))
                sc.op("v", lambda e: e.drain())
                sc.op("v", lambda e: e.reduce_sum(
                    possum[:, :], tmp8[:, :], axis=AX.X))
                sc.op("v", lambda e: e.drain())
                sc.op("v", lambda e, j=j: e.tensor_scalar_add(
                    pos_col[:, j:j + 1], possum[:, :], -1.0))
                sc.op("v", lambda e: e.drain(), ve)
            sc.op("v", lambda e: e.tensor_scalar_add(
                pos_colh[:, :], pos_col[:, :], float(-HSLOT)))
            sc.op("v", lambda e: e.drain(), ve)
            pos_done = V(ve.n)


            with nc.Block() as blk:
                @blk.tensor
                def _(e):
                    sc.replay("t", e)

                @blk.vector
                def _(e):
                    sc.replay("v", e)

                @blk.scalar
                def _(e):
                    sc.replay("s", e)

                @blk.gpsimd
                def _(e):
                    sc.replay("g", e)

                @blk.sync
                def _(e):
                    sc.replay("y", e)

            b1stack.close()

            def fsb2(name, shape, dtype=F32):
                return fstack.enter_context(nc.sbuf_tensor(name, shape, dtype))

            PT_sl = fsb2("PT_sl", [128, 16, CAP], BF16)
            Pg_t = fsb2("Pg_t", [128, 2, CAP // 128, 128], BF16)
            xn_g = fsb2("xn_g", [128, DC, CAP], BF16)
            u_sb = fsb2("u_sb", [128, 32, CAP], BF16)
            sig_sb = fsb2("sig_sb", [128, 8, CAP], BF16)
            w1_ch = fsb2("w1_ch", [128, 2, DC, 1024], BF16)
            w2_ch = fsb2("w2_ch", [128, 2, 32, 128], BF16)
            yT_sb = fsb2("yT_sb", [128, DC, CAP], BF16)
            y_g = fsb2("y_g", [128, CAP // 128, D], BF16)
            r_in = fsb2("r_in", [128, 1, D], F32)
            out_sb = fsb2("out_sb", [128, 2, D], F32)
            ps_b = fstack.enter_context(nc.psum_tensor("ps_b2", [128, 7, 512], F32))
            ps_g2 = fstack.enter_context(nc.psum_tensor("ps_g2", [128, 4, 128], BF16))

            sc = Sched()
            bank_free = {k: None for k in range(8)}
            g2_free = [None]
            w1_used = {0: None, 1: None}
            w2_used = {0: None, 1: None}
            sig_ready = {}
            sig_used = {}
            iota_done = None
            pos_done = None
            acc_zero = None
            bo_done = None
            pt_scat = [None]

            for ex in range(E):
                hh = ex // 4
                sl = ex % 4
                pcol = pos_col if hh == 0 else pos_colh
                sc.wait("v", pt_scat[0])
                for tc in range(16):
                    sc.op("v", lambda e, tc=tc, sl=sl, pcol=pcol: e.tensor_scalar(
                        PT_sl[:, tc, :], iota_bc[:, sl * CAP:(sl + 1) * CAP],
                        pcol[:, tc:tc + 1], None, op0=ALU.is_equal))
                sc.op("v", lambda e: e.drain(), ve)
                pt_built = V(ve.n)
                sc.wait("t", pt_built)
                for dc in range(DC):
                    bk = dc % 4
                    sc.wait("t", bank_free[bk])
                    for tc in range(16):
                        inc = te if tc == 15 else None
                        sc.op("t", lambda e, dc=dc, tc=tc, bk=bk: e.matmul(
                            ps_b[:, bk, 0:CAP], xn_tok[:, tc, dc * 128:(dc + 1) * 128],
                            PT_sl[:, tc, :], start=(tc == 0), stop=(tc == 15)), inc)
                    t_g = T(te.n)
                    sc.wait("v", t_g)
                    sc.op("v", lambda e, dc=dc, bk=bk: e.tensor_copy(
                        xn_g[:, dc, :], ps_b[:, bk, 0:CAP]), ve)
                    bank_free[bk] = V(ve.n)
                xng_done = V(ve.n)

                order = []
                for i in range(4):
                    order += [4 + i, i]
                for oi, ci in enumerate(order):
                    wb = oi % 2
                    sc.wait("g", w1_used[wb])
                    sc.op("g", lambda e, ex=ex, ci=ci, wb=wb: e.dma_start(
                        out=w1_ch[:, wb, :, :],
                        in_=w1[ex, :, ci * 1024:(ci + 1) * 1024]
                        .rearrange("(dc p) f -> p dc f", p=128)), c_w1[wb], 16)
                    g_w1 = (c_w1[wb], c_w1[wb].n)
                    sc.wait("t", g_w1)
                    sc.wait("t", xng_done)
                    is_g = ci >= 4
                    for fm in range(8):
                        bk = 4 + (fm % 3)
                        sc.wait("t", bank_free[bk])
                        for dc in range(DC):
                            inc = te if dc == DC - 1 else None
                            sc.op("t", lambda e, wb=wb, fm=fm, dc=dc, bk=bk: e.matmul(
                                ps_b[:, bk, 0:CAP],
                                w1_ch[:, wb, dc, fm * 128:(fm + 1) * 128],
                                xn_g[:, dc, :],
                                start=(dc == 0), stop=(dc == DC - 1)), inc)
                        t_h = T(te.n)
                        fc64 = ci * 8 + fm
                        fs = fm % 8
                        if is_g:
                            sc.wait("s", t_h)
                            sc.wait("s", sig_used.get(fs))
                            sc.op("s", lambda e, ex=ex, fs=fs, bk=bk, fc64=fc64:
                                  e.activation(
                                      sig_sb[:, fs, :], ps_b[:, bk, 0:CAP], AF.Sigmoid,
                                      bias=b1_sb[:, ex, fc64:fc64 + 1], scale=1.0))
                            sc.op("s", lambda e: e.drain(), se)
                            bank_free[bk] = SE(se.n)
                            sig_ready[fs] = SE(se.n)
                        else:
                            sc.wait("v", t_h)
                            sc.wait("v", sig_ready[fs])
                            fc = ci * 8 + fm
                            sc.op("v", lambda e, ex=ex, fs=fs, bk=bk, fc64=fc64, fc=fc:
                                  e.scalar_tensor_tensor(
                                      u_sb[:, fc, :], ps_b[:, bk, 0:CAP],
                                      b1_sb[:, ex, fc64:fc64 + 1],
                                      sig_sb[:, fs, :],
                                      op0=ALU.add, op1=ALU.mult), ve)
                            bank_free[bk] = V(ve.n)
                            sig_used[fs] = V(ve.n)
                    w1_used[wb] = T(te.n)
                u_all = V(ve.n)

                v_yT_ev = {}
                for dch in range(DC + 1):
                    if dch < DC:
                        wb = dch % 2
                        sc.wait("g", w2_used[wb])
                        sc.op("g", lambda e, ex=ex, dch=dch, wb=wb: e.dma_start(
                            out=w2_ch[:, wb, :, :],
                            in_=w2[ex, :, dch * 128:(dch + 1) * 128]
                            .rearrange("(fc p) d -> p fc d", p=128)), c_w2[wb], 16)
                        g_w2 = (c_w2[wb], c_w2[wb].n)
                        sc.wait("t", g_w2)
                        sc.wait("t", u_all)
                        bk = 4 + (dch % 3)
                        sc.wait("t", bank_free[bk])
                        for fc in range(32):
                            inc = te if fc == 31 else None
                            sc.op("t", lambda e, wb=wb, fc=fc, bk=bk: e.matmul(
                                ps_b[:, bk, 0:CAP], w2_ch[:, wb, fc, :], u_sb[:, fc, :],
                                start=(fc == 0), stop=(fc == 31)), inc)
                        t_y = T(te.n)
                        w2_used[wb] = T(te.n)
                        sc.wait("v", t_y)
                        sc.op("v", lambda e, ex=ex, dch=dch, bk=bk: e.tensor_scalar(
                            yT_sb[:, dch, :], ps_b[:, bk, 0:CAP],
                            b2_sb[:, ex, dch:dch + 1], None, op0=ALU.add), ve)
                        bank_free[bk] = V(ve.n)
                        v_yT_ev[dch] = V(ve.n)
                    if dch >= 1:
                        dj = dch - 1
                        sc.wait("t", v_yT_ev[dj])
                        sc.wait("t", g2_free[0])
                        npc = CAP // 128
                        for js in range(npc):
                            inc = te if js == npc - 1 else None
                            sc.op("t", lambda e, dj=dj, js=js: e.transpose(
                                ps_g2[:, js, :], yT_sb[:, dj, js * 128:(js + 1) * 128],
                                ident[:, :]), inc)
                        t_ytr = T(te.n)
                        sc.wait("v", t_ytr)
                        sc.op("v", lambda e, dj=dj, npc=npc: e.tensor_copy(
                            y_g[:, :, dj * 128:(dj + 1) * 128], ps_g2[:, 0:npc, :]), ve)
                        g2_free[0] = V(ve.n)
                yg_done = V(ve.n)

                pg_ready = {}
                pg_used = {0: None, 1: None}
                for tt in range(17):
                    if tt < 16:
                        gb = tt % 2
                        npc = CAP // 128
                        sc.wait("t", g2_free[0])
                        for pc in range(npc):
                            inc = te if pc == npc - 1 else None
                            sc.op("t", lambda e, tt=tt, pc=pc: e.transpose(
                                ps_g2[:, pc, :], PT_sl[:, tt, pc * 128:(pc + 1) * 128],
                                ident[:, :]), inc)
                        t_pgt = T(te.n)
                        if tt == 15:
                            pt_scat[0] = T(te.n)
                        sc.wait("v", t_pgt)
                        sc.wait("v", pg_used[gb])
                        sc.op("v", lambda e, gb=gb, npc=npc: e.tensor_copy(
                            Pg_t[:, gb, :, :], ps_g2[:, 0:npc, :]))
                        sc.op("v", lambda e: e.drain(), ve)
                        g2_free[0] = V(ve.n)
                        pg_ready[tt] = V(ve.n)
                    if tt >= 1:
                        ts2 = tt - 1
                        gb = ts2 % 2
                        sc.wait("t", pg_ready[ts2])
                        sc.wait("t", yg_done)
                        npc2 = CAP // 128
                        for dh in range(2):
                            bk = dh
                            sc.wait("t", bank_free[bk])
                            for pc in range(npc2):
                                inc = te if pc == npc2 - 1 else None
                                sc.op("t", lambda e, pc=pc, dh=dh, bk=bk, gb=gb:
                                      e.matmul(
                                          ps_b[:, bk, :], Pg_t[:, gb, pc, :],
                                          y_g[:, pc, dh * 512:(dh + 1) * 512],
                                          start=(pc == 0), stop=(pc == npc2 - 1)), inc)
                            t_sc = T(te.n)
                            if dh == 1:
                                pg_used[gb] = T(te.n)
                            sc.wait("v", t_sc)
                            sc.op("v", lambda e, ts2=ts2, dh=dh, bk=bk: e.tensor_tensor(
                                ffn_acc[:, ts2, dh * 512:(dh + 1) * 512],
                                ffn_acc[:, ts2, dh * 512:(dh + 1) * 512],
                                ps_b[:, bk, :], op=ALU.add), ve)
                            bank_free[bk] = V(ve.n)
            ffn_done = V(ve.n)

            rin_dma = {}
            rin_used = {}
            outst = {}
            for tt in range(16):
                par = tt % 2
                sc.wait("y", rin_used.get(tt - 1))
                sc.op("y", lambda e, tt=tt: e.dma_start(
                    out=r_in[:, 0, :], in_=out[tt * 128:(tt + 1) * 128, :]),
                    c_ri[0], 16)
                rin_dma[tt] = (c_ri[0], c_ri[0].n)
                sc.wait("v", ffn_done)
                sc.wait("v", rin_dma[tt])
                sc.wait("v", bo_done)
                sc.wait("v", outst.get(tt - 2))
                sc.op("v", lambda e, tt=tt, par=par: e.tensor_tensor(
                    out_sb[:, par, :], r_in[:, 0, :], ffn_acc[:, tt, :],
                    op=ALU.add))
                sc.op("v", lambda e: e.drain())
                sc.op("v", lambda e, par=par: e.tensor_tensor(
                    out_sb[:, par, :], out_sb[:, par, :], bo_bc[:, :],
                    op=ALU.add), ve)
                rin_used[tt] = V(ve.n)
                v_o = V(ve.n)
                sc.wait("y", v_o)
                sc.op("y", lambda e, tt=tt, par=par: e.dma_start(
                    out=out[tt * 128:(tt + 1) * 128, :], in_=out_sb[:, par, :]),
                    c_os[par], 16)
                outst[tt] = (c_os[par], c_os[par].n)

            sc.wait("y", (c_os[0], c_os[0].n))
            sc.wait("y", (c_os[1], c_os[1].n))

            with nc.Block() as blk:
                @blk.tensor
                def _(e):
                    sc.replay("t", e)

                @blk.vector
                def _(e):
                    sc.replay("v", e)

                @blk.scalar
                def _(e):
                    sc.replay("s", e)

                @blk.gpsimd
                def _(e):
                    sc.replay("g", e)

                @blk.sync
                def _(e):
                    sc.replay("y", e)

    return nc


def kernel(L0_attn_x, L0_attn_w_q, L0_attn_b_q, L0_attn_w_k, L0_attn_b_k,
           L0_attn_w_v, L0_attn_b_v, L0_attn_w_out, L0_attn_b_out,
           L0_ffn_router_w, L0_ffn_router_b, L0_ffn_w1, L0_ffn_b1,
           L0_ffn_w2, L0_ffn_b2):
    nc = build_program()

    x = np.ascontiguousarray(np.asarray(L0_attn_x, dtype=np.float32))
    wqT = np.ascontiguousarray(np.asarray(L0_attn_w_q, np.float32).T)
    wkT = np.ascontiguousarray(np.asarray(L0_attn_w_k, np.float32).T)
    wvT = np.ascontiguousarray(np.asarray(L0_attn_w_v, np.float32).T)
    woT = np.ascontiguousarray(np.asarray(L0_attn_w_out, np.float32).T)
    bq = np.ascontiguousarray(np.asarray(L0_attn_b_q, np.float32).reshape(8, 128).T)
    bk = np.ascontiguousarray(np.asarray(L0_attn_b_k, np.float32).reshape(8, 128).T)
    bv = np.ascontiguousarray(np.asarray(L0_attn_b_v, np.float32).reshape(8, 128).T)
    bo = np.ascontiguousarray(np.asarray(L0_attn_b_out, np.float32).reshape(8, 128).T)
    rwh = np.ascontiguousarray(np.asarray(L0_ffn_router_w, np.float32))
    rbh = np.ascontiguousarray(np.asarray(L0_ffn_router_b, np.float32).reshape(1, E))
    w1h = np.ascontiguousarray(np.asarray(L0_ffn_w1, np.float32))
    b1h = np.ascontiguousarray(
        np.asarray(L0_ffn_b1, np.float32).reshape(E, FF // 128, 128).transpose(2, 0, 1))
    w2h = np.ascontiguousarray(np.asarray(L0_ffn_w2, np.float32))
    b2h = np.ascontiguousarray(
        np.asarray(L0_ffn_b2, np.float32).reshape(E, DC, 128).transpose(2, 0, 1))
    triu = np.triu(np.ones((128, 128), np.float32))
    ones2d_h = np.ones((128, 128), np.float32)
    iota_h = np.arange(HSLOT, dtype=np.float32).reshape(1, HSLOT)
    ecap_h = (np.arange(8, dtype=np.float32) * float(CAP)).reshape(1, 8)
    ident = np.eye(128, dtype=np.float32)
    ones = np.ones((1, 128), dtype=np.float32)

    in_maps = []
    for c in range(8):
        b, qq = c // 4, c % 4
        in_maps.append({
            "xb": np.ascontiguousarray(x[b]),
            "xq": np.ascontiguousarray(x[b, qq * TQ:(qq + 1) * TQ]),
            "wqT": wqT, "wkT": wkT, "wvT": wvT, "woT": woT,
            "bq_in": bq, "bk_in": bk, "bv_in": bv, "bo_in": bo,
            "rw": rwh, "rb_in": rbh,
            "w1": w1h, "b1_in": b1h, "w2": w2h, "b2_in": b2h,
            "triu_in": triu, "ones2d_in": ones2d_h, "iota_in": iota_h,
            "ecap_in": ecap_h,
            "ident_in": ident, "ones_in": ones,
        })

    res = run_bass_kernel_spmd(nc, in_maps, core_ids=list(range(8)))
    full = np.empty((2, S, D), dtype=np.float32)
    for c in range(8):
        b, qq = c // 4, c % 4
        full[b, qq * TQ:(qq + 1) * TQ] = res.results[c]["out"]
    return full


# version 20
# speedup vs baseline: 1.1840x; 1.0088x over previous
import contextlib
import numpy as np
import concourse.bass as bass
from concourse.bass_core_rust import IndirectOffsetOnAxis
import concourse.mybir as mybir
from concourse.bass_utils import run_bass_kernel_spmd

F32 = mybir.dt.float32
BF16 = mybir.dt.bfloat16
I32 = mybir.dt.int32
AF = mybir.ActivationFunctionType
ALU = mybir.AluOpType
AX = mybir.AxisListType

D = 1024
DC = 8
S = 8192
TQ = 2048
KB = 512
NBLK = S // KB
NPRO = TQ // KB
NQT = TQ // 128
E = 8
FF = 8192
TH = 1024
SCALE = 0.03125
BIGPOS = 3.0e38
CAP = 384
HSLOT = 4 * CAP

ENGS = ("t", "v", "s", "g", "y")


class Ctr:
    def __init__(self, stack, nc, name):
        self.sem = stack.enter_context(nc.semaphore(name))
        self.n = 0


class Sched:
    def __init__(self):
        self.q = {k: [] for k in ENGS}
        self.lastw = {k: {} for k in ENGS}

    def op(self, eng, fn, inc=None, k=1):
        if inc is not None:
            inc.n += k
        self.q[eng].append((fn, inc, k))

    def wait(self, eng, ev):
        if ev is None:
            return
        ctr, v = ev
        if v <= 0 or self.lastw[eng].get(id(ctr), -1) >= v:
            return
        self.lastw[eng][id(ctr)] = v
        self.q[eng].append((None, ctr, v))

    def replay(self, eng, e):
        for fn, aux, k in self.q[eng]:
            if fn is None:
                e.wait_ge(aux.sem, k)
            else:
                inst = fn(e)
                if aux is not None:
                    inst.then_inc(aux.sem, k)


def build_program():
    nc = bass.Bass("TRN2", target_bir_lowering=False)

    xb = nc.dram_tensor("xb", [S, D], F32, kind="ExternalInput").ap()
    xq = nc.dram_tensor("xq", [TQ, D], F32, kind="ExternalInput").ap()
    wqT = nc.dram_tensor("wqT", [D, D], F32, kind="ExternalInput").ap()
    wkT = nc.dram_tensor("wkT", [D, D], F32, kind="ExternalInput").ap()
    wvT = nc.dram_tensor("wvT", [D, D], F32, kind="ExternalInput").ap()
    woT = nc.dram_tensor("woT", [D, D], F32, kind="ExternalInput").ap()
    bq_in = nc.dram_tensor("bq_in", [128, 8], F32, kind="ExternalInput").ap()
    bk_in = nc.dram_tensor("bk_in", [128, 8], F32, kind="ExternalInput").ap()
    bv_in = nc.dram_tensor("bv_in", [128, 8], F32, kind="ExternalInput").ap()
    bo_in = nc.dram_tensor("bo_in", [128, 8], F32, kind="ExternalInput").ap()
    rw = nc.dram_tensor("rw", [D, E], F32, kind="ExternalInput").ap()
    rb_in = nc.dram_tensor("rb_in", [1, E], F32, kind="ExternalInput").ap()
    w1 = nc.dram_tensor("w1", [E, D, FF], F32, kind="ExternalInput").ap()
    b1_in = nc.dram_tensor("b1_in", [128, E, FF // 128], F32, kind="ExternalInput").ap()
    w2 = nc.dram_tensor("w2", [E, FF // 2, D], F32, kind="ExternalInput").ap()
    b2_in = nc.dram_tensor("b2_in", [128, E, DC], F32, kind="ExternalInput").ap()
    triu_in = nc.dram_tensor("triu_in", [128, 128], F32, kind="ExternalInput").ap()
    ones2d_in = nc.dram_tensor("ones2d_in", [128, 128], F32, kind="ExternalInput").ap()
    tokid_in = nc.dram_tensor("tokid_in", [128, 16], F32, kind="ExternalInput").ap()
    ecap_in = nc.dram_tensor("ecap_in", [1, E], F32, kind="ExternalInput").ap()
    ident_in = nc.dram_tensor("ident_in", [128, 128], F32, kind="ExternalInput").ap()
    ones_in = nc.dram_tensor("ones_in", [1, 128], F32, kind="ExternalInput").ap()
    out = nc.dram_tensor("out", [TQ, D], F32, kind="ExternalOutput").ap()
    xn_scr = nc.dram_tensor("xn_scr", [TQ, D], BF16, kind="Internal").ap()
    ffn_scr = nc.dram_tensor("ffn_scr", [TQ + 1, D], BF16, kind="Internal").ap()
    inv_scr = nc.dram_tensor("inv_scr", [3200, 1], F32, kind="Internal").ap()

    with contextlib.ExitStack() as pstack:
        def psb(name, shape, dtype=F32):
            return pstack.enter_context(nc.sbuf_tensor(name, shape, dtype))

        ident = psb("ident", [128, 128], BF16)
        ones = psb("ones", [1, 128], BF16)
        eps_col = psb("eps_col", [128, 1], F32)
        bo2_col = psb("bo2_col", [128, 8], F32)

        te = Ctr(pstack, nc, "te")
        ve = Ctr(pstack, nc, "ve")
        se = Ctr(pstack, nc, "se")
        gc = Ctr(pstack, nc, "gc")
        _ctr_n = [0]

        def new_ctr(tag):
            _ctr_n[0] += 1
            return Ctr(pstack, nc, f"c{_ctr_n[0]}_{tag}")

        def T(v):
            return (te, v)

        def V(v):
            return (ve, v)

        def SE(v):
            return (se, v)

        def G(v):
            return (gd, v)

        def Y(v):
            return (sy, v)

        with contextlib.ExitStack() as astack:
            def asb(name, shape, dtype=F32):
                return astack.enter_context(nc.sbuf_tensor(name, shape, dtype))

            wq_wo = asb("wq_wo", [128, DC, D], BF16)
            wk_sb = asb("wk_sb", [128, DC, D], BF16)
            wv_sb = asb("wv_sb", [128, DC, D], BF16)
            bq_col = asb("bq_col", [128, 8], F32)
            bk_col = asb("bk_col", [128, 8], F32)
            bv_col = asb("bv_col", [128, 8], BF16)
            bo_col = asb("bo_col", [128, 8], F32)
            x_blk = asb("x_blk", [128, 4, D], BF16)
            xn_sb = asb("xn_sb", [128, 2, D], BF16)
            xnT = asb("xnT", [128, DC, KB], BF16)
            KT = asb("KT", [128, 8, KB], BF16)
            V_sb = asb("V_sb", [128, 4, D], BF16)
            QT = asb("QT", [128, 8, TQ], BF16)
            OACC = asb("OACC", [128, NQT, D], F32)
            P_sb = asb("P_sb", [128, 2, KB], BF16)
            PT_sb = asb("PT_sb", [128, 2, 4, 128], BF16)
            st12 = asb("st12", [128, 4, 12], F32)
            mv_sb = asb("mv_sb", [128, 4, 2], F32)
            lnv_sb = asb("lnv_sb", [128, 4], F32)
            rstd_sb = asb("rstd_sb", [128, 4], F32)
            nm2 = asb("nm2", [128, 2, NQT], F32)
            l2 = asb("l2", [128, 2, NQT], F32)
            nmblk = asb("nmblk", [128, NQT], F32)
            d_sb = asb("d_sb", [128, NQT], F32)
            alpha = asb("alpha", [128, NQT], F32)
            bsum = asb("bsum", [128, NQT], F32)
            linv = asb("linv", [128, NQT], F32)
            nms = asb("nms", [128, NQT], F32)
            O_bf = asb("O_bf", [128, 2, D], BF16)
            OT_sb = asb("OT_sb", [128, 2, DC, 128], BF16)
            xq_f32 = asb("xq_f32", [128, 1, D], F32)
            r_out = asb("r_out", [128, 2, D], F32)

            ps_s = astack.enter_context(nc.psum_tensor("ps_s", [128, 2, 512], F32))
            ps_o = astack.enter_context(nc.psum_tensor("ps_o", [128, 2, 2, 512], F32))
            ps_tr = astack.enter_context(nc.psum_tensor("ps_tr", [128, 2, 8, 128], BF16))

            sc = Sched()

            ps_s_free = {0: None, 1: None}
            ps_o_free = {0: None, 1: None}
            trcopy = {0: None, 1: None}
            xn_used = {0: None, 1: None}
            pt_used = {0: None, 1: None}
            p_trans = {0: None, 1: None}
            x_used = [None]
            blk_i = [0]

            sc.op("g", lambda e: e.memset(eps_col[:, :], 1e-5), gc)
            sc.op("g", lambda e: e.memset(nm2[:, 0, :], BIGPOS), gc)
            sc.op("g", lambda e: e.memset(l2[:, 0, :], 0.0), gc)
            sc.op("g", lambda e: e.memset(OACC[:, :, :], 0.0), gc)
            setup_evs = []

            def setup_dma(fn, tag):
                c = new_ctr(tag)
                sc.op("g", fn, c, 16)
                setup_evs.append((c, c.n))

            setup_dma(lambda e: e.dma_start(out=ident[:, :], in_=ident_in), "id")
            setup_dma(lambda e: e.dma_start(out=ones[:, :], in_=ones_in), "on")
            setup_dma(lambda e: e.dma_start(
                out=wq_wo[:, :, :], in_=wqT.rearrange("(dc p) e -> p dc e", p=128)), "wq")
            setup_dma(lambda e: e.dma_start(
                out=wk_sb[:, :, :], in_=wkT.rearrange("(dc p) e -> p dc e", p=128)), "wk")
            setup_dma(lambda e: e.dma_start(
                out=wv_sb[:, :, :], in_=wvT.rearrange("(dc p) e -> p dc e", p=128)), "wv")
            setup_dma(lambda e: e.dma_start(out=bq_col[:, :], in_=bq_in), "bq")
            setup_dma(lambda e: e.dma_start(out=bk_col[:, :], in_=bk_in), "bk")
            setup_dma(lambda e: e.dma_start(out=bv_col[:, :], in_=bv_in), "bv")
            setup_dma(lambda e: e.dma_start(out=bo_col[:, :], in_=bo_in), "bo")
            setup_c = (gc, gc.n)

            def wait_setup(eng):
                for ev in setup_evs:
                    sc.wait(eng, ev)
            c_x = new_ctr("x")
            c_wo = new_ctr("wo")
            c_xq = [new_ctr("xq0"), new_ctr("xq1")]
            c_ro = [new_ctr("ro0"), new_ctr("ro1")]

            def ln_block(src_ap, tok0):
                sc.wait("g", x_used[0])
                sc.op("g", lambda e, a=src_ap, t=tok0: e.dma_start(
                    out=x_blk[:, :, :],
                    in_=a[t:t + KB, :].rearrange("(tt p) d -> p tt d", p=128)), c_x, 16)
                x_dma = (c_x, c_x.n)
                sc.wait("v", x_dma)
                last = None
                s_rstd = {}
                for tt in range(4):
                    sc.op("v", lambda e, tt=tt: e.bn_stats(
                        st12[:, tt, 0:6], x_blk[:, tt, 0:512]))
                    sc.op("v", lambda e, tt=tt: e.bn_stats(
                        st12[:, tt, 6:12], x_blk[:, tt, 512:1024]))
                    sc.op("v", lambda e: e.drain())
                    sc.op("v", lambda e, tt=tt: e.bn_aggr(
                        mv_sb[:, tt, :], st12[:, tt, :]))
                    sc.op("v", lambda e: e.drain(), ve)
                    v_st = V(ve.n)
                    sc.wait("s", v_st)
                    wait_setup("s")
                    sc.wait("s", setup_c)
                    sc.op("s", lambda e, tt=tt: e.activation(
                        lnv_sb[:, tt:tt + 1], mv_sb[:, tt, 1:2], AF.Ln,
                        bias=eps_col[:, :], scale=1.0))
                    sc.op("s", lambda e: e.drain())
                    sc.op("s", lambda e, tt=tt: e.activation(
                        rstd_sb[:, tt:tt + 1], lnv_sb[:, tt:tt + 1], AF.Exp,
                        bias=0.0, scale=-0.5))
                    sc.op("s", lambda e: e.drain(), se)
                    s_rstd[tt] = SE(se.n)
                for tt in range(4):
                    xp = tt % 2
                    sc.wait("v", s_rstd[tt])
                    sc.wait("v", xn_used[xp])
                    sc.op("v", lambda e, tt=tt, xp=xp: e.tensor_scalar(
                        xn_sb[:, xp, :], x_blk[:, tt, :],
                        mv_sb[:, tt, 0:1], rstd_sb[:, tt:tt + 1],
                        op0=ALU.subtract, op1=ALU.mult), ve)
                    v_n = V(ve.n)
                    if tt == 3:
                        x_used[0] = v_n
                    par = tt % 2
                    sc.wait("t", v_n)
                    sc.wait("t", trcopy[par])
                    for dc in range(DC):
                        inc = te if dc == DC - 1 else None
                        sc.op("t", lambda e, xp=xp, dc=dc, par=par: e.transpose(
                            ps_tr[:, par, dc, :], xn_sb[:, xp, dc * 128:(dc + 1) * 128],
                            ident[:, :]), inc)
                    xn_used[xp] = T(te.n)
                    sc.wait("v", xn_used[xp])
                    sc.op("v", lambda e, tt=tt, par=par: e.tensor_copy(
                        xnT[:, :, tt * 128:(tt + 1) * 128], ps_tr[:, par, :, :]), ve)
                    trcopy[par] = V(ve.n)
                    last = V(ve.n)
                blk_i[0] += 1
                return last

            wait_setup("t")
            sc.wait("v", setup_c)

            for ob in range(NPRO):
                v_xnT = ln_block(xq, ob * KB)
                sc.wait("t", v_xnT)
                for ec in range(8):
                    par = ec % 2
                    sc.wait("t", ps_s_free[par])
                    for dc in range(DC):
                        inc = te if dc == DC - 1 else None
                        sc.op("t", lambda e, ec=ec, dc=dc, par=par: e.matmul(
                            ps_s[:, par, :], wq_wo[:, dc, ec * 128:(ec + 1) * 128],
                            xnT[:, dc, :], start=(dc == 0), stop=(dc == DC - 1)), inc)
                    t_q = T(te.n)
                    sc.wait("v", t_q)
                    sc.op("v", lambda e, ec=ec, ob=ob, par=par: e.tensor_scalar(
                        QT[:, ec, ob * KB:(ob + 1) * KB], ps_s[:, par, :],
                        bq_col[:, ec:ec + 1], None, op0=ALU.add), ve)
                    ps_s_free[par] = V(ve.n)
            qt_done_te = T(te.n)
            qt_done_ve = V(ve.n)

            sc.wait("g", qt_done_te)
            sc.op("g", lambda e: e.dma_start(
                out=wq_wo[:, :, :], in_=woT.rearrange("(dc p) e -> p dc e", p=128)), c_wo, 16)
            wo_loaded = (c_wo, c_wo.n)

            sc.wait("t", wo_loaded)
            for ec in range(8):
                sc.wait("t", ps_s_free[0])
                for dc in range(DC):
                    inc = te if dc == DC - 1 else None
                    sc.op("t", lambda e, ec=ec, dc=dc: e.matmul(
                        ps_s[:, 0, 0:1], wq_wo[:, dc, ec * 128:(ec + 1) * 128],
                        bv_col[:, dc:dc + 1], start=(dc == 0), stop=(dc == DC - 1)), inc)
                t_b = T(te.n)
                sc.wait("v", t_b)
                sc.op("v", lambda e, ec=ec: e.tensor_scalar(
                    bo2_col[:, ec:ec + 1], ps_s[:, 0, 0:1],
                    bo_col[:, ec:ec + 1], None, op0=ALU.add), ve)
                ps_s_free[0] = V(ve.n)

            v_xnT_cur = ln_block(xb, 0)
            for kb in range(NBLK):
                rp, wp = kb % 2, (kb + 1) % 2
                sc.wait("t", v_xnT_cur)
                sc.wait("t", qt_done_ve)
                kt_ready = None
                v_ready = None

                def emit_kt(ec):
                    nonlocal kt_ready
                    par = ec % 2
                    sc.wait("t", ps_s_free[par])
                    for dc in range(DC):
                        inc = te if dc == DC - 1 else None
                        sc.op("t", lambda e, ec=ec, dc=dc, par=par: e.matmul(
                            ps_s[:, par, :], wk_sb[:, dc, ec * 128:(ec + 1) * 128],
                            xnT[:, dc, :], start=(dc == 0), stop=(dc == DC - 1)), inc)
                    t_k = T(te.n)
                    sc.wait("v", t_k)
                    sc.op("v", lambda e, ec=ec, par=par: e.tensor_scalar(
                        KT[:, ec, :], ps_s[:, par, :],
                        bk_col[:, ec:ec + 1], None, op0=ALU.add), ve)
                    ps_s_free[par] = V(ve.n)
                    kt_ready = V(ve.n)

                def emit_v(tt):
                    nonlocal v_ready
                    par = tt % 2
                    sc.wait("t", ps_o_free[par])
                    for eh in range(2):
                        for dc in range(DC):
                            inc = te if (eh == 1 and dc == DC - 1) else None
                            sc.op("t", lambda e, tt=tt, eh=eh, dc=dc, par=par: e.matmul(
                                ps_o[:, par, eh, :],
                                xnT[:, dc, tt * 128:(tt + 1) * 128],
                                wv_sb[:, dc, eh * 512:(eh + 1) * 512],
                                start=(dc == 0), stop=(dc == DC - 1)), inc)
                    t_v = T(te.n)
                    sc.wait("v", t_v)
                    sc.op("v", lambda e, tt=tt, par=par: e.tensor_copy(
                        V_sb[:, tt, :],
                        ps_o[:, par, :, :].rearrange("p a b -> p (a b)")), ve)
                    ps_o_free[par] = V(ve.n)
                    v_ready = V(ve.n)

                for i4 in range(4):
                    emit_kt(2 * i4)
                    emit_kt(2 * i4 + 1)
                    emit_v(i4)

                v_xnT_cur = (ln_block(xb, (kb + 1) * KB)
                             if kb + 1 < NBLK else None)

                s_p_ev = {}
                v_pt_ev = {}
                for qi in range(NQT + 2):
                    if qi < NQT:
                        par = qi % 2
                        sc.wait("t", kt_ready)
                        sc.wait("t", ps_s_free[par])
                        for ec in range(8):
                            inc = te if ec == 7 else None
                            sc.op("t", lambda e, ec=ec, qi=qi, par=par: e.matmul(
                                ps_s[:, par, :], QT[:, ec, qi * 128:(qi + 1) * 128],
                                KT[:, ec, :], start=(ec == 0), stop=(ec == 7)), inc)
                        t_s = T(te.n)
                        sc.wait("v", t_s)
                        sc.op("v", lambda e, qi=qi, par=par: e.reduce_max(
                            nmblk[:, qi:qi + 1], ps_s[:, par, :], axis=AX.X,
                            negate=True))
                        sc.op("v", lambda e: e.drain())
                        sc.op("v", lambda e, qi=qi, rp=rp, wp=wp: e.tensor_tensor(
                            nm2[:, wp, qi:qi + 1], nm2[:, rp, qi:qi + 1],
                            nmblk[:, qi:qi + 1], op=ALU.min))
                        sc.op("v", lambda e: e.drain())
                        sc.op("v", lambda e, qi=qi, rp=rp, wp=wp: e.tensor_tensor(
                            d_sb[:, qi:qi + 1], nm2[:, wp, qi:qi + 1],
                            nm2[:, rp, qi:qi + 1], op=ALU.subtract))
                        sc.op("v", lambda e: e.drain(), ve)
                        v_stat = V(ve.n)
                        pp = qi % 2
                        sc.wait("s", v_stat)
                        sc.wait("s", p_trans[pp])
                        sc.op("s", lambda e, qi=qi: e.activation(
                            alpha[:, qi:qi + 1], d_sb[:, qi:qi + 1], AF.Exp,
                            bias=0.0, scale=SCALE))
                        sc.op("s", lambda e, qi=qi, wp=wp: e.mul(
                            nms[:, qi:qi + 1], nm2[:, wp, qi:qi + 1], SCALE))
                        sc.op("s", lambda e: e.drain())
                        sc.op("s", lambda e, qi=qi, par=par, pp=pp: e.activation(
                            P_sb[:, pp, :], ps_s[:, par, :], AF.Exp,
                            bias=nms[:, qi:qi + 1], scale=SCALE,
                            accum_out=bsum[:, qi:qi + 1]))
                        sc.op("s", lambda e: e.drain(), se)
                        s_p_ev[qi] = SE(se.n)
                        ps_s_free[par] = SE(se.n)
                        sc.wait("v", s_p_ev[qi])
                        sc.op("v", lambda e, qi=qi, rp=rp, wp=wp: e.scalar_tensor_tensor(
                            l2[:, wp, qi:qi + 1], l2[:, rp, qi:qi + 1],
                            alpha[:, qi:qi + 1], bsum[:, qi:qi + 1],
                            op0=ALU.mult, op1=ALU.add), ve)
                    if 1 <= qi <= NQT:
                        qj = qi - 1
                        par = qj % 2
                        pp = qj % 2
                        sc.wait("t", s_p_ev[qj])
                        sc.wait("t", trcopy[par])
                        for kc in range(4):
                            inc = te if kc == 3 else None
                            sc.op("t", lambda e, kc=kc, par=par, pp=pp: e.transpose(
                                ps_tr[:, par, kc, :],
                                P_sb[:, pp, kc * 128:(kc + 1) * 128],
                                ident[:, :]), inc)
                        p_trans[pp] = T(te.n)
                        t_ptr = T(te.n)
                        sc.wait("v", t_ptr)
                        sc.wait("v", pt_used[par])
                        sc.op("v", lambda e, par=par: e.tensor_copy(
                            PT_sb[:, par, :, :], ps_tr[:, par, 0:4, :]), ve)
                        trcopy[par] = V(ve.n)
                        v_pt_ev[qj] = V(ve.n)
                    if qi >= 2:
                        qj = qi - 2
                        par = qj % 2
                        sc.wait("t", v_pt_ev[qj])
                        sc.wait("t", v_ready)
                        sc.wait("t", ps_o_free[par])
                        for kc in range(4):
                            for h in range(2):
                                inc = te if (kc == 3 and h == 1) else None
                                sc.op("t", lambda e, kc=kc, h=h, par=par: e.matmul(
                                    ps_o[:, par, h, :], PT_sb[:, par, kc, :],
                                    V_sb[:, kc, h * 512:(h + 1) * 512],
                                    start=(kc == 0), stop=(kc == 3)), inc)
                        t_av = T(te.n)
                        pt_used[par] = T(te.n)
                        sc.wait("v", t_av)
                        for h in range(2):
                            sc.op("v",
                                  lambda e, qj=qj, h=h, par=par: e.scalar_tensor_tensor(
                                      OACC[:, qj, h * 512:(h + 1) * 512],
                                      OACC[:, qj, h * 512:(h + 1) * 512],
                                      alpha[:, qj:qj + 1], ps_o[:, par, h, :],
                                      op0=ALU.mult, op1=ALU.add), ve)
                        ps_o_free[par] = V(ve.n)

            final_p = NBLK % 2
            xq_dma = {}
            xq_used = {}
            rstored = {}
            ot_used = {}
            obf_used = {}
            for qi in range(NQT):
                par = qi % 2
                sc.wait("y", xq_used.get(qi - 1))
                sc.op("y", lambda e, qi=qi: e.dma_start(
                    out=xq_f32[:, 0, :], in_=xq[qi * 128:(qi + 1) * 128, :]),
                    c_xq[0], 16)
                xq_dma[qi] = (c_xq[0], c_xq[0].n)
                sc.op("v", lambda e, qi=qi, fp=final_p: e.reciprocal(
                    linv[:, qi:qi + 1], l2[:, fp, qi:qi + 1]))
                sc.op("v", lambda e: e.drain(), ve)
                v_linv = V(ve.n)
                sc.wait("s", v_linv)
                sc.wait("s", obf_used.get(par))
                sc.op("s", lambda e, qi=qi, par=par: e.mul(
                    O_bf[:, par, :], OACC[:, qi, :], linv[:, qi:qi + 1]), se)
                s_obf = SE(se.n)
                sc.wait("t", s_obf)
                sc.wait("t", trcopy[par])
                for dc in range(DC):
                    inc = te if dc == DC - 1 else None
                    sc.op("t", lambda e, dc=dc, par=par: e.transpose(
                        ps_tr[:, par, dc, :], O_bf[:, par, dc * 128:(dc + 1) * 128],
                        ident[:, :]), inc)
                obf_used[par] = T(te.n)
                t_otr = T(te.n)
                sc.wait("v", t_otr)
                sc.wait("v", ot_used.get(par))
                sc.op("v", lambda e, par=par: e.tensor_copy(
                    OT_sb[:, par, :, :], ps_tr[:, par, :, :]), ve)
                trcopy[par] = V(ve.n)
                v_ot = V(ve.n)
                sc.wait("t", v_ot)
                sc.wait("t", wo_loaded)
                sc.wait("t", ps_o_free[par])
                for h in range(2):
                    for dc in range(DC):
                        inc = te if (h == 1 and dc == DC - 1) else None
                        sc.op("t", lambda e, h=h, dc=dc, par=par: e.matmul(
                            ps_o[:, par, h, :], OT_sb[:, par, dc, :],
                            wq_wo[:, dc, h * 512:(h + 1) * 512],
                            start=(dc == 0), stop=(dc == DC - 1)), inc)
                t_proj = T(te.n)
                ot_used[par] = T(te.n)
                sc.wait("v", t_proj)
                sc.wait("v", xq_dma[qi])
                sc.wait("v", rstored.get(qi - 2))
                for h in range(2):
                    inc = ve if h == 1 else None
                    sc.op("v", lambda e, h=h, par=par: e.tensor_tensor(
                        r_out[:, par, h * 512:(h + 1) * 512],
                        xq_f32[:, 0, h * 512:(h + 1) * 512],
                        ps_o[:, par, h, :], op=ALU.add), inc)
                ps_o_free[par] = V(ve.n)
                xq_used[qi] = V(ve.n)
                v_r = V(ve.n)
                sc.wait("y", v_r)
                sc.op("y", lambda e, qi=qi, par=par: e.dma_start(
                    out=out[qi * 128:(qi + 1) * 128, :], in_=r_out[:, par, :]),
                    c_ro[par], 16)
                rstored[qi] = (c_ro[par], c_ro[par].n)

            sc.wait("y", (c_ro[0], c_ro[0].n))
            sc.wait("y", (c_ro[1], c_ro[1].n))

            with nc.Block() as blk:
                @blk.tensor
                def _(e):
                    sc.replay("t", e)

                @blk.vector
                def _(e):
                    sc.replay("v", e)

                @blk.scalar
                def _(e):
                    sc.replay("s", e)

                @blk.gpsimd
                def _(e):
                    sc.replay("g", e)

                @blk.sync
                def _(e):
                    sc.replay("y", e)

        with contextlib.ExitStack() as fstack:
            def fsb(name, shape, dtype=F32):
                return fstack.enter_context(nc.sbuf_tensor(name, shape, dtype))

            pos_col = fsb("pos_col", [128, 16], F32)
            b2_sb = fsb("b2_sb", [128, E, DC], F32)
            ones_f32 = fsb("ones_f32", [1, 128], F32)
            b1_sb = fsb("b1_sb", [128, E, 64], F32)
            bo_bc = fsb("bo_bc", [128, D], BF16)

            b1stack = contextlib.ExitStack()

            def fsb1(name, shape, dtype=F32):
                return b1stack.enter_context(nc.sbuf_tensor(name, shape, dtype))

            xf_bf = fsb1("xf_bf", [128, 4, D], BF16)
            xn_tok = fsb1("xn_tok", [128, 16, D], BF16)
            tokid_sb = fsb1("tokid_sb", [128, 16], F32)
            zfb = fsb1("zfb", [128, D], BF16)
            zf32 = fsb1("zf32", [128, 32], F32)
            pos_i32 = fsb1("pos_i32", [128, 16], I32)
            pos_t1 = fsb1("pos_t1", [128, 16], F32)
            xnTb = fsb1("xnTb", [128, DC, 512], BF16)
            st12f = fsb1("st12f", [128, 4, 12], F32)
            mvf = fsb1("mvf", [128, 4, 2], F32)
            lnvf = fsb1("lnvf", [128, 4], F32)
            rstdf = fsb1("rstdf", [128, 4], F32)
            rw_sb = fsb1("rw_sb", [128, DC, E], BF16)
            rb_row = fsb1("rb_row", [1, E], BF16)
            rb_bc = fsb1("rb_bc", [128, E], BF16)
            logits = fsb1("logits", [128, 16, E], F32)
            lgmax = fsb1("lgmax", [128, 16], F32)
            mask_bf = fsb1("mask_bf", [128, 16, E], BF16)
            triu_sb = fsb1("triu_sb", [128, 128], BF16)
            ones2d = fsb1("ones2d", [128, 128], BF16)
            ecap_row = fsb1("ecap_row", [1, E], F32)
            ecap_bc = fsb1("ecap_bc", [128, E], F32)
            tmp8 = fsb1("tmp8", [128, E], F32)
            possum = fsb1("possum", [128, 1], F32)
            bo2_bf = fsb1("bo2_bf", [128, 8], BF16)
            borow = fsb1("borow", [1, D], BF16)
            ps_b = b1stack.enter_context(nc.psum_tensor("ps_b", [128, 7, 512], F32))
            ps_trf = b1stack.enter_context(nc.psum_tensor("ps_trf", [128, 8, 128], BF16))

            sc = Sched()

            bank_free = {k: None for k in range(8)}
            trf_free = [None]
            w1_used = {0: None, 1: None}
            w2_used = {0: None, 1: None}
            sig_ready = {}
            sig_used = {}
            xf_used = [None]

            c_fs = [new_ctr("fs") for _ in range(5)]
            c_ys = [new_ctr("ys") for _ in range(5)]
            c_xf = new_ctr("xf")
            c_w1 = [new_ctr("w1a"), new_ctr("w1b"), new_ctr("w1c")]
            c_w2 = [new_ctr("w2a"), new_ctr("w2b")]
            c_ri = [new_ctr("ri0"), new_ctr("ri1")]
            c_os = [new_ctr("os0"), new_ctr("os1")]
            c_xs = [new_ctr("xs0"), new_ctr("xs1")]
            c_fz = [new_ctr("fz0"), new_ctr("fz1")]
            c_is = [new_ctr("is0"), new_ctr("is1")]
            c_tk = new_ctr("tk")
            c_iz = new_ctr("iz")
            c_iv = [new_ctr("iv0"), new_ctr("iv1"), new_ctr("iv2")]
            c_gi = [new_ctr("gi0"), new_ctr("gi1"), new_ctr("gi2")]
            c_sc = [new_ctr("sc0"), new_ctr("sc1"), new_ctr("sc2")]
            c_ff = [new_ctr("ff0"), new_ctr("ff1")]

            sc.op("g", lambda e: e.dma_start(
                out=rw_sb[:, :, :], in_=rw.rearrange("(dc p) e -> p dc e", p=128)),
                c_fs[0], 16)
            sc.op("g", lambda e: e.dma_start(out=b1_sb[:, :, :], in_=b1_in), c_fs[1], 16)
            sc.op("g", lambda e: e.dma_start(out=rb_row[:, :], in_=rb_in), c_fs[2], 16)
            sc.op("g", lambda e: e.dma_start(out=triu_sb[:, :], in_=triu_in), c_fs[3], 16)
            sc.op("g", lambda e: e.dma_start(out=ones2d[:, :], in_=ones2d_in), c_fs[4], 16)
            sc.op("y", lambda e: e.dma_start(out=tokid_sb[:, :], in_=tokid_in), c_ys[0], 16)
            sc.op("y", lambda e: e.dma_start(out=ecap_row[:, :], in_=ecap_in), c_ys[1], 16)
            sc.op("g", lambda e: e.dma_start(out=b2_sb[:, :, :], in_=b2_in), c_ys[2], 16)
            sc.op("y", lambda e: e.dma_start(out=ones_f32[:, :], in_=ones_in), c_ys[3], 16)
            f_evs = [(c, 16) for c in c_fs] + [(c, 16) for c in c_ys[:4]]

            def wait_fsetup(eng):
                for ev in f_evs:
                    sc.wait(eng, ev)

            sc.op("v", lambda e: e.memset(zfb[:, :], 0.0))
            sc.op("v", lambda e: e.memset(zf32[:, :], 0.0))
            sc.op("v", lambda e: e.drain(), ve)
            z_done = V(ve.n)
            sc.wait("y", z_done)
            for zz in range(16):
                sc.wait("y", (c_fz[zz % 2], c_fz[zz % 2].n))
                sc.op("y", lambda e, zz=zz: e.dma_start(
                    out=ffn_scr[zz * 128:(zz + 1) * 128, :], in_=zfb[:, :]),
                    c_fz[zz % 2], 16)
            sc.op("y", lambda e: e.dma_start(
                out=ffn_scr[TQ:TQ + 1, :], in_=zfb[0:1, :]), c_fz[0], 16)
            sc.op("y", lambda e: e.dma_start(
                out=inv_scr[:, :].rearrange("(a b) c -> a (b c)", b=25),
                in_=zf32[:, 0:25]), c_iz, 16)

            wait_fsetup("t")
            sc.op("t", lambda e: e.matmul(
                ps_b[:, 0, 0:E], ones[:, :], rb_row[:, :], start=True, stop=True), te)
            t_rb = T(te.n)
            sc.wait("v", t_rb)
            sc.op("v", lambda e: e.tensor_copy(rb_bc[:, :], ps_b[:, 0, 0:E]), ve)
            rb_done = V(ve.n)
            bank_free[0] = rb_done

            sc.op("t", lambda e: e.matmul(
                ps_b[:, 1, 0:E], ones_f32[:, :], ecap_row[:, :],
                start=True, stop=True), te)
            t_ec = T(te.n)
            sc.wait("v", t_ec)
            sc.op("v", lambda e: e.tensor_copy(ecap_bc[:, :], ps_b[:, 1, 0:E]), ve)
            ec_done = V(ve.n)
            bank_free[1] = ec_done

            sc.op("v", lambda e: e.tensor_copy(bo2_bf[:, :], bo2_col[:, :]), ve)
            v_b2 = V(ve.n)
            sc.wait("t", v_b2)
            for dc in range(DC):
                inc = te if dc == DC - 1 else None
                sc.op("t", lambda e, dc=dc: e.transpose(
                    ps_trf[0:1, dc, :], bo2_bf[:, dc:dc + 1], ident[:, :]), inc)
            t_bt = T(te.n)
            sc.wait("v", t_bt)
            sc.op("v", lambda e: e.tensor_copy(
                borow[:, :], ps_trf[0:1, :, :].rearrange("p a b -> p (a b)")))
            sc.op("v", lambda e: e.drain(), ve)
            trf_free[0] = V(ve.n)
            v_brow = V(ve.n)
            sc.wait("t", v_brow)
            for h2 in range(2):
                sc.wait("t", bank_free[3])
                sc.op("t", lambda e, h2=h2: e.matmul(
                    ps_b[:, 3, :], ones[:, :], borow[:, h2 * 512:(h2 + 1) * 512],
                    start=True, stop=True), te)
                t_bo = T(te.n)
                sc.wait("v", t_bo)
                sc.op("v", lambda e, h2=h2: e.tensor_copy(
                    bo_bc[:, h2 * 512:(h2 + 1) * 512], ps_b[:, 3, :]), ve)
                bank_free[3] = V(ve.n)
            bo_done = V(ve.n)

            for b4 in range(4):
                sc.wait("g", xf_used[0])
                sc.op("g", lambda e, t0=b4 * 512: e.dma_start(
                    out=xf_bf[:, :, :],
                    in_=xq[t0:t0 + 512, :].rearrange("(tt p) d -> p tt d", p=128)),
                    c_xf, 16)
                g_x = (c_xf, c_xf.n)
                sc.wait("v", g_x)
                s_rstd = {}
                for tt in range(4):
                    sc.op("v", lambda e, tt=tt: e.bn_stats(
                        st12f[:, tt, 0:6], xf_bf[:, tt, 0:512]))
                    sc.op("v", lambda e, tt=tt: e.bn_stats(
                        st12f[:, tt, 6:12], xf_bf[:, tt, 512:1024]))
                    sc.op("v", lambda e: e.drain())
                    sc.op("v", lambda e, tt=tt: e.bn_aggr(
                        mvf[:, tt, :], st12f[:, tt, :]))
                    sc.op("v", lambda e: e.drain(), ve)
                    v_st = V(ve.n)
                    sc.wait("s", v_st)
                    sc.op("s", lambda e, tt=tt: e.activation(
                        lnvf[:, tt:tt + 1], mvf[:, tt, 1:2], AF.Ln,
                        bias=eps_col[:, :], scale=1.0))
                    sc.op("s", lambda e: e.drain())
                    sc.op("s", lambda e, tt=tt: e.activation(
                        rstdf[:, tt:tt + 1], lnvf[:, tt:tt + 1], AF.Exp,
                        bias=0.0, scale=-0.5))
                    sc.op("s", lambda e: e.drain(), se)
                    s_rstd[tt] = SE(se.n)
                for tt in range(4):
                    gt = b4 * 4 + tt
                    sc.wait("v", s_rstd[tt])
                    sc.op("v", lambda e, tt=tt, gt=gt: e.tensor_scalar(
                        xn_tok[:, gt, :], xf_bf[:, tt, :],
                        mvf[:, tt, 0:1], rstdf[:, tt:tt + 1],
                        op0=ALU.subtract, op1=ALU.mult), ve)
                    v_n = V(ve.n)
                    sc.wait("y", v_n)
                    sc.wait("y", (c_xs[gt % 2], c_xs[gt % 2].n))
                    sc.op("y", lambda e, gt=gt: e.dma_start(
                        out=xn_scr[gt * 128:(gt + 1) * 128, :],
                        in_=xn_tok[:, gt, :]), c_xs[gt % 2], 16)
                    sc.wait("t", v_n)
                    sc.wait("t", trf_free[0])
                    for dc in range(DC):
                        inc = te if dc == DC - 1 else None
                        sc.op("t", lambda e, gt=gt, dc=dc: e.transpose(
                            ps_trf[:, dc, :], xn_tok[:, gt, dc * 128:(dc + 1) * 128],
                            ident[:, :]), inc)
                    t_tr = T(te.n)
                    sc.wait("v", t_tr)
                    sc.op("v", lambda e, tt=tt: e.tensor_copy(
                        xnTb[:, :, tt * 128:(tt + 1) * 128], ps_trf[:, :, :]), ve)
                    trf_free[0] = V(ve.n)
                xnTb_done = V(ve.n)
                xf_used[0] = V(ve.n)
                sc.wait("t", xnTb_done)
                for tt in range(4):
                    gt = b4 * 4 + tt
                    bk = 4 + (tt % 2)
                    sc.wait("t", bank_free[bk])
                    for dc in range(DC):
                        inc = te if dc == DC - 1 else None
                        sc.op("t", lambda e, tt=tt, dc=dc, bk=bk: e.matmul(
                            ps_b[:, bk, 0:E], xnTb[:, dc, tt * 128:(tt + 1) * 128],
                            rw_sb[:, dc, :], start=(dc == 0), stop=(dc == DC - 1)), inc)
                    t_lg = T(te.n)
                    sc.wait("v", t_lg)
                    sc.wait("v", rb_done)
                    sc.op("v", lambda e, gt=gt, bk=bk: e.tensor_tensor(
                        logits[:, gt, :], ps_b[:, bk, 0:E], rb_bc[:, :], op=ALU.add), ve)
                    bank_free[bk] = V(ve.n)
                    sc.op("v", lambda e: e.drain())
                    sc.op("v", lambda e, gt=gt: e.reduce_max(
                        lgmax[:, gt:gt + 1], logits[:, gt, :], axis=AX.X))
                    sc.op("v", lambda e: e.drain())
                    sc.op("v", lambda e, gt=gt: e.tensor_scalar(
                        mask_bf[:, gt, :], logits[:, gt, :],
                        lgmax[:, gt:gt + 1], None, op0=ALU.is_ge))
                    sc.op("v", lambda e: e.drain(), ve)
            mask_done = V(ve.n)

            sc.wait("t", mask_done)
            for j in range(16):
                bk = 5 + (j % 2)
                sc.wait("t", bank_free[bk])
                for i in range(j + 1):
                    lhs = triu_sb if i == j else ones2d
                    inc = te if i == j else None
                    sc.op("t", lambda e, i=i, j=j, bk=bk, lhs=lhs: e.matmul(
                        ps_b[:, bk, 0:E], lhs[:, :], mask_bf[:, i, :],
                        start=(i == 0), stop=(i == j)), inc)
                t_cum = T(te.n)
                sc.wait("v", t_cum)
                sc.wait("v", ec_done)
                sc.op("v", lambda e, bk=bk: e.tensor_tensor(
                    tmp8[:, :], ps_b[:, bk, 0:E], ecap_bc[:, :], op=ALU.add), ve)
                bank_free[bk] = V(ve.n)
                sc.op("v", lambda e: e.drain())
                sc.op("v", lambda e, j=j: e.tensor_tensor(
                    tmp8[:, :], tmp8[:, :], mask_bf[:, j, :], op=ALU.mult))
                sc.op("v", lambda e: e.drain())
                sc.op("v", lambda e: e.reduce_sum(
                    possum[:, :], tmp8[:, :], axis=AX.X))
                sc.op("v", lambda e: e.drain())
                sc.op("v", lambda e, j=j: e.tensor_scalar_add(
                    pos_col[:, j:j + 1], possum[:, :], -1.0))
                sc.op("v", lambda e: e.drain(), ve)
            sc.op("v", lambda e: e.tensor_scalar_max(
                pos_t1[:, :], pos_col[:, :], 0.0))
            sc.op("v", lambda e: e.drain())
            sc.op("v", lambda e: e.tensor_scalar_min(
                pos_t1[:, :], pos_t1[:, :], 3100.0))
            sc.op("v", lambda e: e.drain())
            sc.op("v", lambda e: e.tensor_copy(pos_i32[:, :], pos_t1[:, :]))
            sc.op("v", lambda e: e.drain(), ve)
            pos_done = V(ve.n)
            sc.wait("g", pos_done)
            sc.wait("g", (c_tk, 0))
            sc.wait("g", (c_ys[0], 16))
            sc.wait("g", (c_iz, 16))
            for tc in range(16):
                sc.wait("g", (c_is[tc % 2], c_is[tc % 2].n))
                sc.op("g", lambda e, tc=tc: e.indirect_dma_start(
                    inv_scr[0:128, :],
                    IndirectOffsetOnAxis(pos_i32[:, tc:tc + 1], 0),
                    tokid_sb[:, tc:tc + 1], None), c_is[tc % 2], 16)
            sc.wait("g", (c_is[0], c_is[0].n))
            sc.wait("g", (c_is[1], c_is[1].n))
            sc.wait("y", (c_xs[0], c_xs[0].n))
            sc.wait("y", (c_xs[1], c_xs[1].n))
            sc.wait("y", (c_fz[0], c_fz[0].n))
            sc.wait("y", (c_fz[1], c_fz[1].n))


            with nc.Block() as blk:
                @blk.tensor
                def _(e):
                    sc.replay("t", e)

                @blk.vector
                def _(e):
                    sc.replay("v", e)

                @blk.scalar
                def _(e):
                    sc.replay("s", e)

                @blk.gpsimd
                def _(e):
                    sc.replay("g", e)

                @blk.sync
                def _(e):
                    sc.replay("y", e)

            b1stack.close()

            def fsb2(name, shape, dtype=F32):
                return fstack.enter_context(nc.sbuf_tensor(name, shape, dtype))

            xnrows = fsb2("xnrows", [128, 3, D], BF16)
            inv_f = fsb2("inv_f", [128, 3], F32)
            inv_t1 = fsb2("inv_t1", [128, 3], F32)
            inv_eq = fsb2("inv_eq", [128, 3], F32)
            gidx = fsb2("gidx", [128, 3], I32)
            sidx_f = fsb2("sidx_f", [128, 3], F32)
            sidx = fsb2("sidx", [128, 3], I32)
            ffw = fsb2("ffw", [128, 2, D], BF16)
            xn_g = fsb2("xn_g", [128, DC, CAP], BF16)
            u_sb = fsb2("u_sb", [128, 32, CAP], BF16)
            sig_sb = fsb2("sig_sb", [128, 8, CAP], BF16)
            w1_ch = fsb2("w1_ch", [128, 2, DC, 1024], BF16)
            w2_ch = fsb2("w2_ch", [128, 2, 32, 128], BF16)
            yT_sb = fsb2("yT_sb", [128, DC, CAP], BF16)
            y_g = fsb2("y_g", [128, CAP // 128, D], BF16)
            r_in = fsb2("r_in", [128, 1, D], F32)
            out_sb = fsb2("out_sb", [128, 2, D], F32)
            ps_b = fstack.enter_context(nc.psum_tensor("ps_b2", [128, 7, 512], F32))
            ps_g2 = fstack.enter_context(nc.psum_tensor("ps_g2", [128, 4, 128], BF16))

            sc = Sched()
            bank_free = {k: None for k in range(8)}
            g2_free = [None]
            w1_used = {0: None, 1: None}
            w2_used = {0: None, 1: None}
            sig_ready = {}
            sig_used = {}
            bo_done = None
            xnrows_used = [None]
            sc_done = {0: None, 1: None, 2: None}

            for ex in range(E):
                for pc in range(3):
                    r0 = ex * CAP + pc * 128
                    sc.wait("y", sc_done[pc])
                    sc.wait("y", (c_iv[pc], c_iv[pc].n))
                    sc.op("y", lambda e, r0=r0, pc=pc: e.dma_start(
                        out=inv_f[:, pc:pc + 1], in_=inv_scr[r0:r0 + 128, :]),
                        c_iv[pc], 16)
                iv_ev = [(c_iv[pc], c_iv[pc].n) for pc in range(3)]
                for ev in iv_ev:
                    sc.wait("v", ev)
                sc.op("v", lambda e: e.tensor_scalar_add(
                    inv_t1[:, :], inv_f[:, :], -1.0))
                sc.op("v", lambda e: e.drain())
                sc.op("v", lambda e: e.tensor_scalar(
                    inv_eq[:, :], inv_f[:, :], 0.0, None, op0=ALU.is_equal))
                sc.op("v", lambda e: e.drain())
                sc.op("v", lambda e: e.tensor_scalar_max(
                    inv_f[:, :], inv_t1[:, :], 0.0))
                sc.op("v", lambda e: e.drain())
                sc.op("v", lambda e: e.tensor_copy(gidx[:, :], inv_f[:, :]))
                sc.op("v", lambda e: e.drain())
                sc.op("v", lambda e: e.scalar_tensor_tensor(
                    sidx_f[:, :], inv_eq[:, :], float(TQ + 1), inv_t1[:, :],
                    op0=ALU.mult, op1=ALU.add))
                sc.op("v", lambda e: e.drain())
                sc.op("v", lambda e: e.tensor_copy(sidx[:, :], sidx_f[:, :]))
                sc.op("v", lambda e: e.drain(), ve)
                idx_done = V(ve.n)
                sc.wait("g", idx_done)
                sc.wait("g", xnrows_used[0])
                for pc in range(3):
                    sc.wait("g", (c_gi[pc], c_gi[pc].n))
                    sc.op("g", lambda e, pc=pc: e.indirect_dma_start(
                        xnrows[:, pc, :], None,
                        xn_scr[0:128, :],
                        IndirectOffsetOnAxis(gidx[:, pc:pc + 1], 0)), c_gi[pc], 16)
                gi_ev = [(c_gi[pc], c_gi[pc].n) for pc in range(3)]
                for pc in range(3):
                    sc.wait("t", gi_ev[pc])
                    for dg in range(2):
                        sc.wait("t", g2_free[0])
                        for j in range(4):
                            dc = dg * 4 + j
                            inc = te if j == 3 else None
                            sc.op("t", lambda e, pc=pc, dc=dc, j=j: e.transpose(
                                ps_g2[:, j, :],
                                xnrows[:, pc, dc * 128:(dc + 1) * 128],
                                ident[:, :]), inc)
                        t_x = T(te.n)
                        if pc == 2 and dg == 1:
                            xnrows_used[0] = T(te.n)
                        sc.wait("v", t_x)
                        sc.op("v", lambda e, pc=pc, dg=dg: e.tensor_copy(
                            xn_g[:, dg * 4:(dg + 1) * 4, pc * 128:(pc + 1) * 128],
                            ps_g2[:, :, :]), ve)
                        g2_free[0] = V(ve.n)
                xng_done = V(ve.n)

                order = []
                for i in range(4):
                    order += [4 + i, i]
                for oi, ci in enumerate(order):
                    wb = oi % 2
                    sc.wait("g", w1_used[wb])
                    sc.op("g", lambda e, ex=ex, ci=ci, wb=wb: e.dma_start(
                        out=w1_ch[:, wb, :, :],
                        in_=w1[ex, :, ci * 1024:(ci + 1) * 1024]
                        .rearrange("(dc p) f -> p dc f", p=128)), c_w1[wb], 16)
                    g_w1 = (c_w1[wb], c_w1[wb].n)
                    sc.wait("t", g_w1)
                    sc.wait("t", xng_done)
                    is_g = ci >= 4
                    for fm in range(8):
                        bk = 4 + (fm % 3)
                        sc.wait("t", bank_free[bk])
                        for dc in range(DC):
                            inc = te if dc == DC - 1 else None
                            sc.op("t", lambda e, wb=wb, fm=fm, dc=dc, bk=bk: e.matmul(
                                ps_b[:, bk, 0:CAP],
                                w1_ch[:, wb, dc, fm * 128:(fm + 1) * 128],
                                xn_g[:, dc, :],
                                start=(dc == 0), stop=(dc == DC - 1)), inc)
                        t_h = T(te.n)
                        fc64 = ci * 8 + fm
                        fs = fm % 8
                        if is_g:
                            sc.wait("s", t_h)
                            sc.wait("s", sig_used.get(fs))
                            sc.op("s", lambda e, ex=ex, fs=fs, bk=bk, fc64=fc64:
                                  e.activation(
                                      sig_sb[:, fs, :], ps_b[:, bk, 0:CAP], AF.Sigmoid,
                                      bias=b1_sb[:, ex, fc64:fc64 + 1], scale=1.0))
                            sc.op("s", lambda e: e.drain(), se)
                            bank_free[bk] = SE(se.n)
                            sig_ready[fs] = SE(se.n)
                        else:
                            sc.wait("v", t_h)
                            sc.wait("v", sig_ready[fs])
                            fc = ci * 8 + fm
                            sc.op("v", lambda e, ex=ex, fs=fs, bk=bk, fc64=fc64, fc=fc:
                                  e.scalar_tensor_tensor(
                                      u_sb[:, fc, :], ps_b[:, bk, 0:CAP],
                                      b1_sb[:, ex, fc64:fc64 + 1],
                                      sig_sb[:, fs, :],
                                      op0=ALU.add, op1=ALU.mult), ve)
                            bank_free[bk] = V(ve.n)
                            sig_used[fs] = V(ve.n)
                    w1_used[wb] = T(te.n)
                u_all = V(ve.n)

                v_yT_ev = {}
                for dch in range(DC + 1):
                    if dch < DC:
                        wb = dch % 2
                        sc.wait("g", w2_used[wb])
                        sc.op("g", lambda e, ex=ex, dch=dch, wb=wb: e.dma_start(
                            out=w2_ch[:, wb, :, :],
                            in_=w2[ex, :, dch * 128:(dch + 1) * 128]
                            .rearrange("(fc p) d -> p fc d", p=128)), c_w2[wb], 16)
                        g_w2 = (c_w2[wb], c_w2[wb].n)
                        sc.wait("t", g_w2)
                        sc.wait("t", u_all)
                        bk = 4 + (dch % 3)
                        sc.wait("t", bank_free[bk])
                        for fc in range(32):
                            inc = te if fc == 31 else None
                            sc.op("t", lambda e, wb=wb, fc=fc, bk=bk: e.matmul(
                                ps_b[:, bk, 0:CAP], w2_ch[:, wb, fc, :], u_sb[:, fc, :],
                                start=(fc == 0), stop=(fc == 31)), inc)
                        t_y = T(te.n)
                        w2_used[wb] = T(te.n)
                        sc.wait("v", t_y)
                        sc.op("v", lambda e, ex=ex, dch=dch, bk=bk: e.tensor_scalar(
                            yT_sb[:, dch, :], ps_b[:, bk, 0:CAP],
                            b2_sb[:, ex, dch:dch + 1], None, op0=ALU.add), ve)
                        bank_free[bk] = V(ve.n)
                        v_yT_ev[dch] = V(ve.n)
                    if dch >= 1:
                        dj = dch - 1
                        sc.wait("t", v_yT_ev[dj])
                        sc.wait("t", g2_free[0])
                        npc = CAP // 128
                        for js in range(npc):
                            inc = te if js == npc - 1 else None
                            sc.op("t", lambda e, dj=dj, js=js: e.transpose(
                                ps_g2[:, js, :], yT_sb[:, dj, js * 128:(js + 1) * 128],
                                ident[:, :]), inc)
                        t_ytr = T(te.n)
                        sc.wait("v", t_ytr)
                        for pc in range(3):
                            sc.wait("v", sc_done[pc])
                        sc.op("v", lambda e, dj=dj, npc=npc: e.tensor_copy(
                            y_g[:, :, dj * 128:(dj + 1) * 128], ps_g2[:, 0:npc, :]), ve)
                        g2_free[0] = V(ve.n)
                yg_done = V(ve.n)

                sc.wait("g", yg_done)
                sc.wait("g", idx_done)
                for pc in range(3):
                    sc.wait("g", (c_sc[pc], c_sc[pc].n))
                    sc.op("g", lambda e, pc=pc: e.indirect_dma_start(
                        ffn_scr[0:128, :],
                        IndirectOffsetOnAxis(sidx[:, pc:pc + 1], 0),
                        y_g[:, pc, :], None), c_sc[pc], 16)
                    sc_done[pc] = (c_sc[pc], c_sc[pc].n)
            ffn_done = V(ve.n)

            rin_dma = {}
            rin_used = {}
            outst = {}
            for tt in range(16):
                par = tt % 2
                sc.wait("y", rin_used.get(tt - 1))
                sc.op("y", lambda e, tt=tt: e.dma_start(
                    out=r_in[:, 0, :], in_=out[tt * 128:(tt + 1) * 128, :]),
                    c_ri[0], 16)
                rin_dma[tt] = (c_ri[0], c_ri[0].n)
                for pc in range(3):
                    sc.wait("y", sc_done[pc])
                sc.wait("y", (c_ff[par], c_ff[par].n))
                sc.op("y", lambda e, tt=tt, par=par: e.dma_start(
                    out=ffw[:, par, :], in_=ffn_scr[tt * 128:(tt + 1) * 128, :]),
                    c_ff[par], 16)
                ffw_ev = (c_ff[par], c_ff[par].n)
                sc.wait("v", ffw_ev)
                sc.wait("v", rin_dma[tt])
                sc.wait("v", bo_done)
                sc.wait("v", outst.get(tt - 2))
                sc.op("v", lambda e, tt=tt, par=par: e.tensor_tensor(
                    out_sb[:, par, :], r_in[:, 0, :], ffw[:, par, :],
                    op=ALU.add))
                sc.op("v", lambda e: e.drain())
                sc.op("v", lambda e, par=par: e.tensor_tensor(
                    out_sb[:, par, :], out_sb[:, par, :], bo_bc[:, :],
                    op=ALU.add), ve)
                rin_used[tt] = V(ve.n)
                v_o = V(ve.n)
                sc.wait("y", v_o)
                sc.op("y", lambda e, tt=tt, par=par: e.dma_start(
                    out=out[tt * 128:(tt + 1) * 128, :], in_=out_sb[:, par, :]),
                    c_os[par], 16)
                outst[tt] = (c_os[par], c_os[par].n)

            sc.wait("y", (c_os[0], c_os[0].n))
            sc.wait("y", (c_os[1], c_os[1].n))

            with nc.Block() as blk:
                @blk.tensor
                def _(e):
                    sc.replay("t", e)

                @blk.vector
                def _(e):
                    sc.replay("v", e)

                @blk.scalar
                def _(e):
                    sc.replay("s", e)

                @blk.gpsimd
                def _(e):
                    sc.replay("g", e)

                @blk.sync
                def _(e):
                    sc.replay("y", e)

    return nc


def kernel(L0_attn_x, L0_attn_w_q, L0_attn_b_q, L0_attn_w_k, L0_attn_b_k,
           L0_attn_w_v, L0_attn_b_v, L0_attn_w_out, L0_attn_b_out,
           L0_ffn_router_w, L0_ffn_router_b, L0_ffn_w1, L0_ffn_b1,
           L0_ffn_w2, L0_ffn_b2):
    nc = build_program()

    x = np.ascontiguousarray(np.asarray(L0_attn_x, dtype=np.float32))
    wqT = np.ascontiguousarray(np.asarray(L0_attn_w_q, np.float32).T)
    wkT = np.ascontiguousarray(np.asarray(L0_attn_w_k, np.float32).T)
    wvT = np.ascontiguousarray(np.asarray(L0_attn_w_v, np.float32).T)
    woT = np.ascontiguousarray(np.asarray(L0_attn_w_out, np.float32).T)
    bq = np.ascontiguousarray(np.asarray(L0_attn_b_q, np.float32).reshape(8, 128).T)
    bk = np.ascontiguousarray(np.asarray(L0_attn_b_k, np.float32).reshape(8, 128).T)
    bv = np.ascontiguousarray(np.asarray(L0_attn_b_v, np.float32).reshape(8, 128).T)
    bo = np.ascontiguousarray(np.asarray(L0_attn_b_out, np.float32).reshape(8, 128).T)
    rwh = np.ascontiguousarray(np.asarray(L0_ffn_router_w, np.float32))
    rbh = np.ascontiguousarray(np.asarray(L0_ffn_router_b, np.float32).reshape(1, E))
    w1h = np.ascontiguousarray(np.asarray(L0_ffn_w1, np.float32))
    b1h = np.ascontiguousarray(
        np.asarray(L0_ffn_b1, np.float32).reshape(E, FF // 128, 128).transpose(2, 0, 1))
    w2h = np.ascontiguousarray(np.asarray(L0_ffn_w2, np.float32))
    b2h = np.ascontiguousarray(
        np.asarray(L0_ffn_b2, np.float32).reshape(E, DC, 128).transpose(2, 0, 1))
    triu = np.triu(np.ones((128, 128), np.float32))
    ones2d_h = np.ones((128, 128), np.float32)
    tokid_h = np.ascontiguousarray(
        (np.arange(16, dtype=np.float32)[None, :] * 128.0
         + np.arange(128, dtype=np.float32)[:, None] + 1.0))
    ecap_h = (np.arange(8, dtype=np.float32) * float(CAP)).reshape(1, 8)
    ident = np.eye(128, dtype=np.float32)
    ones = np.ones((1, 128), dtype=np.float32)

    in_maps = []
    for c in range(8):
        b, qq = c // 4, c % 4
        in_maps.append({
            "xb": np.ascontiguousarray(x[b]),
            "xq": np.ascontiguousarray(x[b, qq * TQ:(qq + 1) * TQ]),
            "wqT": wqT, "wkT": wkT, "wvT": wvT, "woT": woT,
            "bq_in": bq, "bk_in": bk, "bv_in": bv, "bo_in": bo,
            "rw": rwh, "rb_in": rbh,
            "w1": w1h, "b1_in": b1h, "w2": w2h, "b2_in": b2h,
            "triu_in": triu, "ones2d_in": ones2d_h, "tokid_in": tokid_h,
            "ecap_in": ecap_h,
            "ident_in": ident, "ones_in": ones,
        })

    res = run_bass_kernel_spmd(nc, in_maps, core_ids=list(range(8)))
    full = np.empty((2, S, D), dtype=np.float32)
    for c in range(8):
        b, qq = c // 4, c % 4
        full[b, qq * TQ:(qq + 1) * TQ] = res.results[c]["out"]
    return full


# version 21
# speedup vs baseline: 1.2189x; 1.0295x over previous
import contextlib
import numpy as np
import concourse.bass as bass
from concourse.bass_core_rust import IndirectOffsetOnAxis
import concourse.mybir as mybir
from concourse.bass_utils import run_bass_kernel_spmd

F32 = mybir.dt.float32
BF16 = mybir.dt.bfloat16
I32 = mybir.dt.int32
AF = mybir.ActivationFunctionType
ALU = mybir.AluOpType
AX = mybir.AxisListType

D = 1024
DC = 8
S = 8192
TQ = 2048
KB = 512
NBLK = S // KB
NPRO = TQ // KB
NQT = TQ // 128
E = 8
FF = 8192
TH = 1024
SCALE = 0.03125
BIGPOS = 3.0e38
CAP = 384
HSLOT = 4 * CAP

ENGS = ("t", "v", "s", "g", "y")


class Ctr:
    def __init__(self, stack, nc, name):
        self.sem = stack.enter_context(nc.semaphore(name))
        self.n = 0


class Sched:
    def __init__(self):
        self.q = {k: [] for k in ENGS}
        self.lastw = {k: {} for k in ENGS}

    def op(self, eng, fn, inc=None, k=1):
        if inc is not None:
            inc.n += k
        self.q[eng].append((fn, inc, k))

    def wait(self, eng, ev):
        if ev is None:
            return
        ctr, v = ev
        if v <= 0 or self.lastw[eng].get(id(ctr), -1) >= v:
            return
        self.lastw[eng][id(ctr)] = v
        self.q[eng].append((None, ctr, v))

    def replay(self, eng, e):
        for fn, aux, k in self.q[eng]:
            if fn is None:
                e.wait_ge(aux.sem, k)
            else:
                inst = fn(e)
                if aux is not None:
                    inst.then_inc(aux.sem, k)


def build_program():
    nc = bass.Bass("TRN2", target_bir_lowering=False)

    xb = nc.dram_tensor("xb", [S, D], F32, kind="ExternalInput").ap()
    xq = nc.dram_tensor("xq", [TQ, D], F32, kind="ExternalInput").ap()
    wqT = nc.dram_tensor("wqT", [D, D], F32, kind="ExternalInput").ap()
    wkT = nc.dram_tensor("wkT", [D, D], F32, kind="ExternalInput").ap()
    wvT = nc.dram_tensor("wvT", [D, D], F32, kind="ExternalInput").ap()
    woT = nc.dram_tensor("woT", [D, D], F32, kind="ExternalInput").ap()
    bq_in = nc.dram_tensor("bq_in", [128, 8], F32, kind="ExternalInput").ap()
    bk_in = nc.dram_tensor("bk_in", [128, 8], F32, kind="ExternalInput").ap()
    bv_in = nc.dram_tensor("bv_in", [128, 8], F32, kind="ExternalInput").ap()
    bo_in = nc.dram_tensor("bo_in", [128, 8], F32, kind="ExternalInput").ap()
    rw = nc.dram_tensor("rw", [D, E], F32, kind="ExternalInput").ap()
    rb_in = nc.dram_tensor("rb_in", [1, E], F32, kind="ExternalInput").ap()
    w1 = nc.dram_tensor("w1", [E, D, FF], F32, kind="ExternalInput").ap()
    b1_in = nc.dram_tensor("b1_in", [128, E, FF // 128], F32, kind="ExternalInput").ap()
    w2 = nc.dram_tensor("w2", [E, FF // 2, D], F32, kind="ExternalInput").ap()
    b2_in = nc.dram_tensor("b2_in", [128, E, DC], F32, kind="ExternalInput").ap()
    triu_in = nc.dram_tensor("triu_in", [128, 128], F32, kind="ExternalInput").ap()
    ones2d_in = nc.dram_tensor("ones2d_in", [128, 128], F32, kind="ExternalInput").ap()
    tokid_in = nc.dram_tensor("tokid_in", [128, 16], F32, kind="ExternalInput").ap()
    ecap_in = nc.dram_tensor("ecap_in", [1, E], F32, kind="ExternalInput").ap()
    ident_in = nc.dram_tensor("ident_in", [128, 128], F32, kind="ExternalInput").ap()
    ones_in = nc.dram_tensor("ones_in", [1, 128], F32, kind="ExternalInput").ap()
    out = nc.dram_tensor("out", [TQ, D], F32, kind="ExternalOutput").ap()
    xn_scr = nc.dram_tensor("xn_scr", [TQ, D], BF16, kind="Internal").ap()
    ffn_scr = nc.dram_tensor("ffn_scr", [TQ + 1, D], BF16, kind="Internal").ap()
    inv_scr = nc.dram_tensor("inv_scr", [3200, 1], F32, kind="Internal").ap()

    with contextlib.ExitStack() as pstack:
        def psb(name, shape, dtype=F32):
            return pstack.enter_context(nc.sbuf_tensor(name, shape, dtype))

        ident = psb("ident", [128, 128], BF16)
        ones = psb("ones", [1, 128], BF16)
        eps_col = psb("eps_col", [128, 1], F32)
        bo2_col = psb("bo2_col", [128, 8], F32)

        te = Ctr(pstack, nc, "te")
        ve = Ctr(pstack, nc, "ve")
        se = Ctr(pstack, nc, "se")
        gc = Ctr(pstack, nc, "gc")
        _ctr_n = [0]

        def new_ctr(tag):
            _ctr_n[0] += 1
            return Ctr(pstack, nc, f"c{_ctr_n[0]}_{tag}")

        def T(v):
            return (te, v)

        def V(v):
            return (ve, v)

        def SE(v):
            return (se, v)

        def G(v):
            return (gd, v)

        def Y(v):
            return (sy, v)

        with contextlib.ExitStack() as astack:
            def asb(name, shape, dtype=F32):
                return astack.enter_context(nc.sbuf_tensor(name, shape, dtype))

            wq_wo = asb("wq_wo", [128, DC, D], BF16)
            wk_sb = asb("wk_sb", [128, DC, D], BF16)
            wv_sb = asb("wv_sb", [128, DC, D], BF16)
            bq_col = asb("bq_col", [128, 8], F32)
            bk_col = asb("bk_col", [128, 8], F32)
            bv_col = asb("bv_col", [128, 8], BF16)
            bo_col = asb("bo_col", [128, 8], F32)
            x_blk = asb("x_blk", [128, 4, D], BF16)
            xn_sb = asb("xn_sb", [128, 2, D], BF16)
            xnT = asb("xnT", [128, DC, KB], BF16)
            KT = asb("KT", [128, 8, KB], BF16)
            V_sb = asb("V_sb", [128, 4, D], BF16)
            QT = asb("QT", [128, 8, TQ], BF16)
            OACC = asb("OACC", [128, NQT, D], F32)
            P_sb = asb("P_sb", [128, 2, KB], BF16)
            PT_sb = asb("PT_sb", [128, 2, 4, 128], BF16)
            st12 = asb("st12", [128, 4, 12], F32)
            mv_sb = asb("mv_sb", [128, 4, 2], F32)
            lnv_sb = asb("lnv_sb", [128, 4], F32)
            rstd_sb = asb("rstd_sb", [128, 4], F32)
            nm2 = asb("nm2", [128, 2, NQT], F32)
            l2 = asb("l2", [128, 2, NQT], F32)
            nmblk = asb("nmblk", [128, NQT], F32)
            d_sb = asb("d_sb", [128, NQT], F32)
            alpha = asb("alpha", [128, NQT], F32)
            bsum = asb("bsum", [128, NQT], F32)
            linv = asb("linv", [128, NQT], F32)
            nms = asb("nms", [128, NQT], F32)
            O_bf = asb("O_bf", [128, 2, D], BF16)
            OT_sb = asb("OT_sb", [128, 2, DC, 128], BF16)
            xq_f32 = asb("xq_f32", [128, 1, D], F32)
            r_out = asb("r_out", [128, 2, D], F32)

            ps_s = astack.enter_context(nc.psum_tensor("ps_s", [128, 2, 512], F32))
            ps_o = astack.enter_context(nc.psum_tensor("ps_o", [128, 2, 2, 512], F32))
            ps_tr = astack.enter_context(nc.psum_tensor("ps_tr", [128, 2, 8, 128], BF16))

            sc = Sched()

            ps_s_free = {0: None, 1: None}
            ps_o_free = {0: None, 1: None}
            trcopy = {0: None, 1: None}
            xn_used = {0: None, 1: None}
            pt_used = {0: None, 1: None}
            p_trans = {0: None, 1: None}
            x_used = [None]
            blk_i = [0]

            sc.op("g", lambda e: e.memset(eps_col[:, :], 1e-5), gc)
            sc.op("g", lambda e: e.memset(nm2[:, 0, :], BIGPOS), gc)
            sc.op("g", lambda e: e.memset(l2[:, 0, :], 0.0), gc)
            sc.op("g", lambda e: e.memset(OACC[:, :, :], 0.0), gc)
            setup_evs = []

            def setup_dma(fn, tag):
                c = new_ctr(tag)
                sc.op("g", fn, c, 16)
                setup_evs.append((c, c.n))

            setup_dma(lambda e: e.dma_start(out=ident[:, :], in_=ident_in), "id")
            setup_dma(lambda e: e.dma_start(out=ones[:, :], in_=ones_in), "on")
            setup_dma(lambda e: e.dma_start(
                out=wq_wo[:, :, :], in_=wqT.rearrange("(dc p) e -> p dc e", p=128)), "wq")
            setup_dma(lambda e: e.dma_start(
                out=wk_sb[:, :, :], in_=wkT.rearrange("(dc p) e -> p dc e", p=128)), "wk")
            setup_dma(lambda e: e.dma_start(
                out=wv_sb[:, :, :], in_=wvT.rearrange("(dc p) e -> p dc e", p=128)), "wv")
            setup_dma(lambda e: e.dma_start(out=bq_col[:, :], in_=bq_in), "bq")
            setup_dma(lambda e: e.dma_start(out=bk_col[:, :], in_=bk_in), "bk")
            setup_dma(lambda e: e.dma_start(out=bv_col[:, :], in_=bv_in), "bv")
            setup_dma(lambda e: e.dma_start(out=bo_col[:, :], in_=bo_in), "bo")
            setup_c = (gc, gc.n)

            def wait_setup(eng):
                for ev in setup_evs:
                    sc.wait(eng, ev)
            c_x = new_ctr("x")
            c_wo = new_ctr("wo")
            c_xq = [new_ctr("xq0"), new_ctr("xq1")]
            c_ro = [new_ctr("ro0"), new_ctr("ro1")]

            def ln_block(src_ap, tok0):
                sc.wait("g", x_used[0])
                sc.op("g", lambda e, a=src_ap, t=tok0: e.dma_start(
                    out=x_blk[:, :, :],
                    in_=a[t:t + KB, :].rearrange("(tt p) d -> p tt d", p=128)), c_x, 16)
                x_dma = (c_x, c_x.n)
                sc.wait("v", x_dma)
                last = None
                s_rstd = {}
                for tt in range(4):
                    sc.op("v", lambda e, tt=tt: e.bn_stats(
                        st12[:, tt, 0:6], x_blk[:, tt, 0:512]))
                    sc.op("v", lambda e, tt=tt: e.bn_stats(
                        st12[:, tt, 6:12], x_blk[:, tt, 512:1024]))
                    sc.op("v", lambda e: e.drain())
                    sc.op("v", lambda e, tt=tt: e.bn_aggr(
                        mv_sb[:, tt, :], st12[:, tt, :]))
                    sc.op("v", lambda e: e.drain(), ve)
                    v_st = V(ve.n)
                    sc.wait("s", v_st)
                    wait_setup("s")
                    sc.wait("s", setup_c)
                    sc.op("s", lambda e, tt=tt: e.activation(
                        lnv_sb[:, tt:tt + 1], mv_sb[:, tt, 1:2], AF.Ln,
                        bias=eps_col[:, :], scale=1.0))
                    sc.op("s", lambda e: e.drain())
                    sc.op("s", lambda e, tt=tt: e.activation(
                        rstd_sb[:, tt:tt + 1], lnv_sb[:, tt:tt + 1], AF.Exp,
                        bias=0.0, scale=-0.5))
                    sc.op("s", lambda e: e.drain(), se)
                    s_rstd[tt] = SE(se.n)
                for tt in range(4):
                    xp = tt % 2
                    sc.wait("v", s_rstd[tt])
                    sc.wait("v", xn_used[xp])
                    sc.op("v", lambda e, tt=tt, xp=xp: e.tensor_scalar(
                        xn_sb[:, xp, :], x_blk[:, tt, :],
                        mv_sb[:, tt, 0:1], rstd_sb[:, tt:tt + 1],
                        op0=ALU.subtract, op1=ALU.mult), ve)
                    v_n = V(ve.n)
                    if tt == 3:
                        x_used[0] = v_n
                    par = tt % 2
                    sc.wait("t", v_n)
                    sc.wait("t", trcopy[par])
                    for dc in range(DC):
                        inc = te if dc == DC - 1 else None
                        sc.op("t", lambda e, xp=xp, dc=dc, par=par: e.transpose(
                            ps_tr[:, par, dc, :], xn_sb[:, xp, dc * 128:(dc + 1) * 128],
                            ident[:, :]), inc)
                    xn_used[xp] = T(te.n)
                    sc.wait("v", xn_used[xp])
                    sc.op("v", lambda e, tt=tt, par=par: e.tensor_copy(
                        xnT[:, :, tt * 128:(tt + 1) * 128], ps_tr[:, par, :, :]), ve)
                    trcopy[par] = V(ve.n)
                    last = V(ve.n)
                blk_i[0] += 1
                return last

            wait_setup("t")
            sc.wait("v", setup_c)

            for ob in range(NPRO):
                v_xnT = ln_block(xq, ob * KB)
                sc.wait("t", v_xnT)
                for ec in range(8):
                    par = ec % 2
                    sc.wait("t", ps_s_free[par])
                    for dc in range(DC):
                        inc = te if dc == DC - 1 else None
                        sc.op("t", lambda e, ec=ec, dc=dc, par=par: e.matmul(
                            ps_s[:, par, :], wq_wo[:, dc, ec * 128:(ec + 1) * 128],
                            xnT[:, dc, :], start=(dc == 0), stop=(dc == DC - 1)), inc)
                    t_q = T(te.n)
                    sc.wait("v", t_q)
                    sc.op("v", lambda e, ec=ec, ob=ob, par=par: e.tensor_scalar(
                        QT[:, ec, ob * KB:(ob + 1) * KB], ps_s[:, par, :],
                        bq_col[:, ec:ec + 1], None, op0=ALU.add), ve)
                    ps_s_free[par] = V(ve.n)
            qt_done_te = T(te.n)
            qt_done_ve = V(ve.n)

            sc.wait("g", qt_done_te)
            sc.op("g", lambda e: e.dma_start(
                out=wq_wo[:, :, :], in_=woT.rearrange("(dc p) e -> p dc e", p=128)), c_wo, 16)
            wo_loaded = (c_wo, c_wo.n)

            sc.wait("t", wo_loaded)
            for ec in range(8):
                sc.wait("t", ps_s_free[0])
                for dc in range(DC):
                    inc = te if dc == DC - 1 else None
                    sc.op("t", lambda e, ec=ec, dc=dc: e.matmul(
                        ps_s[:, 0, 0:1], wq_wo[:, dc, ec * 128:(ec + 1) * 128],
                        bv_col[:, dc:dc + 1], start=(dc == 0), stop=(dc == DC - 1)), inc)
                t_b = T(te.n)
                sc.wait("v", t_b)
                sc.op("v", lambda e, ec=ec: e.tensor_scalar(
                    bo2_col[:, ec:ec + 1], ps_s[:, 0, 0:1],
                    bo_col[:, ec:ec + 1], None, op0=ALU.add), ve)
                ps_s_free[0] = V(ve.n)

            v_xnT_cur = ln_block(xb, 0)
            for kb in range(NBLK):
                rp, wp = kb % 2, (kb + 1) % 2
                sc.wait("t", v_xnT_cur)
                sc.wait("t", qt_done_ve)
                kt_ready = None
                v_ready = None

                def emit_kt(ec):
                    nonlocal kt_ready
                    par = ec % 2
                    sc.wait("t", ps_s_free[par])
                    for dc in range(DC):
                        inc = te if dc == DC - 1 else None
                        sc.op("t", lambda e, ec=ec, dc=dc, par=par: e.matmul(
                            ps_s[:, par, :], wk_sb[:, dc, ec * 128:(ec + 1) * 128],
                            xnT[:, dc, :], start=(dc == 0), stop=(dc == DC - 1)), inc)
                    t_k = T(te.n)
                    sc.wait("v", t_k)
                    sc.op("v", lambda e, ec=ec, par=par: e.tensor_scalar(
                        KT[:, ec, :], ps_s[:, par, :],
                        bk_col[:, ec:ec + 1], None, op0=ALU.add), ve)
                    ps_s_free[par] = V(ve.n)
                    kt_ready = V(ve.n)

                def emit_v(tt):
                    nonlocal v_ready
                    par = tt % 2
                    sc.wait("t", ps_o_free[par])
                    for eh in range(2):
                        for dc in range(DC):
                            inc = te if (eh == 1 and dc == DC - 1) else None
                            sc.op("t", lambda e, tt=tt, eh=eh, dc=dc, par=par: e.matmul(
                                ps_o[:, par, eh, :],
                                xnT[:, dc, tt * 128:(tt + 1) * 128],
                                wv_sb[:, dc, eh * 512:(eh + 1) * 512],
                                start=(dc == 0), stop=(dc == DC - 1)), inc)
                    t_v = T(te.n)
                    sc.wait("v", t_v)
                    sc.op("v", lambda e, tt=tt, par=par: e.tensor_copy(
                        V_sb[:, tt, :],
                        ps_o[:, par, :, :].rearrange("p a b -> p (a b)")), ve)
                    ps_o_free[par] = V(ve.n)
                    v_ready = V(ve.n)

                for i4 in range(4):
                    emit_kt(2 * i4)
                    emit_kt(2 * i4 + 1)
                    emit_v(i4)

                v_xnT_cur = (ln_block(xb, (kb + 1) * KB)
                             if kb + 1 < NBLK else None)

                s_p_ev = {}
                v_pt_ev = {}
                for qi in range(NQT + 2):
                    if qi < NQT:
                        par = qi % 2
                        sc.wait("t", kt_ready)
                        sc.wait("t", ps_s_free[par])
                        for ec in range(8):
                            inc = te if ec == 7 else None
                            sc.op("t", lambda e, ec=ec, qi=qi, par=par: e.matmul(
                                ps_s[:, par, :], QT[:, ec, qi * 128:(qi + 1) * 128],
                                KT[:, ec, :], start=(ec == 0), stop=(ec == 7)), inc)
                        t_s = T(te.n)
                        sc.wait("v", t_s)
                        sc.op("v", lambda e, qi=qi, par=par: e.reduce_max(
                            nmblk[:, qi:qi + 1], ps_s[:, par, :], axis=AX.X,
                            negate=True))
                        sc.op("v", lambda e: e.drain())
                        sc.op("v", lambda e, qi=qi, rp=rp, wp=wp: e.tensor_tensor(
                            nm2[:, wp, qi:qi + 1], nm2[:, rp, qi:qi + 1],
                            nmblk[:, qi:qi + 1], op=ALU.min))
                        sc.op("v", lambda e: e.drain())
                        sc.op("v", lambda e, qi=qi, rp=rp, wp=wp: e.tensor_tensor(
                            d_sb[:, qi:qi + 1], nm2[:, wp, qi:qi + 1],
                            nm2[:, rp, qi:qi + 1], op=ALU.subtract))
                        sc.op("v", lambda e: e.drain(), ve)
                        v_stat = V(ve.n)
                        pp = qi % 2
                        sc.wait("s", v_stat)
                        sc.wait("s", p_trans[pp])
                        sc.op("s", lambda e, qi=qi: e.activation(
                            alpha[:, qi:qi + 1], d_sb[:, qi:qi + 1], AF.Exp,
                            bias=0.0, scale=SCALE))
                        sc.op("s", lambda e, qi=qi, wp=wp: e.mul(
                            nms[:, qi:qi + 1], nm2[:, wp, qi:qi + 1], SCALE))
                        sc.op("s", lambda e: e.drain())
                        sc.op("s", lambda e, qi=qi, par=par, pp=pp: e.activation(
                            P_sb[:, pp, :], ps_s[:, par, :], AF.Exp,
                            bias=nms[:, qi:qi + 1], scale=SCALE,
                            accum_out=bsum[:, qi:qi + 1]))
                        sc.op("s", lambda e: e.drain(), se)
                        s_p_ev[qi] = SE(se.n)
                        ps_s_free[par] = SE(se.n)
                        sc.wait("v", s_p_ev[qi])
                        sc.op("v", lambda e, qi=qi, rp=rp, wp=wp: e.scalar_tensor_tensor(
                            l2[:, wp, qi:qi + 1], l2[:, rp, qi:qi + 1],
                            alpha[:, qi:qi + 1], bsum[:, qi:qi + 1],
                            op0=ALU.mult, op1=ALU.add), ve)
                    if 1 <= qi <= NQT:
                        qj = qi - 1
                        par = qj % 2
                        pp = qj % 2
                        sc.wait("t", s_p_ev[qj])
                        sc.wait("t", trcopy[par])
                        for kc in range(4):
                            inc = te if kc == 3 else None
                            sc.op("t", lambda e, kc=kc, par=par, pp=pp: e.transpose(
                                ps_tr[:, par, kc, :],
                                P_sb[:, pp, kc * 128:(kc + 1) * 128],
                                ident[:, :]), inc)
                        p_trans[pp] = T(te.n)
                        t_ptr = T(te.n)
                        sc.wait("v", t_ptr)
                        sc.wait("v", pt_used[par])
                        sc.op("v", lambda e, par=par: e.tensor_copy(
                            PT_sb[:, par, :, :], ps_tr[:, par, 0:4, :]), ve)
                        trcopy[par] = V(ve.n)
                        v_pt_ev[qj] = V(ve.n)
                    if qi >= 2:
                        qj = qi - 2
                        par = qj % 2
                        sc.wait("t", v_pt_ev[qj])
                        sc.wait("t", v_ready)
                        sc.wait("t", ps_o_free[par])
                        for kc in range(4):
                            for h in range(2):
                                inc = te if (kc == 3 and h == 1) else None
                                sc.op("t", lambda e, kc=kc, h=h, par=par: e.matmul(
                                    ps_o[:, par, h, :], PT_sb[:, par, kc, :],
                                    V_sb[:, kc, h * 512:(h + 1) * 512],
                                    start=(kc == 0), stop=(kc == 3)), inc)
                        t_av = T(te.n)
                        pt_used[par] = T(te.n)
                        sc.wait("v", t_av)
                        for h in range(2):
                            sc.op("v",
                                  lambda e, qj=qj, h=h, par=par: e.scalar_tensor_tensor(
                                      OACC[:, qj, h * 512:(h + 1) * 512],
                                      OACC[:, qj, h * 512:(h + 1) * 512],
                                      alpha[:, qj:qj + 1], ps_o[:, par, h, :],
                                      op0=ALU.mult, op1=ALU.add), ve)
                        ps_o_free[par] = V(ve.n)

            final_p = NBLK % 2
            xq_dma = {}
            xq_used = {}
            rstored = {}
            ot_used = {}
            obf_used = {}
            for qi in range(NQT):
                par = qi % 2
                sc.wait("y", xq_used.get(qi - 1))
                sc.op("y", lambda e, qi=qi: e.dma_start(
                    out=xq_f32[:, 0, :], in_=xq[qi * 128:(qi + 1) * 128, :]),
                    c_xq[0], 16)
                xq_dma[qi] = (c_xq[0], c_xq[0].n)
                sc.op("v", lambda e, qi=qi, fp=final_p: e.reciprocal(
                    linv[:, qi:qi + 1], l2[:, fp, qi:qi + 1]))
                sc.op("v", lambda e: e.drain(), ve)
                v_linv = V(ve.n)
                sc.wait("s", v_linv)
                sc.wait("s", obf_used.get(par))
                sc.op("s", lambda e, qi=qi, par=par: e.mul(
                    O_bf[:, par, :], OACC[:, qi, :], linv[:, qi:qi + 1]), se)
                s_obf = SE(se.n)
                sc.wait("t", s_obf)
                sc.wait("t", trcopy[par])
                for dc in range(DC):
                    inc = te if dc == DC - 1 else None
                    sc.op("t", lambda e, dc=dc, par=par: e.transpose(
                        ps_tr[:, par, dc, :], O_bf[:, par, dc * 128:(dc + 1) * 128],
                        ident[:, :]), inc)
                obf_used[par] = T(te.n)
                t_otr = T(te.n)
                sc.wait("v", t_otr)
                sc.wait("v", ot_used.get(par))
                sc.op("v", lambda e, par=par: e.tensor_copy(
                    OT_sb[:, par, :, :], ps_tr[:, par, :, :]), ve)
                trcopy[par] = V(ve.n)
                v_ot = V(ve.n)
                sc.wait("t", v_ot)
                sc.wait("t", wo_loaded)
                sc.wait("t", ps_o_free[par])
                for h in range(2):
                    for dc in range(DC):
                        inc = te if (h == 1 and dc == DC - 1) else None
                        sc.op("t", lambda e, h=h, dc=dc, par=par: e.matmul(
                            ps_o[:, par, h, :], OT_sb[:, par, dc, :],
                            wq_wo[:, dc, h * 512:(h + 1) * 512],
                            start=(dc == 0), stop=(dc == DC - 1)), inc)
                t_proj = T(te.n)
                ot_used[par] = T(te.n)
                sc.wait("v", t_proj)
                sc.wait("v", xq_dma[qi])
                sc.wait("v", rstored.get(qi - 2))
                for h in range(2):
                    inc = ve if h == 1 else None
                    sc.op("v", lambda e, h=h, par=par: e.tensor_tensor(
                        r_out[:, par, h * 512:(h + 1) * 512],
                        xq_f32[:, 0, h * 512:(h + 1) * 512],
                        ps_o[:, par, h, :], op=ALU.add), inc)
                ps_o_free[par] = V(ve.n)
                xq_used[qi] = V(ve.n)
                v_r = V(ve.n)
                sc.wait("y", v_r)
                sc.op("y", lambda e, qi=qi, par=par: e.dma_start(
                    out=out[qi * 128:(qi + 1) * 128, :], in_=r_out[:, par, :]),
                    c_ro[par], 16)
                rstored[qi] = (c_ro[par], c_ro[par].n)

            sc.wait("y", (c_ro[0], c_ro[0].n))
            sc.wait("y", (c_ro[1], c_ro[1].n))

            with nc.Block() as blk:
                @blk.tensor
                def _(e):
                    sc.replay("t", e)

                @blk.vector
                def _(e):
                    sc.replay("v", e)

                @blk.scalar
                def _(e):
                    sc.replay("s", e)

                @blk.gpsimd
                def _(e):
                    sc.replay("g", e)

                @blk.sync
                def _(e):
                    sc.replay("y", e)

        with contextlib.ExitStack() as fstack:
            def fsb(name, shape, dtype=F32):
                return fstack.enter_context(nc.sbuf_tensor(name, shape, dtype))

            pos_col = fsb("pos_col", [128, 16], F32)
            b2_sb = fsb("b2_sb", [128, E, DC], F32)
            ones_f32 = fsb("ones_f32", [1, 128], F32)
            b1_sb = fsb("b1_sb", [128, E, 64], F32)
            bo_bc = fsb("bo_bc", [128, D], BF16)

            b1stack = contextlib.ExitStack()

            def fsb1(name, shape, dtype=F32):
                return b1stack.enter_context(nc.sbuf_tensor(name, shape, dtype))

            xf_bf = fsb1("xf_bf", [128, 4, D], BF16)
            xn_tok = fsb1("xn_tok", [128, 16, D], BF16)
            tokid_sb = fsb1("tokid_sb", [128, 16], F32)
            zfb = fsb1("zfb", [128, D], BF16)
            zf32 = fsb1("zf32", [128, 32], F32)
            pos_i32 = fsb1("pos_i32", [128, 16], I32)
            pos_t1 = fsb1("pos_t1", [128, 16], F32)
            xnTb = fsb1("xnTb", [128, DC, 512], BF16)
            st12f = fsb1("st12f", [128, 4, 12], F32)
            mvf = fsb1("mvf", [128, 4, 2], F32)
            lnvf = fsb1("lnvf", [128, 4], F32)
            rstdf = fsb1("rstdf", [128, 4], F32)
            rw_sb = fsb1("rw_sb", [128, DC, E], BF16)
            rb_row = fsb1("rb_row", [1, E], BF16)
            rb_bc = fsb1("rb_bc", [128, E], BF16)
            logits = fsb1("logits", [128, 16, E], F32)
            lgmax = fsb1("lgmax", [128, 16], F32)
            mask_bf = fsb1("mask_bf", [128, 16, E], BF16)
            triu_sb = fsb1("triu_sb", [128, 128], BF16)
            ones2d = fsb1("ones2d", [128, 128], BF16)
            ecap_row = fsb1("ecap_row", [1, E], F32)
            ecap_bc = fsb1("ecap_bc", [128, E], F32)
            tmp8 = fsb1("tmp8", [128, E], F32)
            possum = fsb1("possum", [128, 1], F32)
            bo2_bf = fsb1("bo2_bf", [128, 8], BF16)
            borow = fsb1("borow", [1, D], BF16)
            ps_b = b1stack.enter_context(nc.psum_tensor("ps_b", [128, 7, 512], F32))
            ps_trf = b1stack.enter_context(nc.psum_tensor("ps_trf", [128, 8, 128], BF16))

            sc = Sched()

            bank_free = {k: None for k in range(8)}
            trf_free = [None]
            w1_used = {0: None, 1: None}
            w2_used = {0: None, 1: None}
            sig_ready = {}
            sig_used = {}
            xf_used = [None]

            c_fs = [new_ctr("fs") for _ in range(5)]
            c_ys = [new_ctr("ys") for _ in range(5)]
            c_xf = new_ctr("xf")
            c_w1 = [new_ctr("w1a"), new_ctr("w1b"), new_ctr("w1c")]
            c_w2 = [new_ctr("w2a"), new_ctr("w2b")]
            c_ri = [new_ctr("ri0"), new_ctr("ri1")]
            c_os = [new_ctr("os0"), new_ctr("os1")]
            c_xs = [new_ctr("xs0"), new_ctr("xs1")]
            c_fz = [new_ctr("fz0"), new_ctr("fz1")]
            c_is = [new_ctr("is0"), new_ctr("is1")]
            c_tk = new_ctr("tk")
            c_iz = new_ctr("iz")
            c_iv = [new_ctr("iv0"), new_ctr("iv1"), new_ctr("iv2")]
            c_gi = [new_ctr("gi0"), new_ctr("gi1"), new_ctr("gi2")]
            c_sc = [new_ctr("sc0"), new_ctr("sc1"), new_ctr("sc2")]
            c_ff = [new_ctr("ff0"), new_ctr("ff1")]
            c_w1y = [new_ctr("w1y0"), new_ctr("w1y1")]

            sc.op("g", lambda e: e.dma_start(
                out=rw_sb[:, :, :], in_=rw.rearrange("(dc p) e -> p dc e", p=128)),
                c_fs[0], 16)
            sc.op("g", lambda e: e.dma_start(out=b1_sb[:, :, :], in_=b1_in), c_fs[1], 16)
            sc.op("g", lambda e: e.dma_start(out=rb_row[:, :], in_=rb_in), c_fs[2], 16)
            sc.op("g", lambda e: e.dma_start(out=triu_sb[:, :], in_=triu_in), c_fs[3], 16)
            sc.op("g", lambda e: e.dma_start(out=ones2d[:, :], in_=ones2d_in), c_fs[4], 16)
            sc.op("y", lambda e: e.dma_start(out=tokid_sb[:, :], in_=tokid_in), c_ys[0], 16)
            sc.op("y", lambda e: e.dma_start(out=ecap_row[:, :], in_=ecap_in), c_ys[1], 16)
            sc.op("g", lambda e: e.dma_start(out=b2_sb[:, :, :], in_=b2_in), c_ys[2], 16)
            sc.op("y", lambda e: e.dma_start(out=ones_f32[:, :], in_=ones_in), c_ys[3], 16)
            f_evs = [(c, 16) for c in c_fs] + [(c, 16) for c in c_ys[:4]]

            def wait_fsetup(eng):
                for ev in f_evs:
                    sc.wait(eng, ev)

            sc.op("v", lambda e: e.memset(zfb[:, :], 0.0))
            sc.op("v", lambda e: e.memset(zf32[:, :], 0.0))
            sc.op("v", lambda e: e.drain(), ve)
            z_done = V(ve.n)
            sc.wait("y", z_done)
            for zz in range(16):
                sc.wait("y", (c_fz[zz % 2], c_fz[zz % 2].n))
                sc.op("y", lambda e, zz=zz: e.dma_start(
                    out=ffn_scr[zz * 128:(zz + 1) * 128, :], in_=zfb[:, :]),
                    c_fz[zz % 2], 16)
            sc.op("y", lambda e: e.dma_start(
                out=ffn_scr[TQ:TQ + 1, :], in_=zfb[0:1, :]), c_fz[0], 16)
            sc.op("y", lambda e: e.dma_start(
                out=inv_scr[:, :].rearrange("(a b) c -> a (b c)", b=25),
                in_=zf32[:, 0:25]), c_iz, 16)

            wait_fsetup("t")
            sc.op("t", lambda e: e.matmul(
                ps_b[:, 0, 0:E], ones[:, :], rb_row[:, :], start=True, stop=True), te)
            t_rb = T(te.n)
            sc.wait("v", t_rb)
            sc.op("v", lambda e: e.tensor_copy(rb_bc[:, :], ps_b[:, 0, 0:E]), ve)
            rb_done = V(ve.n)
            bank_free[0] = rb_done

            sc.op("t", lambda e: e.matmul(
                ps_b[:, 1, 0:E], ones_f32[:, :], ecap_row[:, :],
                start=True, stop=True), te)
            t_ec = T(te.n)
            sc.wait("v", t_ec)
            sc.op("v", lambda e: e.tensor_copy(ecap_bc[:, :], ps_b[:, 1, 0:E]), ve)
            ec_done = V(ve.n)
            bank_free[1] = ec_done

            sc.op("v", lambda e: e.tensor_copy(bo2_bf[:, :], bo2_col[:, :]), ve)
            v_b2 = V(ve.n)
            sc.wait("t", v_b2)
            for dc in range(DC):
                inc = te if dc == DC - 1 else None
                sc.op("t", lambda e, dc=dc: e.transpose(
                    ps_trf[0:1, dc, :], bo2_bf[:, dc:dc + 1], ident[:, :]), inc)
            t_bt = T(te.n)
            sc.wait("v", t_bt)
            sc.op("v", lambda e: e.tensor_copy(
                borow[:, :], ps_trf[0:1, :, :].rearrange("p a b -> p (a b)")))
            sc.op("v", lambda e: e.drain(), ve)
            trf_free[0] = V(ve.n)
            v_brow = V(ve.n)
            sc.wait("t", v_brow)
            for h2 in range(2):
                sc.wait("t", bank_free[3])
                sc.op("t", lambda e, h2=h2: e.matmul(
                    ps_b[:, 3, :], ones[:, :], borow[:, h2 * 512:(h2 + 1) * 512],
                    start=True, stop=True), te)
                t_bo = T(te.n)
                sc.wait("v", t_bo)
                sc.op("v", lambda e, h2=h2: e.tensor_copy(
                    bo_bc[:, h2 * 512:(h2 + 1) * 512], ps_b[:, 3, :]), ve)
                bank_free[3] = V(ve.n)
            bo_done = V(ve.n)

            for b4 in range(4):
                sc.wait("g", xf_used[0])
                sc.op("g", lambda e, t0=b4 * 512: e.dma_start(
                    out=xf_bf[:, :, :],
                    in_=xq[t0:t0 + 512, :].rearrange("(tt p) d -> p tt d", p=128)),
                    c_xf, 16)
                g_x = (c_xf, c_xf.n)
                sc.wait("v", g_x)
                s_rstd = {}
                for tt in range(4):
                    sc.op("v", lambda e, tt=tt: e.bn_stats(
                        st12f[:, tt, 0:6], xf_bf[:, tt, 0:512]))
                    sc.op("v", lambda e, tt=tt: e.bn_stats(
                        st12f[:, tt, 6:12], xf_bf[:, tt, 512:1024]))
                    sc.op("v", lambda e: e.drain())
                    sc.op("v", lambda e, tt=tt: e.bn_aggr(
                        mvf[:, tt, :], st12f[:, tt, :]))
                    sc.op("v", lambda e: e.drain(), ve)
                    v_st = V(ve.n)
                    sc.wait("s", v_st)
                    sc.op("s", lambda e, tt=tt: e.activation(
                        lnvf[:, tt:tt + 1], mvf[:, tt, 1:2], AF.Ln,
                        bias=eps_col[:, :], scale=1.0))
                    sc.op("s", lambda e: e.drain())
                    sc.op("s", lambda e, tt=tt: e.activation(
                        rstdf[:, tt:tt + 1], lnvf[:, tt:tt + 1], AF.Exp,
                        bias=0.0, scale=-0.5))
                    sc.op("s", lambda e: e.drain(), se)
                    s_rstd[tt] = SE(se.n)
                for tt in range(4):
                    gt = b4 * 4 + tt
                    sc.wait("v", s_rstd[tt])
                    sc.op("v", lambda e, tt=tt, gt=gt: e.tensor_scalar(
                        xn_tok[:, gt, :], xf_bf[:, tt, :],
                        mvf[:, tt, 0:1], rstdf[:, tt:tt + 1],
                        op0=ALU.subtract, op1=ALU.mult), ve)
                    v_n = V(ve.n)
                    sc.wait("y", v_n)
                    sc.wait("y", (c_xs[gt % 2], c_xs[gt % 2].n))
                    sc.op("y", lambda e, gt=gt: e.dma_start(
                        out=xn_scr[gt * 128:(gt + 1) * 128, :],
                        in_=xn_tok[:, gt, :]), c_xs[gt % 2], 16)
                    sc.wait("t", v_n)
                    sc.wait("t", trf_free[0])
                    for dc in range(DC):
                        inc = te if dc == DC - 1 else None
                        sc.op("t", lambda e, gt=gt, dc=dc: e.transpose(
                            ps_trf[:, dc, :], xn_tok[:, gt, dc * 128:(dc + 1) * 128],
                            ident[:, :]), inc)
                    t_tr = T(te.n)
                    sc.wait("v", t_tr)
                    sc.op("v", lambda e, tt=tt: e.tensor_copy(
                        xnTb[:, :, tt * 128:(tt + 1) * 128], ps_trf[:, :, :]), ve)
                    trf_free[0] = V(ve.n)
                xnTb_done = V(ve.n)
                xf_used[0] = V(ve.n)
                sc.wait("t", xnTb_done)
                for tt in range(4):
                    gt = b4 * 4 + tt
                    bk = 4 + (tt % 2)
                    sc.wait("t", bank_free[bk])
                    for dc in range(DC):
                        inc = te if dc == DC - 1 else None
                        sc.op("t", lambda e, tt=tt, dc=dc, bk=bk: e.matmul(
                            ps_b[:, bk, 0:E], xnTb[:, dc, tt * 128:(tt + 1) * 128],
                            rw_sb[:, dc, :], start=(dc == 0), stop=(dc == DC - 1)), inc)
                    t_lg = T(te.n)
                    sc.wait("v", t_lg)
                    sc.wait("v", rb_done)
                    sc.op("v", lambda e, gt=gt, bk=bk: e.tensor_tensor(
                        logits[:, gt, :], ps_b[:, bk, 0:E], rb_bc[:, :], op=ALU.add), ve)
                    bank_free[bk] = V(ve.n)
                    sc.op("v", lambda e: e.drain())
                    sc.op("v", lambda e, gt=gt: e.reduce_max(
                        lgmax[:, gt:gt + 1], logits[:, gt, :], axis=AX.X))
                    sc.op("v", lambda e: e.drain())
                    sc.op("v", lambda e, gt=gt: e.tensor_scalar(
                        mask_bf[:, gt, :], logits[:, gt, :],
                        lgmax[:, gt:gt + 1], None, op0=ALU.is_ge))
                    sc.op("v", lambda e: e.drain(), ve)
            mask_done = V(ve.n)

            sc.wait("t", mask_done)
            for j in range(16):
                bk = 5 + (j % 2)
                sc.wait("t", bank_free[bk])
                for i in range(j + 1):
                    lhs = triu_sb if i == j else ones2d
                    inc = te if i == j else None
                    sc.op("t", lambda e, i=i, j=j, bk=bk, lhs=lhs: e.matmul(
                        ps_b[:, bk, 0:E], lhs[:, :], mask_bf[:, i, :],
                        start=(i == 0), stop=(i == j)), inc)
                t_cum = T(te.n)
                sc.wait("v", t_cum)
                sc.wait("v", ec_done)
                sc.op("v", lambda e, bk=bk: e.tensor_tensor(
                    tmp8[:, :], ps_b[:, bk, 0:E], ecap_bc[:, :], op=ALU.add), ve)
                bank_free[bk] = V(ve.n)
                sc.op("v", lambda e: e.drain())
                sc.op("v", lambda e, j=j: e.tensor_tensor(
                    tmp8[:, :], tmp8[:, :], mask_bf[:, j, :], op=ALU.mult))
                sc.op("v", lambda e: e.drain())
                sc.op("v", lambda e: e.reduce_sum(
                    possum[:, :], tmp8[:, :], axis=AX.X))
                sc.op("v", lambda e: e.drain())
                sc.op("v", lambda e, j=j: e.tensor_scalar_add(
                    pos_col[:, j:j + 1], possum[:, :], -1.0))
                sc.op("v", lambda e: e.drain(), ve)
            sc.op("v", lambda e: e.tensor_scalar_max(
                pos_t1[:, :], pos_col[:, :], 0.0))
            sc.op("v", lambda e: e.drain())
            sc.op("v", lambda e: e.tensor_scalar_min(
                pos_t1[:, :], pos_t1[:, :], 3100.0))
            sc.op("v", lambda e: e.drain())
            sc.op("v", lambda e: e.tensor_copy(pos_i32[:, :], pos_t1[:, :]))
            sc.op("v", lambda e: e.drain(), ve)
            pos_done = V(ve.n)
            sc.wait("g", pos_done)
            sc.wait("g", (c_tk, 0))
            sc.wait("g", (c_ys[0], 16))
            sc.wait("g", (c_iz, 16))
            for tc in range(16):
                sc.wait("g", (c_is[tc % 2], c_is[tc % 2].n))
                sc.op("g", lambda e, tc=tc: e.indirect_dma_start(
                    inv_scr[0:128, :],
                    IndirectOffsetOnAxis(pos_i32[:, tc:tc + 1], 0),
                    tokid_sb[:, tc:tc + 1], None), c_is[tc % 2], 16)
            sc.wait("g", (c_is[0], c_is[0].n))
            sc.wait("g", (c_is[1], c_is[1].n))
            sc.wait("y", (c_xs[0], c_xs[0].n))
            sc.wait("y", (c_xs[1], c_xs[1].n))
            sc.wait("y", (c_fz[0], c_fz[0].n))
            sc.wait("y", (c_fz[1], c_fz[1].n))


            with nc.Block() as blk:
                @blk.tensor
                def _(e):
                    sc.replay("t", e)

                @blk.vector
                def _(e):
                    sc.replay("v", e)

                @blk.scalar
                def _(e):
                    sc.replay("s", e)

                @blk.gpsimd
                def _(e):
                    sc.replay("g", e)

                @blk.sync
                def _(e):
                    sc.replay("y", e)

            b1stack.close()

            def fsb2(name, shape, dtype=F32):
                return fstack.enter_context(nc.sbuf_tensor(name, shape, dtype))

            xnrows = fsb2("xnrows", [128, 3, D], BF16)
            inv_f = fsb2("inv_f", [128, 3], F32)
            inv_t1 = fsb2("inv_t1", [128, 3], F32)
            inv_eq = fsb2("inv_eq", [128, 3], F32)
            gidx = fsb2("gidx", [128, 3], I32)
            sidx_f = fsb2("sidx_f", [128, 3], F32)
            sidx = fsb2("sidx", [128, 3], I32)
            ffw = fsb2("ffw", [128, 2, D], BF16)
            xn_g = fsb2("xn_g", [128, DC, CAP], BF16)
            w1f32 = fsb2("w1f32", [128, 2, DC, 1024], F32)
            u_sb = fsb2("u_sb", [128, 32, CAP], BF16)
            sig_sb = fsb2("sig_sb", [128, 8, CAP], BF16)
            w1_ch = fsb2("w1_ch", [128, 2, DC, 1024], BF16)
            w2_ch = fsb2("w2_ch", [128, 2, 32, 128], BF16)
            yT_sb = fsb2("yT_sb", [128, DC, CAP], BF16)
            y_g = fsb2("y_g", [128, CAP // 128, D], BF16)
            r_in = fsb2("r_in", [128, 1, D], F32)
            out_sb = fsb2("out_sb", [128, 2, D], F32)
            ps_b = fstack.enter_context(nc.psum_tensor("ps_b2", [128, 7, 512], F32))
            ps_g2 = fstack.enter_context(nc.psum_tensor("ps_g2", [128, 4, 128], BF16))

            sc = Sched()
            bank_free = {k: None for k in range(8)}
            g2_free = [None]
            w1_used = {0: None, 1: None}
            w2_used = {0: None, 1: None}
            sig_ready = {}
            sig_used = {}
            bo_done = None
            xnrows_used = [None]
            w1f_free = {0: None, 1: None}
            sc_done = {0: None, 1: None, 2: None}

            for ex in range(E):
                for pc in range(3):
                    r0 = ex * CAP + pc * 128
                    sc.wait("y", sc_done[pc])
                    sc.wait("y", (c_iv[pc], c_iv[pc].n))
                    sc.op("y", lambda e, r0=r0, pc=pc: e.dma_start(
                        out=inv_f[:, pc:pc + 1], in_=inv_scr[r0:r0 + 128, :]),
                        c_iv[pc], 16)
                iv_ev = [(c_iv[pc], c_iv[pc].n) for pc in range(3)]
                for ev in iv_ev:
                    sc.wait("v", ev)
                sc.op("v", lambda e: e.tensor_scalar_add(
                    inv_t1[:, :], inv_f[:, :], -1.0))
                sc.op("v", lambda e: e.drain())
                sc.op("v", lambda e: e.tensor_scalar(
                    inv_eq[:, :], inv_f[:, :], 0.0, None, op0=ALU.is_equal))
                sc.op("v", lambda e: e.drain())
                sc.op("v", lambda e: e.tensor_scalar_max(
                    inv_f[:, :], inv_t1[:, :], 0.0))
                sc.op("v", lambda e: e.drain())
                sc.op("v", lambda e: e.tensor_copy(gidx[:, :], inv_f[:, :]))
                sc.op("v", lambda e: e.drain())
                sc.op("v", lambda e: e.scalar_tensor_tensor(
                    sidx_f[:, :], inv_eq[:, :], float(TQ + 1), inv_t1[:, :],
                    op0=ALU.mult, op1=ALU.add))
                sc.op("v", lambda e: e.drain())
                sc.op("v", lambda e: e.tensor_copy(sidx[:, :], sidx_f[:, :]))
                sc.op("v", lambda e: e.drain(), ve)
                idx_done = V(ve.n)
                sc.wait("g", idx_done)
                sc.wait("g", xnrows_used[0])
                for pc in range(3):
                    sc.wait("g", (c_gi[pc], c_gi[pc].n))
                    sc.op("g", lambda e, pc=pc: e.indirect_dma_start(
                        xnrows[:, pc, :], None,
                        xn_scr[0:128, :],
                        IndirectOffsetOnAxis(gidx[:, pc:pc + 1], 0)), c_gi[pc], 16)
                gi_ev = [(c_gi[pc], c_gi[pc].n) for pc in range(3)]
                for pc in range(3):
                    sc.wait("t", gi_ev[pc])
                    for dg in range(2):
                        sc.wait("t", g2_free[0])
                        for j in range(4):
                            dc = dg * 4 + j
                            inc = te if j == 3 else None
                            sc.op("t", lambda e, pc=pc, dc=dc, j=j: e.transpose(
                                ps_g2[:, j, :],
                                xnrows[:, pc, dc * 128:(dc + 1) * 128],
                                ident[:, :]), inc)
                        t_x = T(te.n)
                        if pc == 2 and dg == 1:
                            xnrows_used[0] = T(te.n)
                        sc.wait("v", t_x)
                        sc.op("v", lambda e, pc=pc, dg=dg: e.tensor_copy(
                            xn_g[:, dg * 4:(dg + 1) * 4, pc * 128:(pc + 1) * 128],
                            ps_g2[:, :, :]), ve)
                        g2_free[0] = V(ve.n)
                xng_done = V(ve.n)

                order = []
                for i in range(4):
                    order += [4 + i, i]
                for oi, ci in enumerate(order):
                    wb = oi % 2
                    sc.wait("y", w1f_free[wb])
                    sc.op("y", lambda e, ex=ex, ci=ci, wb=wb: e.dma_start(
                        out=w1f32[:, wb, :, :],
                        in_=w1[ex, :, ci * 1024:(ci + 1) * 1024]
                        .rearrange("(dc p) f -> p dc f", p=128)), c_w1y[wb], 16)
                    y_w1 = (c_w1y[wb], c_w1y[wb].n)
                    sc.wait("v", y_w1)
                    sc.wait("v", w1_used[wb])
                    sc.op("v", lambda e, wb=wb: e.tensor_copy(
                        w1_ch[:, wb, :, :], w1f32[:, wb, :, :]))
                    sc.op("v", lambda e: e.drain(), ve)
                    w1f_free[wb] = V(ve.n)
                    g_w1 = V(ve.n)
                    sc.wait("t", g_w1)
                    sc.wait("t", xng_done)
                    is_g = ci >= 4
                    for fm in range(8):
                        bk = 4 + (fm % 3)
                        sc.wait("t", bank_free[bk])
                        for dc in range(DC):
                            inc = te if dc == DC - 1 else None
                            sc.op("t", lambda e, wb=wb, fm=fm, dc=dc, bk=bk: e.matmul(
                                ps_b[:, bk, 0:CAP],
                                w1_ch[:, wb, dc, fm * 128:(fm + 1) * 128],
                                xn_g[:, dc, :],
                                start=(dc == 0), stop=(dc == DC - 1)), inc)
                        t_h = T(te.n)
                        fc64 = ci * 8 + fm
                        fs = fm % 8
                        if is_g:
                            sc.wait("s", t_h)
                            sc.wait("s", sig_used.get(fs))
                            sc.op("s", lambda e, ex=ex, fs=fs, bk=bk, fc64=fc64:
                                  e.activation(
                                      sig_sb[:, fs, :], ps_b[:, bk, 0:CAP], AF.Sigmoid,
                                      bias=b1_sb[:, ex, fc64:fc64 + 1], scale=1.0))
                            sc.op("s", lambda e: e.drain(), se)
                            bank_free[bk] = SE(se.n)
                            sig_ready[fs] = SE(se.n)
                        else:
                            sc.wait("v", t_h)
                            sc.wait("v", sig_ready[fs])
                            fc = ci * 8 + fm
                            sc.op("v", lambda e, ex=ex, fs=fs, bk=bk, fc64=fc64, fc=fc:
                                  e.scalar_tensor_tensor(
                                      u_sb[:, fc, :], ps_b[:, bk, 0:CAP],
                                      b1_sb[:, ex, fc64:fc64 + 1],
                                      sig_sb[:, fs, :],
                                      op0=ALU.add, op1=ALU.mult), ve)
                            bank_free[bk] = V(ve.n)
                            sig_used[fs] = V(ve.n)
                    w1_used[wb] = T(te.n)
                u_all = V(ve.n)

                v_yT_ev = {}
                for dch in range(DC + 1):
                    if dch < DC:
                        wb = dch % 2
                        sc.wait("g", w2_used[wb])
                        sc.op("g", lambda e, ex=ex, dch=dch, wb=wb: e.dma_start(
                            out=w2_ch[:, wb, :, :],
                            in_=w2[ex, :, dch * 128:(dch + 1) * 128]
                            .rearrange("(fc p) d -> p fc d", p=128)), c_w2[wb], 16)
                        g_w2 = (c_w2[wb], c_w2[wb].n)
                        sc.wait("t", g_w2)
                        sc.wait("t", u_all)
                        bk = 4 + (dch % 3)
                        sc.wait("t", bank_free[bk])
                        for fc in range(32):
                            inc = te if fc == 31 else None
                            sc.op("t", lambda e, wb=wb, fc=fc, bk=bk: e.matmul(
                                ps_b[:, bk, 0:CAP], w2_ch[:, wb, fc, :], u_sb[:, fc, :],
                                start=(fc == 0), stop=(fc == 31)), inc)
                        t_y = T(te.n)
                        w2_used[wb] = T(te.n)
                        sc.wait("v", t_y)
                        sc.op("v", lambda e, ex=ex, dch=dch, bk=bk: e.tensor_scalar(
                            yT_sb[:, dch, :], ps_b[:, bk, 0:CAP],
                            b2_sb[:, ex, dch:dch + 1], None, op0=ALU.add), ve)
                        bank_free[bk] = V(ve.n)
                        v_yT_ev[dch] = V(ve.n)
                    if dch >= 1:
                        dj = dch - 1
                        sc.wait("t", v_yT_ev[dj])
                        sc.wait("t", g2_free[0])
                        npc = CAP // 128
                        for js in range(npc):
                            inc = te if js == npc - 1 else None
                            sc.op("t", lambda e, dj=dj, js=js: e.transpose(
                                ps_g2[:, js, :], yT_sb[:, dj, js * 128:(js + 1) * 128],
                                ident[:, :]), inc)
                        t_ytr = T(te.n)
                        sc.wait("v", t_ytr)
                        for pc in range(3):
                            sc.wait("v", sc_done[pc])
                        sc.op("v", lambda e, dj=dj, npc=npc: e.tensor_copy(
                            y_g[:, :, dj * 128:(dj + 1) * 128], ps_g2[:, 0:npc, :]), ve)
                        g2_free[0] = V(ve.n)
                yg_done = V(ve.n)

                sc.wait("g", yg_done)
                sc.wait("g", idx_done)
                for pc in range(3):
                    sc.wait("g", (c_sc[pc], c_sc[pc].n))
                    sc.op("g", lambda e, pc=pc: e.indirect_dma_start(
                        ffn_scr[0:128, :],
                        IndirectOffsetOnAxis(sidx[:, pc:pc + 1], 0),
                        y_g[:, pc, :], None), c_sc[pc], 16)
                    sc_done[pc] = (c_sc[pc], c_sc[pc].n)
            ffn_done = V(ve.n)

            rin_dma = {}
            rin_used = {}
            outst = {}
            for tt in range(16):
                par = tt % 2
                sc.wait("y", rin_used.get(tt - 1))
                sc.op("y", lambda e, tt=tt: e.dma_start(
                    out=r_in[:, 0, :], in_=out[tt * 128:(tt + 1) * 128, :]),
                    c_ri[0], 16)
                rin_dma[tt] = (c_ri[0], c_ri[0].n)
                for pc in range(3):
                    sc.wait("y", sc_done[pc])
                sc.wait("y", (c_ff[par], c_ff[par].n))
                sc.op("y", lambda e, tt=tt, par=par: e.dma_start(
                    out=ffw[:, par, :], in_=ffn_scr[tt * 128:(tt + 1) * 128, :]),
                    c_ff[par], 16)
                ffw_ev = (c_ff[par], c_ff[par].n)
                sc.wait("v", ffw_ev)
                sc.wait("v", rin_dma[tt])
                sc.wait("v", bo_done)
                sc.wait("v", outst.get(tt - 2))
                sc.op("v", lambda e, tt=tt, par=par: e.tensor_tensor(
                    out_sb[:, par, :], r_in[:, 0, :], ffw[:, par, :],
                    op=ALU.add))
                sc.op("v", lambda e: e.drain())
                sc.op("v", lambda e, par=par: e.tensor_tensor(
                    out_sb[:, par, :], out_sb[:, par, :], bo_bc[:, :],
                    op=ALU.add), ve)
                rin_used[tt] = V(ve.n)
                v_o = V(ve.n)
                sc.wait("y", v_o)
                sc.op("y", lambda e, tt=tt, par=par: e.dma_start(
                    out=out[tt * 128:(tt + 1) * 128, :], in_=out_sb[:, par, :]),
                    c_os[par], 16)
                outst[tt] = (c_os[par], c_os[par].n)

            sc.wait("y", (c_os[0], c_os[0].n))
            sc.wait("y", (c_os[1], c_os[1].n))

            with nc.Block() as blk:
                @blk.tensor
                def _(e):
                    sc.replay("t", e)

                @blk.vector
                def _(e):
                    sc.replay("v", e)

                @blk.scalar
                def _(e):
                    sc.replay("s", e)

                @blk.gpsimd
                def _(e):
                    sc.replay("g", e)

                @blk.sync
                def _(e):
                    sc.replay("y", e)

    return nc


def kernel(L0_attn_x, L0_attn_w_q, L0_attn_b_q, L0_attn_w_k, L0_attn_b_k,
           L0_attn_w_v, L0_attn_b_v, L0_attn_w_out, L0_attn_b_out,
           L0_ffn_router_w, L0_ffn_router_b, L0_ffn_w1, L0_ffn_b1,
           L0_ffn_w2, L0_ffn_b2):
    nc = build_program()

    x = np.ascontiguousarray(np.asarray(L0_attn_x, dtype=np.float32))
    wqT = np.ascontiguousarray(np.asarray(L0_attn_w_q, np.float32).T)
    wkT = np.ascontiguousarray(np.asarray(L0_attn_w_k, np.float32).T)
    wvT = np.ascontiguousarray(np.asarray(L0_attn_w_v, np.float32).T)
    woT = np.ascontiguousarray(np.asarray(L0_attn_w_out, np.float32).T)
    bq = np.ascontiguousarray(np.asarray(L0_attn_b_q, np.float32).reshape(8, 128).T)
    bk = np.ascontiguousarray(np.asarray(L0_attn_b_k, np.float32).reshape(8, 128).T)
    bv = np.ascontiguousarray(np.asarray(L0_attn_b_v, np.float32).reshape(8, 128).T)
    bo = np.ascontiguousarray(np.asarray(L0_attn_b_out, np.float32).reshape(8, 128).T)
    rwh = np.ascontiguousarray(np.asarray(L0_ffn_router_w, np.float32))
    rbh = np.ascontiguousarray(np.asarray(L0_ffn_router_b, np.float32).reshape(1, E))
    w1h = np.ascontiguousarray(np.asarray(L0_ffn_w1, np.float32))
    b1h = np.ascontiguousarray(
        np.asarray(L0_ffn_b1, np.float32).reshape(E, FF // 128, 128).transpose(2, 0, 1))
    w2h = np.ascontiguousarray(np.asarray(L0_ffn_w2, np.float32))
    b2h = np.ascontiguousarray(
        np.asarray(L0_ffn_b2, np.float32).reshape(E, DC, 128).transpose(2, 0, 1))
    triu = np.triu(np.ones((128, 128), np.float32))
    ones2d_h = np.ones((128, 128), np.float32)
    tokid_h = np.ascontiguousarray(
        (np.arange(16, dtype=np.float32)[None, :] * 128.0
         + np.arange(128, dtype=np.float32)[:, None] + 1.0))
    ecap_h = (np.arange(8, dtype=np.float32) * float(CAP)).reshape(1, 8)
    ident = np.eye(128, dtype=np.float32)
    ones = np.ones((1, 128), dtype=np.float32)

    in_maps = []
    for c in range(8):
        b, qq = c // 4, c % 4
        in_maps.append({
            "xb": np.ascontiguousarray(x[b]),
            "xq": np.ascontiguousarray(x[b, qq * TQ:(qq + 1) * TQ]),
            "wqT": wqT, "wkT": wkT, "wvT": wvT, "woT": woT,
            "bq_in": bq, "bk_in": bk, "bv_in": bv, "bo_in": bo,
            "rw": rwh, "rb_in": rbh,
            "w1": w1h, "b1_in": b1h, "w2": w2h, "b2_in": b2h,
            "triu_in": triu, "ones2d_in": ones2d_h, "tokid_in": tokid_h,
            "ecap_in": ecap_h,
            "ident_in": ident, "ones_in": ones,
        })

    res = run_bass_kernel_spmd(nc, in_maps, core_ids=list(range(8)))
    full = np.empty((2, S, D), dtype=np.float32)
    for c in range(8):
        b, qq = c // 4, c % 4
        full[b, qq * TQ:(qq + 1) * TQ] = res.results[c]["out"]
    return full
